# version 26
# baseline (speedup 1.0000x reference)
import numpy as np
from contextlib import ExitStack

import concourse.bass as bass
import concourse.mybir as mybir
from concourse.bass_utils import run_bass_kernel_spmd

F32 = mybir.dt.float32
BF16 = mybir.dt.bfloat16
AF = mybir.ActivationFunctionType
ALU = mybir.AluOpType
AX = mybir.AxisListType
AP = bass.AP

T = 1024
D = 2048
NQB = 8
HALO_A = 4
NKB_A = 12
NKB_B = 16
NKV = 4
SCALE = float(1.0 / np.sqrt(128.0))
WIN_A = 512
WIN_B = 1025
MAGIC = 8388608.0 + 4194304.0
BIGV = 1.0e4
HBIAS = -30000.0
TWO_PI = float(2 * np.pi)
TWO_PI_HI = 6.28125
TWO_PI_LO = float(2 * np.pi - 6.28125)
LN10K = float(np.log(10000.0))
RG = [[0, 1, 2, 3, 4, 5, 6, 7]]

N_SUBLAYERS = 4
DEBUG_PROJ = False
DEBUG_FFN = False


class Led:
    def __init__(self, handle):
        self.h = handle
        self.n = 0


class Plan:
    ENGS = ("pe", "dve", "act", "pool", "sync")

    def __init__(self):
        self.ops = {e: [] for e in self.ENGS}

    def op(self, eng, emit, waits=(), inc=None, k=1):
        if inc is not None:
            inc.n += k
        self.ops[eng].append((list(waits), emit, inc, k))
        return inc.n if inc is not None else None

    def _replay(self, eng, e):
        for waits, emit, inc, k in self.ops[eng]:
            for led, val in waits:
                e.wait_ge(led.h, val)
            ins = emit(e)
            if inc is not None:
                assert ins is not None, f"op on {eng} returned None but has inc"
                ins.then_inc(inc.h, k)

    def run(self, nc):
        with nc.Block() as blk:
            for eng, meth in (("pe", blk.tensor), ("dve", blk.vector),
                              ("act", blk.scalar), ("pool", blk.gpsimd),
                              ("sync", blk.sync)):
                if self.ops[eng]:
                    meth(lambda e, eng=eng: self._replay(eng, e))


def qk_block_class(c, W):
    if c < 0 or c > W + 126:
        return None
    if c == 0:
        return "causal"
    if -126 <= c - (W - 1) <= 127:
        return "upper"
    return "full"


def runs_for(is_a, grp, kb):
    W = WIN_A if is_a else WIN_B
    qoff = 512 if is_a else 1024
    runs = []
    cur = None
    for qs in (0, 1):
        c = (qoff + grp * 256 + qs * 128) - kb * 128
        cls = qk_block_class(c, W)
        if cls is None:
            if cur:
                runs.append(tuple(cur))
                cur = None
            continue
        m = None if cls == "full" else cls
        if cur is None:
            cur = [qs * 128, 128, [(0, m)] if m else []]
        else:
            if m:
                cur[2].append((cur[1], m))
            cur[1] += 128
    if cur:
        runs.append(tuple(cur))
    return runs


def build_nc():
    nc = bass.Bass("TRN2", target_bir_lowering=False)

    x_in = nc.dram_tensor("x_in", [T + 512, D], F32, kind="ExternalInput").ap()
    wqkv_a = nc.dram_tensor("wqkv_a", [D, 3072], F32, kind="ExternalInput").ap()
    wo_a = nc.dram_tensor("wo_a", [D, D], F32, kind="ExternalInput").ap()
    wqkv_b = nc.dram_tensor("wqkv_b", [D, 3072], F32, kind="ExternalInput").ap()
    wo_b = nc.dram_tensor("wo_b", [D, D], F32, kind="ExternalInput").ap()
    tbases = nc.dram_tensor("tbases", [64, 32, 16], F32, kind="ExternalInput").ap()
    tslopes = nc.dram_tensor("tslopes", [64, 32, 16], F32, kind="ExternalInput").ap()
    lgamma = nc.dram_tensor("lgamma", [D], F32, kind="ExternalInput").ap()
    lbeta = nc.dram_tensor("lbeta", [D], F32, kind="ExternalInput").ap()
    oscale = nc.dram_tensor("oscale", [64], F32, kind="ExternalInput").ap()
    hb_a_in = nc.dram_tensor("hb_a_in", [NKB_A], F32, kind="ExternalInput").ap()
    hb_b_in = nc.dram_tensor("hb_b_in", [NKB_B], F32, kind="ExternalInput").ap()
    posb_in = nc.dram_tensor("posb_in", [NKB_A], F32, kind="ExternalInput").ap()
    x_out = nc.dram_tensor("x_out", [T, D], F32, kind="ExternalOutput").ap()

    if DEBUG_FFN:
        ffsc_dbg = nc.dram_tensor("ffsc_dbg", [128, 816], F32, kind="ExternalOutput").ap()
        tmin_dbg = nc.dram_tensor("tmin_dbg", [64, 18], F32, kind="ExternalOutput").ap()
        agr_dbg = nc.dram_tensor("agr_dbg", [64, 4], F32, kind="ExternalOutput").ap()
        bt_dbg = nc.dram_tensor("bt_dbg", [128, 4096], BF16, kind="ExternalOutput").ap()
        st_dbg = nc.dram_tensor("st_dbg", [128, 4096], BF16, kind="ExternalOutput").ap()
    if DEBUG_PROJ:
        kT_dbg = nc.dram_tensor("kT_dbg", [128, 8192], BF16, kind="ExternalOutput").ap()
        v_dbg = nc.dram_tensor("v_dbg", [128, 8192], BF16, kind="ExternalOutput").ap()
        q_dbg = nc.dram_tensor("q_dbg", [128, 16384], BF16, kind="ExternalOutput").ap()
    rows_d = nc.dram_tensor("rows_d", [2, D], F32, kind="Internal").ap()
    rope_d = nc.dram_tensor("rope_d", [5, 128, 1024], F32, kind="Internal").ap()
    ag_in = nc.dram_tensor("ag_in", [128, 8192], BF16, kind="Internal").ap()
    ag_out = nc.dram_tensor("ag_out", [1024, 8192], BF16, kind="Internal").ap()
    ar_in = nc.dram_tensor("ar_in", [64, 2], F32, kind="Internal").ap()
    ar_out = nc.dram_tensor("ar_out", [64, 2], F32, kind="Internal").ap()

    st = ExitStack()

    def sb(name, shape, dt=F32):
        return st.enter_context(nc.sbuf_tensor(name, shape, dt))

    def sem(name):
        return Led(st.enter_context(nc.semaphore(name)))

    x_sb = sb("x_sb", [128, NQB * D])
    big16 = sb("big16", [128, 16384], BF16)
    kT = sb("kT", [128, 8192], BF16)
    v_sb = sb("v_sb", [128, 8192], BF16)
    arena = sb("arena", [128, 24576], BF16)
    y16 = sb("y16", [128, D], BF16)
    qk16 = sb("qk16", [128, 1024], BF16)
    rsc = sb("rsc", [128, 1792])
    cosg = sb("cosg", [128, 512])
    sing = sb("sing", [128, 512])
    tableT = sb("tableT", [64, 68])
    table16 = sb("table16", [64, 1024], BF16)
    ffsc = sb("ffsc", [128, 816])
    stats = sb("stats", [128, 156])
    small = sb("small", [128, 256])
    consts = sb("consts", [128, 8])
    iota64 = sb("iota64", [128, 64])
    iota64m = sb("iota64m", [128, 64])
    iota8m = sb("iota8m", [128, 8])
    ca64 = sb("ca64", [128, 64])
    mcaus = sb("mcaus", [128, 128], BF16)
    mstrU = sb("mstrU", [128, 128], BF16)
    minclU = sb("minclU", [128, 128], BF16)
    ident16 = sb("ident16", [128, 128], BF16)
    identf = sb("identf", [128, 128])
    ones_bf = sb("ones_bf", [128, 128], BF16)
    ones1 = sb("ones1", [1, 128])
    hbB = sb("hbB", [128, NKB_A + NKB_B])
    pidx = sb("pidx", [128, 1])
    invfA = sb("invfA", [128, 128])
    invfB = sb("invfB", [128, 128])
    sgnA = sb("sgnA", [128, 128])
    sgnB = sb("sgnB", [128, 128])
    tmin_s = sb("tmin_s", [64, 18])
    agr = sb("agr", [64, 4])
    ohq = sb("ohq", [128, 64])
    ohq16 = sb("ohq16", [128, 64], BF16)
    ohq_u8 = sb("ohq_u8", [128, 64], mybir.dt.uint8)
    ohall_u8 = sb("ohall_u8", [128, 512], mybir.dt.uint8)
    ohT16 = sb("ohT16", [64, 128], BF16)
    ohTf = sb("ohTf", [64, 128])
    gsel = sb("gsel", [128, 66])

    FC_TMIN, FC_TMAX, FC_TIDX, FC_OSEL = 0, 8, 16, 24
    FC_GI, FC_XL, FC_XS = 32, 288, 544

    s_dma = sem("s_dma")
    s_wdma = sem("s_wdma")
    s_wdma2 = sem("s_wdma2")
    s_hdma = sem("s_hdma")
    s_pe = sem("s_pe")
    s_dve = sem("s_dve")
    s_act = sem("s_act")
    s_pool = sem("s_pool")
    s_cc = sem("s_cc")
    ALL_SEMS = [s_dma, s_wdma, s_wdma2, s_hdma, s_pe, s_dve, s_act, s_pool, s_cc]

    with nc.Block() as b0:
        @b0.gpsimd
        def _(g):
            for s in ALL_SEMS:
                g.sem_clear(s.h)
    nc.all_engine_barrier()

    def a16(off, n):
        return arena[:, off:off + n]

    def a32(off_bf16, n_f32):
        return arena[:, off_bf16:off_bf16 + 2 * n_f32].bitcast(F32)

    P = Plan()

    def em(pl, eng, fn, waits=(), inc=None, k=1):
        return pl.op(eng, fn, waits=waits, inc=inc, k=k)

    def rope_tables_for_group(p, is_a, group, base_waits):
        invf = invfA if is_a else invfB
        sgn = sgnA if is_a else sgnB
        ng = len(group)
        W = 128 * ng
        posB = small[:, 224:225]
        ang = rsc[:, 0:512]
        ev_cos = ev_sin = None
        for si, u in enumerate(group):
            pb_col = 144 + (u if is_a else 4 + u)
            em(p, "dve", lambda v, pb_col=pb_col: v.tensor_single_scalar(
                posB, pidx[:], stats[:, pb_col:pb_col + 1], ALU.add),
                waits=(list(base_waits) if si == 0 else []))
            em(p, "dve", lambda v: v.drain())
            em(p, "dve", lambda v, si=si: v.tensor_single_scalar(
                ang[:, si * 128:(si + 1) * 128], invf[:], posB, ALU.mult))
            em(p, "dve", lambda v: v.drain())
        for trig in ("sin", "cos"):
            aa = rsc[:, 512:512 + W] if trig == "sin" else rsc[:, 1280:1280 + W]
            kk = rsc[:, 1280:1280 + W] if trig == "sin" else rsc[:, 512:512 + W]
            if trig == "sin":
                em(p, "dve", lambda v, aa=aa: v.tensor_copy(aa, ang[:, 0:W]))
            else:
                em(p, "dve", lambda v, aa=aa: v.tensor_single_scalar(
                    aa, ang[:, 0:W], float(0.5 * np.pi), ALU.add))
            em(p, "dve", lambda v: v.drain())
            em(p, "dve", lambda v, aa=aa, kk=kk: v.tensor_single_scalar(
                kk, aa, float(1.0 / TWO_PI), ALU.mult))
            em(p, "dve", lambda v: v.drain())
            em(p, "dve", lambda v, kk=kk: v.tensor_single_scalar(
                kk, kk, consts[:, 0:1], ALU.add))
            em(p, "dve", lambda v: v.drain())
            em(p, "dve", lambda v, kk=kk: v.tensor_single_scalar(
                kk, kk, MAGIC, ALU.subtract))
            em(p, "dve", lambda v: v.drain())
            em(p, "dve", lambda v, aa=aa, kk=kk: v.scalar_tensor_tensor(
                aa, kk, consts[:, 3:4], aa, ALU.mult, ALU.add))
            em(p, "dve", lambda v: v.drain())
            ev = em(p, "dve", lambda v, aa=aa, kk=kk: v.scalar_tensor_tensor(
                aa, kk, consts[:, 4:5], aa, ALU.mult, ALU.add), inc=s_dve)
            if trig == "sin":
                eva = em(p, "act", lambda s, aa=aa: s.activation(
                    rsc[:, 1280:1280 + W], aa, AF.Sin), waits=[(s_dve, ev)],
                    inc=s_act)
                sgb = AP(sgn, 0, [[128, 128], [0, ng], [1, 128]])
                ev_sin = em(p, "dve", lambda v, W=W, sgb=sgb: v.tensor_tensor(
                    sing[:, 0:W].rearrange("p (s w) -> p s w", w=128),
                    rsc[:, 1280:1280 + W].rearrange("p (s w) -> p s w", w=128),
                    sgb, ALU.mult), waits=[(s_act, eva)], inc=s_dve)
            else:
                ev_cos = em(p, "act", lambda s, aa=aa, W=W: s.activation(
                    cosg[:, 0:W], aa, AF.Sin), waits=[(s_dve, ev)], inc=s_act)
        return ev_cos, ev_sin

    def DMA(pl, dst, src, led=s_dma, eng="pool", waits=()):
        return pl.op(eng, lambda e, d=dst, s=src: e.dma_start(out=d, in_=s),
                     waits=waits, inc=led, k=16)

    DMA(P, AP(x_sb, 0, [[NQB * D, 128], [D, 4], [1, D]]),
        AP(x_in.tensor, 512 * D, [[D, 128], [128 * D, 4], [1, D]]), led=s_hdma)
    DMA(P, AP(x_sb, 4 * D, [[NQB * D, 128], [D, 4], [1, D]]),
        AP(x_in.tensor, (512 + 4 * 128) * D, [[D, 128], [128 * D, 4], [1, D]]),
        eng="sync")
    DMA(P, a32(4096, 512)[0:64, :], tslopes.rearrange("t j g -> t (j g)"),
        led=s_hdma)
    DMA(P, tableT[0:64, 0:32], lgamma.rearrange("(t j) -> t j", t=64), led=s_hdma)
    DMA(P, tableT[0:64, 32:64], lbeta.rearrange("(t j) -> t j", t=64), led=s_hdma)
    DMA(P, tableT[0:64, 64:65], oscale.rearrange("(t o) -> t o", o=1), led=s_hdma)
    DMA(P, table16[0:64, 0:512], tbases.rearrange("t j g -> t (j g)"), led=s_hdma)
    DMA(P, table16[0:64, 512:1024], tslopes.rearrange("t j g -> t (j g)"),
        led=s_hdma)
    DMA(P, small[0:1, 0:NKB_A], hb_a_in.rearrange("(o k) -> o k", o=1), led=s_hdma)
    DMA(P, small[0:1, 32:32 + NKB_B], hb_b_in.rearrange("(o k) -> o k", o=1),
        led=s_hdma)
    DMA(P, small[0:1, 64:64 + NKB_A], posb_in.rearrange("(o k) -> o k", o=1),
        led=s_hdma)
    n_setup_dma = s_dma.n
    n_setup_hdma = s_hdma.n

    qqB = a32(0, 128)
    kkB = a32(256, 128)
    jA = a32(512, 128)
    jB = a32(768, 128)
    h01 = a32(1024, 128)
    p01 = a32(1280, 128)
    em(P, "pool", lambda g: g.iota(small[:, 128:136], [[0, 8]], channel_multiplier=1,
                                   allow_small_or_imprecise_dtypes=True))
    em(P, "pool", lambda g: g.iota(iota64[:], [[1, 64]], channel_multiplier=0,
                                   allow_small_or_imprecise_dtypes=True))
    em(P, "pool", lambda g: g.iota(ca64[:], [[1, 8], [0, 8]], channel_multiplier=0,
                                   allow_small_or_imprecise_dtypes=True))
    em(P, "pool", lambda g: g.iota(qqB, [[1, 128]], channel_multiplier=0,
                                   allow_small_or_imprecise_dtypes=True))
    em(P, "pool", lambda g: g.iota(kkB, [[0, 128]], channel_multiplier=1,
                                   allow_small_or_imprecise_dtypes=True))
    em(P, "pool", lambda g: g.iota(jA, [[0, 2], [1, 64]], channel_multiplier=0,
                                   allow_small_or_imprecise_dtypes=True))
    em(P, "pool", lambda g: g.iota(jB, [[1, 64], [0, 2]], channel_multiplier=0,
                                   allow_small_or_imprecise_dtypes=True))
    em(P, "pool", lambda g: g.iota(h01, [[1, 2], [0, 64]], channel_multiplier=0,
                                   allow_small_or_imprecise_dtypes=True))
    em(P, "pool", lambda g: g.iota(p01, [[0, 64], [1, 2]], channel_multiplier=0,
                                   allow_small_or_imprecise_dtypes=True))
    em(P, "pool", lambda g: g.memset(ones1[:], 1.0))
    em(P, "pool", lambda g: g.memset(ones_bf[:], 1.0))
    for i, val in ((0, MAGIC), (1, 1e-5), (2, 1.0 / 2048.0), (3, -TWO_PI_HI),
                   (4, -TWO_PI_LO), (5, 1.0), (6, 1e-8), (7, BIGV)):
        em(P, "pool", lambda g, i=i, val=val: g.memset(consts[:, i:i + 1], float(val)))
    em(P, "pool", lambda g: g.nop(), inc=s_pool)
    pl_set = s_pool.n

    w0 = [(s_dma, n_setup_dma), (s_hdma, n_setup_hdma),
          (s_pool, pl_set)]
    em(P, "dve", lambda v: v.tensor_copy(pidx[:], small[:, 128:129]), waits=w0)
    em(P, "dve", lambda v: v.drain())
    em(P, "dve", lambda v: v.tensor_tensor(ident16[:], qqB, kkB, ALU.is_equal))
    em(P, "dve", lambda v: v.tensor_tensor(identf[:], qqB, kkB, ALU.is_equal))
    em(P, "dve", lambda v: v.tensor_tensor(mcaus[:], qqB, kkB, ALU.is_ge))
    em(P, "dve", lambda v: v.tensor_tensor(minclU[:], qqB, kkB, ALU.is_le))
    em(P, "dve", lambda v: v.tensor_tensor(mstrU[:], qqB, kkB, ALU.is_lt))
    em(P, "dve", lambda v: v.tensor_single_scalar(iota64m[:], iota64[:], BIGV, ALU.subtract))
    em(P, "dve", lambda v: v.tensor_single_scalar(iota8m[:], iota64[:, 0:8], BIGV, ALU.subtract))
    em(P, "dve", lambda v: v.tensor_single_scalar(sgnA[:], h01, 2.0, ALU.mult))
    em(P, "dve", lambda v: v.tensor_single_scalar(sgnB[:], p01, 2.0, ALU.mult))
    em(P, "dve", lambda v: v.drain())
    em(P, "dve", lambda v: v.tensor_single_scalar(sgnA[:], sgnA[:], 1.0, ALU.subtract))
    em(P, "dve", lambda v: v.tensor_single_scalar(sgnB[:], sgnB[:], 1.0, ALU.subtract),
       inc=s_dve)
    dv_set1 = s_dve.n

    em(P, "act", lambda s: s.activation(invfA[:], jA, AF.Exp, scale=-LN10K / 64.0),
       waits=w0)
    em(P, "act", lambda s: s.activation(invfB[:], jB, AF.Exp, scale=-LN10K / 64.0),
       inc=s_act)
    ac_set1 = s_act.n

    with nc.psum_tensor("ps_b", [128, 512], F32) as ps_b:
        em(P, "pe", lambda t: t.matmul(ps_b[:, 0:NKB_A], ones1[:], small[0:1, 0:NKB_A],
                                       start=True, stop=True),
           waits=[(s_hdma, n_setup_hdma), (s_pool, pl_set)])
        em(P, "pe", lambda t: t.matmul(ps_b[:, 64:64 + NKB_B], ones1[:],
                                       small[0:1, 32:32 + NKB_B], start=True, stop=True))
        em(P, "pe", lambda t: t.matmul(ps_b[:, 128:128 + NKB_A], ones1[:],
                                       small[0:1, 64:64 + NKB_A], start=True, stop=True),
           inc=s_pe)
        pe_set1 = s_pe.n
        em(P, "dve", lambda v: v.tensor_copy(hbB[:, 0:NKB_A], ps_b[:, 0:NKB_A]),
           waits=[(s_pe, pe_set1)])
        em(P, "dve", lambda v: v.tensor_copy(hbB[:, NKB_A:NKB_A + NKB_B],
                                             ps_b[:, 64:64 + NKB_B]))
        em(P, "dve", lambda v: v.tensor_copy(stats[:, 144:144 + NKB_A],
                                             ps_b[:, 128:128 + NKB_A]), inc=s_dve)
        dv_set2 = s_dve.n

        em(P, "dve", lambda v: v.tensor_reduce(
            small[0:64, 224:256],
            a32(4096, 512)[0:64, :].rearrange("p (j g) -> p j g", j=32),
            AX.X, ALU.add), waits=[(s_hdma, n_setup_hdma)], inc=s_dve)
        ev_ss = s_dve.n
        em(P, "act", lambda s: s.activation(small[0:64, 224:256], small[0:64, 224:256],
                                            AF.Sign), waits=[(s_dve, ev_ss)], inc=s_act)
        ev_sg = s_act.n
        em(P, "dve", lambda v: v.tensor_tensor(small[0:64, 0:32], tableT[0:64, 0:32],
                                               small[0:64, 224:256], ALU.mult),
           waits=[(s_act, ev_sg)])
        em(P, "dve", lambda v: v.tensor_tensor(small[0:64, 32:64], tableT[0:64, 32:64],
                                               small[0:64, 224:256], ALU.mult),
           inc=s_dve)
        ev_rows = s_dve.n
        em(P, "sync", lambda sy: sy.dma_start(
            out=rows_d[0:1, :].rearrange("r (t j) -> (r t) j", t=64),
            in_=small[0:64, 0:32]), waits=[(s_dve, ev_rows)], inc=s_dma, k=16)
        em(P, "sync", lambda sy: sy.dma_start(
            out=rows_d[1:2, :].rearrange("r (t j) -> (r t) j", t=64),
            in_=small[0:64, 32:64]), inc=s_dma, k=16)
        s1 = big16[:, 0:3072].bitcast(F32)
        s2 = big16[:, 3072:6144].bitcast(F32)
        s3 = big16[:, 6144:9216].bitcast(F32)
        s4 = big16[:, 9216:12288].bitcast(F32)
        prev_stash = 0
        for nsets, nunits, invf, sgn, pb0, base in (
                (3, 12, invfA, sgnA, 144, 0), (2, 8, invfB, sgnB, 148, 3)):
            W = nsets * 512
            ang = rsc[:, 0:W]
            a3 = ang.rearrange("p (u w) -> p u w", w=128)
            pidxb = AP(pidx, 0, [[1, 128], [0, nunits], [0, 128]])
            posbb = AP(stats, pb0, [[156, 128], [1, nunits], [0, 128]])
            invfb_ = AP(invf, 0, [[128, 128], [0, nunits], [1, 128]])
            sgnb_ = AP(sgn, 0, [[128, 128], [0, nunits], [1, 128]])
            bw0 = [(s_dve, dv_set2), (s_act, ac_set1)]
            if prev_stash:
                bw0.append((s_dma, prev_stash))
            em(P, "dve", lambda v, a3=a3, pidxb=pidxb, posbb=posbb:
               v.tensor_tensor(a3, pidxb, posbb, ALU.add), waits=bw0)
            em(P, "dve", lambda v: v.drain())
            em(P, "dve", lambda v, a3=a3, invfb_=invfb_: v.tensor_tensor(
                a3, a3, invfb_, ALU.mult))
            em(P, "dve", lambda v: v.drain())
            evt = {}
            for trig, aa, kk in (("sin", s1, s2), ("cos", s3, s4)):
                av = aa[:, 0:W]
                kv = kk[:, 0:W]
                if trig == "sin":
                    em(P, "dve", lambda v, av=av, W=W, ang=ang: v.tensor_copy(
                        av, ang[:, 0:W]))
                else:
                    em(P, "dve", lambda v, av=av, W=W, ang=ang:
                       v.tensor_single_scalar(
                        av, ang[:, 0:W], float(0.5 * np.pi), ALU.add))
                em(P, "dve", lambda v: v.drain())
                em(P, "dve", lambda v, av=av, kv=kv: v.tensor_single_scalar(
                    kv, av, float(1.0 / TWO_PI), ALU.mult))
                em(P, "dve", lambda v: v.drain())
                em(P, "dve", lambda v, kv=kv: v.tensor_scalar(
                    kv, kv, consts[:, 0:1], MAGIC, ALU.add, ALU.subtract))
                em(P, "dve", lambda v: v.drain())
                em(P, "dve", lambda v, av=av, kv=kv: v.scalar_tensor_tensor(
                    av, kv, consts[:, 3:4], av, ALU.mult, ALU.add))
                em(P, "dve", lambda v: v.drain())
                ev = em(P, "dve", lambda v, av=av, kv=kv: v.scalar_tensor_tensor(
                    av, kv, consts[:, 4:5], av, ALU.mult, ALU.add), inc=s_dve)
                if trig == "sin":
                    eva = em(P, "act", lambda s, kv=kv, av=av: s.activation(
                        kv, av, AF.Sin), waits=[(s_dve, ev)], inc=s_act)
                    evt["sin"] = em(P, "dve", lambda v, av=av, kv=kv, W=W,
                                    sgnb_=sgnb_: v.tensor_tensor(
                        av.rearrange("p (u w) -> p u w", w=128),
                        kv.rearrange("p (u w) -> p u w", w=128),
                        sgnb_, ALU.mult), waits=[(s_act, eva)], inc=s_dve)
                else:
                    evt["cos"] = em(P, "act", lambda s, kv=kv, av=av: s.activation(
                        kv, av, AF.Sin), waits=[(s_dve, ev)], inc=s_act)
            em(P, "sync", lambda sy, base=base, nsets=nsets, W=W: sy.dma_start(
                out=AP(rope_d.tensor, base * 131072,
                       [[1024, 128], [131072, nsets], [1, 512]]),
                in_=s4[:, 0:W].rearrange("p (s c) -> p s c", c=512)),
                waits=[(s_act, evt["cos"])], inc=s_dma, k=16)
            prev_stash = em(P, "sync", lambda sy, base=base, nsets=nsets, W=W:
                            sy.dma_start(
                out=AP(rope_d.tensor, base * 131072 + 512,
                       [[1024, 128], [131072, nsets], [1, 512]]),
                in_=s1[:, 0:W].rearrange("p (s c) -> p s c", c=512)),
                waits=[(s_dve, evt["sin"])], inc=s_dma, k=16)
        P.run(nc)


    def scalar_chain_ln(p, u, xs_ap, want_C, extra_waits=()):
        sums = stats[:, u:u + 1]
        sumsq = stats[:, 32 + u:33 + u]
        negmu = stats[:, 64 + u:65 + u]
        rstd = stats[:, 96 + u:97 + u]
        Ccol = stats[:, 128 + u:129 + u]
        mu = small[:, 240:241]
        var = small[:, 241:242]
        t0 = small[:, 242:243]
        v2 = small[:, 243:244]
        em(p, "act", lambda s: s.activation(y16[:], xs_ap, AF.Square,
                                            accum_out=sumsq),
           waits=list(extra_waits))
        eva = em(p, "act", lambda s: s.activation(y16[:], xs_ap, AF.Identity,
                                                  accum_out=sums), inc=s_act)
        em(p, "dve", lambda v: v.tensor_single_scalar(mu, sums, consts[:, 2:3], ALU.mult),
           waits=[(s_act, eva)])
        em(p, "dve", lambda v: v.drain())
        em(p, "dve", lambda v: v.tensor_single_scalar(negmu, mu, -1.0, ALU.mult))
        em(p, "dve", lambda v: v.tensor_single_scalar(t0, mu, mu, ALU.mult),
           waits=[(s_act, eva)])
        em(p, "dve", lambda v: v.drain())
        em(p, "dve", lambda v: v.scalar_tensor_tensor(var, sumsq, consts[:, 2:3], t0,
                                                      ALU.mult, ALU.subtract))
        ev1 = em(p, "dve", lambda v: v.drain(), inc=s_dve)
        em(p, "act", lambda s: s.activation(t0, var, AF.Ln, bias=consts[:, 1:2]),
           waits=[(s_dve, ev1)])
        ev2 = em(p, "act", lambda s: s.activation(rstd, t0, AF.Exp, scale=-0.5),
                 inc=s_act)
        if not want_C:
            return ev2
        em(p, "act", lambda s: s.activation(v2, t0, AF.Exp, scale=-1.0))
        ev3 = em(p, "act", lambda s: s.nop(), inc=s_act)
        em(p, "dve", lambda v: v.tensor_single_scalar(v2, var, v2, ALU.mult),
           waits=[(s_act, ev3)])
        ev4 = em(p, "dve", lambda v: v.drain(), inc=s_dve)
        em(p, "act", lambda s: s.activation(v2, v2, AF.Ln, bias=consts[:, 1:2]),
           waits=[(s_dve, ev4)])
        ev5 = em(p, "act", lambda s: s.activation(v2, v2, AF.Exp, scale=-0.5),
                 inc=s_act)
        em(p, "dve", lambda v: v.tensor_single_scalar(Ccol, rstd, v2, ALU.mult),
           waits=[(s_act, ev5)])
        ev6 = em(p, "dve", lambda v: v.drain(), inc=s_dve)
        return ev2, ev6


    def attn_layer(variant):
        is_a = variant == "a"
        wqkv = wqkv_a if is_a else wqkv_b
        wo = wo_a if is_a else wo_b
        nkb = NKB_A if is_a else NKB_B
        hb_off = 0 if is_a else NKB_A
        units = list(range(NKB_A)) if is_a else list(range(NQB))
        groups = [units[i:i + 4] for i in range(0, len(units), 4)]

        CW = 512

        def uinfo(u):
            if is_a:
                return (u < HALO_A), (None if u < HALO_A else u - HALO_A), u
            return False, u, 8 + u

        def slots_of(c):
            out = []
            for i in range(CW // 128):
                col = CW * c + 128 * i
                if col < 2048:
                    out.append(("q", col // 128, i))
                elif col < 2560:
                    out.append(("k", (col - 2048) // 128, i))
                else:
                    out.append(("v", (col - 2560) // 128, i))
            return out

        p = Plan()
        with ExitStack() as pst:
            ps_tr = [pst.enter_context(nc.psum_tensor(f"pt{variant}{i}", [128, 2048],
                                                      BF16)) for i in range(2)]
            ps_qkv = [pst.enter_context(nc.psum_tensor(f"pq{variant}{i}", [128, 512],
                                                       F32)) for i in range(2)]
            wbuf = [arena[:, 0:8192], arena[:, 8192:16384]]
            yTg_off = 16384

            tr_evac = [0, 0]
            qkv_done = [0, 0]
            chunk_mm_done = {}
            citer = 0
            wchunk = 0
            pending_tr = []
            trctr = [0]
            tr_read = {0: None, 1: None}
            prev_tr_pe = 0
            yT_ev = {}
            ag_pending = [None]
            first_ln = [True]
            if is_a:
                passes = [("all", 0), ("all", 1), ("all", 2)]
            else:
                passes = [("kv", 0), ("kv", 1), ("AG", None), ("qq", None)]

            def load_ropetab(rset):
                wpre = [(s_dve, s_dve.n), (s_act, s_act.n)]
                em(p, "sync", lambda sy, rset=rset: sy.dma_start(
                    out=cosg[:, 0:512],
                    in_=AP(rope_d.tensor, rset * 128 * 1024,
                           [[1024, 128], [1, 512]])),
                   waits=wpre, inc=s_dma, k=16)
                ev_rld = em(p, "sync", lambda sy, rset=rset: sy.dma_start(
                    out=sing[:, 0:512],
                    in_=AP(rope_d.tensor, rset * 128 * 1024 + 512,
                           [[1024, 128], [1, 512]])), inc=s_dma, k=16)
                return ((s_dma, ev_rld), (s_dma, ev_rld))

            for ph, gi in passes:
                if ph == "AG":
                    for fn in pending_tr:
                        fn()
                    pending_tr = []
                    wpk = [(s_act, s_act.n), (s_dve, s_dve.n), (s_pe, s_pe.n)]
                    DMA(p, AP(ag_in.tensor, 0,
                              [[8192, 128], [1024, 4], [1, 1024]]),
                        AP(kT, 1024, [[8192, 128], [2048, 4], [1, 1024]]),
                        eng="sync", waits=wpk)
                    ev_pack = DMA(p, AP(ag_in.tensor, 4096,
                                        [[8192, 128], [1024, 4], [1, 1024]]),
                                  AP(v_sb, 1024, [[8192, 128], [2048, 4],
                                                  [1, 1024]]), eng="sync")
                    ag_pending[0] = ev_pack
                    continue
                if ph == "qq":
                    group = []
                    units_iter = [(gi2, si, si, u) for gi2 in (0, 1)
                                  for si, u in enumerate(groups[gi2])]
                    cos_ev = {}
                else:
                    group = groups[gi]
                    units_iter = [(gi, si, si, u) for si, u in enumerate(group)]
                    gtab = load_ropetab(gi if is_a else 3 + gi)
                    cos_ev = {u: gtab for u in group}
                for si, u in (enumerate(group) if ph != "qq" else []):
                    halo, qb, kb = uinfo(u)
                    xw = []
                    if first_ln[0]:
                        xw += [(s_dve, s_dve.n), (s_pool, s_pool.n)]
                        first_ln[0] = False
                    if halo:
                        hw = [(s_dve, s_dve.n), (s_act, s_act.n)]
                        hx_ap = a32(0, 2048)
                        ev_hx = DMA(p, hx_ap, AP(x_in.tensor, u * 128 * D,
                                                 [[D, 128], [1, D]]),
                                    led=s_hdma, waits=hw)
                        xs_ap = hx_ap
                        xw = [(s_hdma, ev_hx)]
                    else:
                        xs_ap = x_sb[:, qb * D:(qb + 1) * D]
                    ev_rstd = scalar_chain_ln(p, u % 32, xs_ap, False, extra_waits=xw)
                    wv_y = [(s_act, ev_rstd)]
                    if prev_tr_pe:
                        wv_y.append((s_pe, prev_tr_pe))
                    evy = em(p, "dve", lambda v, xs_ap=xs_ap, u=u: v.tensor_scalar(
                        y16[:], xs_ap, stats[:, 64 + (u % 32):65 + (u % 32)],
                        stats[:, 96 + (u % 32):97 + (u % 32)], ALU.add, ALU.mult),
                        waits=wv_y, inc=s_dve)
                    par = si % 2
                    wv = [(s_dve, evy)]
                    if tr_evac[par]:
                        wv.append((s_act, tr_evac[par]))
                    ev_tr = None
                    for i in range(16):
                        ev_tr = em(p, "pe", lambda t, i=i, par=par: t.transpose(
                            ps_tr[par][:, i * 128:(i + 1) * 128],
                            y16[:, i * 128:(i + 1) * 128], ident16[:]),
                            waits=(wv if i == 0 else []),
                            inc=(s_pe if i == 15 else None))
                    if (not is_a) and gi == 0:
                        d1 = kT[:, si * 2048:si * 2048 + 1024]
                        d2 = v_sb[:, si * 2048:si * 2048 + 1024]
                        em(p, "act", lambda s, d1=d1, par=par: s.activation(
                            d1, ps_tr[par][:, 0:1024], AF.Identity),
                            waits=[(s_pe, ev_tr)])
                        tr_evac[par] = em(p, "act", lambda s, d2=d2, par=par:
                                          s.activation(d2, ps_tr[par][:, 1024:2048],
                                                       AF.Identity), inc=s_act)
                    else:
                        dst_yT = a16(yTg_off + si * 2048, 2048)
                        tr_evac[par] = em(p, "act", lambda s, dst_yT=dst_yT, par=par:
                                          s.activation(dst_yT, ps_tr[par][:],
                                                       AF.Identity),
                                          waits=[(s_pe, ev_tr)], inc=s_act)
                    yT_ev[u] = tr_evac[par]
                    prev_tr_pe = ev_tr

                first_chunk = True
                if ph == "all":
                    clist = [4, 5] if group[0] < HALO_A else [0, 1, 2, 3, 4, 5]
                else:
                    clist = {"kv": [4, 5], "qq": [0, 1, 2, 3]}[ph]
                cur_g = [None]
                for c in clist:
                    par = wchunk % 2
                    wvw = ([(s_pe, chunk_mm_done[wchunk - 2])]
                           if wchunk - 2 in chunk_mm_done else [])
                    if first_chunk:
                        wvw += [(s_dve, s_dve.n), (s_act, s_act.n)]
                        first_chunk = False
                    wled = s_wdma
                    weng = "pool"
                    ev_w = DMA(p, AP(arena, par * (16 * CW),
                                     [[24576, 128], [CW, 16], [1, CW]]),
                               AP(wqkv.tensor, CW * c,
                                  [[3072, 128], [128 * 3072, 16], [1, CW]]),
                               led=wled, eng=weng, waits=wvw)
                    if ag_pending[0] is not None:
                        em(p, "pool", lambda g: g.collective_compute(
                            "AllGather", ALU.bypass, RG, [ag_in], [ag_out]),
                            waits=[(s_dma, ag_pending[0])], inc=s_cc, k=1)
                        ag_pending[0] = None
                    for gi2, jslot, si, u in units_iter:
                        if ph == "qq" and cur_g[0] != gi2:
                            gtab = load_ropetab(gi2 if is_a else 3 + gi2)
                            for uu in groups[gi2]:
                                cos_ev[uu] = gtab
                            cur_g[0] = gi2
                        halo, qb, kb = uinfo(u)
                        qp = citer % 2
                        wv = [(wled, ev_w), (s_act, yT_ev[u])]
                        if qkv_done[qp]:
                            wv.append((s_dve, qkv_done[qp]))
                        ev_mm = None
                        for dc in range(16):
                            if (not is_a) and gi2 == 0:
                                if dc < 8:
                                    lhs = kT[:, jslot * 2048 + dc * 128:
                                             jslot * 2048 + dc * 128 + 128]
                                else:
                                    lhs = v_sb[:, jslot * 2048 + (dc - 8) * 128:
                                               jslot * 2048 + (dc - 8) * 128 + 128]
                            else:
                                lhs = a16(yTg_off + jslot * 2048 + dc * 128, 128)
                            rhs = wbuf[par][:, dc * CW:(dc + 1) * CW]
                            ev_mm = em(p, "pe", lambda t, lhs=lhs, rhs=rhs, dc=dc,
                                       qp=qp: t.matmul(ps_qkv[qp][:, 0:CW], lhs, rhs,
                                                       start=(dc == 0),
                                                       stop=(dc == 15)),
                                       waits=(wv if dc == 0 else []),
                                       inc=(s_pe if dc == 15 else None))
                        chunk_mm_done[wchunk] = ev_mm
                        if len(pending_tr) > 1:
                            pending_tr.pop(0)()
                        sl = slots_of(c)
                        qk_slots = [s0 for s0 in sl if s0[0] in ("q", "k")]
                        v_slots = [s0 for s0 in sl if s0[0] == "v"]
                        last_consume = None
                        if qk_slots:
                            ns = len(qk_slots)
                            lo = qk_slots[0][2] * 128
                            if is_a:
                                off2, stp = 64, 1
                            else:
                                off2, stp = 1, 2
                            P1 = AP(ps_qkv[qp], lo, [[512, 128], [128, ns], [stp, 64]])
                            P2 = AP(ps_qkv[qp], lo + off2,
                                    [[512, 128], [128, ns], [stp, 64]])
                            C1 = AP(cosg, si * 128, [[512, 128], [0, ns], [stp, 64]])
                            C2 = AP(cosg, si * 128 + off2,
                                    [[512, 128], [0, ns], [stp, 64]])
                            S1 = AP(sing, si * 128, [[512, 128], [0, ns], [stp, 64]])
                            S2 = AP(sing, si * 128 + off2,
                                    [[512, 128], [0, ns], [stp, 64]])
                            m1 = AP(rsc, 768, [[1792, 128], [64, ns], [1, 64]])
                            m2 = AP(rsc, 1024, [[1792, 128], [64, ns], [1, 64]])
                            kp = citer % 2
                            if is_a:
                                o1 = AP(rsc, 1280, [[1792, 128], [128, ns], [stp, 64]])
                                o2 = AP(rsc, 1280 + off2,
                                        [[1792, 128], [128, ns], [stp, 64]])
                            else:
                                o1 = AP(qk16, kp * 512, [[1024, 128], [128, ns],
                                                         [stp, 64]])
                                o2 = AP(qk16, kp * 512 + off2,
                                        [[1024, 128], [128, ns], [stp, 64]])
                            qkr = rsc[:, 1280:1280 + 128 * ns]
                            wvr = [(s_pe, ev_mm), cos_ev[u][0], cos_ev[u][1]]
                            if tr_read[kp]:
                                wvr.append((s_pe, tr_read[kp]))
                            em(p, "dve", lambda v, P1=P1, C1=C1, m1=m1: v.tensor_tensor(
                                m1, P1, C1, ALU.mult), waits=wvr)
                            em(p, "dve", lambda v, P2=P2, S1=S1, m2=m2: v.tensor_tensor(
                                m2, P2, S1, ALU.mult))
                            em(p, "dve", lambda v: v.drain())
                            em(p, "dve", lambda v, o1=o1, m1=m1, m2=m2: v.tensor_tensor(
                                o1, m1, m2, ALU.add))
                            em(p, "dve", lambda v, P2=P2, C2=C2, m1=m1: v.tensor_tensor(
                                m1, P2, C2, ALU.mult))
                            em(p, "dve", lambda v, P1=P1, S2=S2, m2=m2: v.tensor_tensor(
                                m2, P1, S2, ALU.mult))
                            em(p, "dve", lambda v: v.drain())
                            ev_rope = em(p, "dve", lambda v, o2=o2, m1=m1, m2=m2:
                                         v.tensor_tensor(o2, m1, m2, ALU.add),
                                         inc=s_dve)
                            if is_a:
                                ev_sq = em(p, "act", lambda s, qkr=qkr, ns=ns:
                                           s.activation(rsc[:, 768:768 + 128 * ns], qkr,
                                                        AF.Square),
                                           waits=[(s_dve, ev_rope)], inc=s_act)
                                em(p, "dve", lambda v, ns=ns: v.tensor_reduce(
                                    small[:, 244:244 + ns],
                                    AP(rsc, 768, [[1792, 128], [128, ns], [1, 128]]),
                                    AX.X, ALU.add), waits=[(s_act, ev_sq)])
                                em(p, "dve", lambda v: v.drain())
                                em(p, "dve", lambda v, ns=ns: v.tensor_single_scalar(
                                    small[:, 244:244 + ns], small[:, 244:244 + ns],
                                    1e-24, ALU.max))
                                ev_ns = em(p, "dve", lambda v: v.drain(), inc=s_dve)
                                em(p, "act", lambda s, ns=ns: s.activation(
                                    small[:, 248:248 + ns], small[:, 244:244 + ns],
                                    AF.Ln), waits=[(s_dve, ev_ns)])
                                ev_rn = em(p, "act", lambda s, ns=ns: s.activation(
                                    small[:, 248:248 + ns], small[:, 248:248 + ns],
                                    AF.Exp, scale=-0.5), inc=s_act)
                                evq = None
                                for j in range(ns):
                                    evq = em(p, "dve", lambda v, j=j, kp=kp:
                                             v.tensor_single_scalar(
                                        qk16[:, kp * 512 + j * 128:
                                             kp * 512 + (j + 1) * 128],
                                        rsc[:, 1280 + j * 128:1280 + (j + 1) * 128],
                                        small[:, 248 + j:249 + j], ALU.mult),
                                        waits=([(s_act, ev_rn)] if j == 0 else []),
                                        inc=(s_dve if j == ns - 1 else None))
                            else:
                                evq = ev_rope
                            last_consume = evq

                            def mk_tr(evq=evq, kp=kp, qk_slots=tuple(qk_slots),
                                      qb=qb, kb=kb):
                                tp = trctr[0] % 2
                                trctr[0] += 1
                                wv2 = [(s_dve, evq)]
                                if tr_evac[tp]:
                                    wv2.append((s_act, tr_evac[tp]))
                                ev_t2 = None
                                for j in range(len(qk_slots)):
                                    ev_t2 = em(p, "pe", lambda t, j=j, tp=tp, kp=kp:
                                               t.transpose(
                                        ps_tr[tp][:, j * 128:(j + 1) * 128],
                                        qk16[:, kp * 512 + j * 128:
                                             kp * 512 + (j + 1) * 128],
                                        ident16[:]),
                                        waits=(wv2 if j == 0 else []),
                                        inc=(s_pe if j == len(qk_slots) - 1 else None))
                                tr_read[kp] = ev_t2
                                last_evac = None
                                for j, (kind, idx, slot_i) in enumerate(qk_slots):
                                    if kind == "q":
                                        dst = big16[:, idx * 1024 + qb * 128:
                                                    idx * 1024 + qb * 128 + 128]
                                    else:
                                        dst = kT[:, (idx * 16 + kb) * 128:
                                                 (idx * 16 + kb) * 128 + 128]
                                    last_evac = em(p, "act",
                                                   lambda s, j=j, tp=tp, dst=dst:
                                                   s.activation(
                                        dst, ps_tr[tp][:, j * 128:(j + 1) * 128],
                                        AF.Identity),
                                        waits=([(s_pe, ev_t2)] if j == 0 else []),
                                        inc=(s_act if j == len(qk_slots) - 1
                                             else None))
                                tr_evac[tp] = last_evac
                            pending_tr.append(mk_tr)
                        if v_slots:
                            lo = v_slots[0][2] * 128
                            nv = len(v_slots)
                            kvh0 = v_slots[0][1]
                            vdst = AP(v_sb, (kvh0 * 16 + kb) * 128,
                                      [[8192, 128], [2048, nv], [1, 128]])
                            vsrc = AP(ps_qkv[qp], lo, [[512, 128], [128, nv], [1, 128]])
                            last_consume = em(p, "dve", lambda v, vdst=vdst, vsrc=vsrc:
                                              v.tensor_copy(vdst, vsrc),
                                              waits=[(s_pe, ev_mm)], inc=s_dve)
                        qkv_done[qp] = s_dve.n
                        citer += 1
                    wchunk += 1
            for fn in pending_tr:
                fn()
            pending_tr = []
            p.run(nc)

        if DEBUG_PROJ and is_a:
            p = Plan()
            DMA(p, kT_dbg, kT[:, :], eng="sync")
            DMA(p, v_dbg, v_sb[:, :], eng="sync")
            ev0 = DMA(p, q_dbg, big16[:, :], eng="sync")
            em(p, "sync", lambda sy: sy.nop(), waits=[(s_dma, ev0)])
            p.run(nc)

        if not is_a:
            def unpack_k(g):
                with g.register("rk") as rk:
                    g.cc_rank_ld(rk, replica_groups=RG)
                    g.reg_add(rk, rk, 7)
                    g.reg_mod(rk, rk, 8)
                    g.reg_mul(rk, rk, 128 * 8192)
                    return g.dma_start(
                        out=AP(kT, 0, [[8192, 128], [2048, 4], [1, 1024]]),
                        in_=AP(ag_out.tensor, rk, [[8192, 128], [1024, 4], [1, 1024]]))

            def unpack_v(g):
                with g.register("rk2") as rk:
                    g.cc_rank_ld(rk, replica_groups=RG)
                    g.reg_add(rk, rk, 7)
                    g.reg_mod(rk, rk, 8)
                    g.reg_mul(rk, rk, 128 * 8192)
                    g.reg_add(rk, rk, 4096)
                    return g.dma_start(
                        out=AP(v_sb, 0, [[8192, 128], [2048, 4], [1, 1024]]),
                        in_=AP(ag_out.tensor, rk, [[8192, 128], [1024, 4], [1, 1024]]))

        p = Plan()
        upk_ev = None
        if not is_a:
            em(p, "pool", unpack_k, waits=[(s_cc, s_cc.n)], inc=s_hdma, k=16)
            upk_ev = em(p, "pool", unpack_v, inc=s_hdma, k=16)
        with ExitStack() as pst:
            ps_l = [pst.enter_context(nc.psum_tensor(f"pl{variant}{i}", [128, 1024],
                                                     F32)) for i in range(2)]
            ps_ao = pst.enter_context(nc.psum_tensor(f"pa{variant}", [128, 1024], F32))
            ps_s = pst.enter_context(nc.psum_tensor(f"pss{variant}", [128, 1024], F32))
            probs = [a16(0, 1024), a16(1024, 1024)]
            AO = 2048
            rbuf_v = a32(18432, 1024)
            lnr_v = a32(20480, 1024)

            exp_done = [0, 0]
            mask_done = [0, 0]
            pv_done = [0, 0]
            aodiv_ev = 0
            r_ev = 0
            for grp in range(4):
                for kvh in range(NKV):
                    kbs = [kb for kb in range(nkb) if runs_for(is_a, grp, kb)]
                    if not is_a:
                        kbs = [kb for kb in kbs if kb >= 8] + \
                              [kb for kb in kbs if kb < 8]
                    bank_started = [False, False]
                    esm = [0]
                    halo_waited = [False]
                    pend = []

                    def emit_pv(ent, kvh=kvh, bank_started=bank_started, esm=esm):
                        runs, lp, kb, wv_pv, first_kvh = ent
                        first = True
                        for qp2 in range(2):
                            for (q0, ln, masks) in runs:
                                stA = not bank_started[qp2]
                                wv = list(wv_pv) if first else []
                                if first and first_kvh:
                                    if aodiv_ev:
                                        wv.append((s_dve, aodiv_ev))
                                    if r_ev:
                                        wv.append((s_act, r_ev))
                                first = False
                                vt = v_sb[:, (kvh * 16 + kb) * 128:
                                          (kvh * 16 + kb) * 128 + 128]
                                pr = AP(arena, lp * 1024 + qp2 * 512 + q0,
                                        [[24576, 128], [256, 2], [1, ln]])
                                aod = AP(ps_ao, qp2 * 512 + q0,
                                         [[1024, 128], [256, 2], [1, ln]])
                                smd = AP(ps_s, qp2 * 512 + q0,
                                         [[1024, 128], [256, 2], [1, ln]])
                                em(p, "pe", lambda t, aod=aod, vt=vt, pr=pr,
                                   stA=stA: t.matmul(aod, vt, pr, start=stA,
                                                     stop=False),
                                   waits=wv, inc=s_pe)
                                esm[0] = em(p, "pe", lambda t, smd=smd, pr=pr,
                                            stA=stA: t.matmul(smd, ones_bf[:], pr,
                                                              start=stA, stop=False),
                                            inc=s_pe)
                                bank_started[qp2] = True
                        pv_done[lp] = s_pe.n

                    kiter = 0
                    for kb in kbs:
                        runs = runs_for(is_a, grp, kb)
                        lp = kiter % 2
                        ev_mm_l = None
                        first = True
                        for qp2 in range(2):
                            h = kvh * 4 + 2 * qp2
                            for (q0, ln, masks) in runs:
                                wq = []
                                if first:
                                    if exp_done[lp]:
                                        wq.append((s_act, exp_done[lp]))
                                    if (not is_a) and kb < 8 and not halo_waited[0]:
                                        wq.append((s_hdma, upk_ev))
                                        halo_waited[0] = True
                                    first = False
                                lhs = kT[:, (kvh * 16 + kb) * 128:
                                         (kvh * 16 + kb) * 128 + 128]
                                rhs = AP(big16, h * 1024 + grp * 256 + q0,
                                         [[16384, 128], [1024, 2], [1, ln]])
                                dst = AP(ps_l[lp], qp2 * 512 + q0,
                                         [[1024, 128], [256, 2], [1, ln]])
                                ev_mm_l = em(p, "pe", lambda t, dst=dst, lhs=lhs,
                                             rhs=rhs: t.matmul(dst, lhs, rhs,
                                                               start=True, stop=True),
                                             waits=wq, inc=s_pe)
                        ev_exp = None
                        first = True
                        for (q0, ln, masks) in runs:
                            wv = [(s_pe, ev_mm_l)]
                            if first:
                                if pv_done[lp]:
                                    wv.append((s_pe, pv_done[lp]))
                                if mask_done[lp]:
                                    wv.append((s_dve, mask_done[lp]))
                                first = False
                            pdst = AP(arena, (lp * 1024) + q0, [[24576, 128], [256, 4],
                                                               [1, ln]])
                            psrc = AP(ps_l[lp], q0, [[1024, 128], [256, 4], [1, ln]])
                            bias = hbB[:, hb_off + kb:hb_off + kb + 1]
                            ev_exp = em(p, "act", lambda s, pdst=pdst, psrc=psrc,
                                        bias=bias: s.activation(
                                pdst, psrc, AF.Exp, scale=SCALE, bias=bias),
                                waits=wv, inc=s_act)
                        exp_done[lp] = ev_exp
                        nmask = 0
                        for (q0, ln, masks) in runs:
                            for (off, mcls) in masks:
                                mt = {"causal": mcaus,
                                      "upper": (mstrU if is_a else minclU)}[mcls]
                                pap = AP(arena, lp * 1024 + q0 + off,
                                         [[24576, 128], [256, 4], [1, 128]])
                                mb = AP(mt, 0, [[128, 128], [0, 4], [1, 128]])
                                em(p, "dve", lambda g, pap=pap, mb=mb:
                                   g.tensor_tensor(pap, pap, mb, ALU.mult),
                                   waits=([(s_act, ev_exp)] if nmask == 0 else []),
                                   inc=s_dve)
                                nmask += 1
                        if nmask:
                            mask_done[lp] = s_dve.n
                        wv_pv = [(s_act, ev_exp)]
                        if nmask:
                            wv_pv.append((s_dve, mask_done[lp]))
                        pend.append((runs, lp, kb, wv_pv, kiter == 0))
                        if len(pend) >= 2:
                            emit_pv(pend.pop(0))
                        kiter += 1
                    while pend:
                        emit_pv(pend.pop(0))
                    em(p, "act", lambda s: s.activation(lnr_v, ps_s[:], AF.Ln),
                       waits=[(s_pe, esm[0])])
                    r_ev = em(p, "act", lambda s: s.activation(
                        rbuf_v, lnr_v, AF.Exp, scale=-1.0), inc=s_act)
                    ao_dst = AP(arena, AO + (kvh * 4) * 1024 + grp * 256,
                                [[24576, 128], [1024, 4], [1, 256]])
                    ao_src = AP(ps_ao, 0, [[1024, 128], [256, 4], [1, 256]])
                    rr = rbuf_v.rearrange("p (q c) -> p q c", q=4)
                    aodiv_ev = em(p, "dve", lambda v, ao_dst=ao_dst, ao_src=ao_src,
                                  rr=rr: v.tensor_tensor(ao_dst, ao_src, rr, ALU.mult),
                                  waits=[(s_act, r_ev)], inc=s_dve)
            p.run(nc)

        p = Plan()
        with ExitStack() as pst:
            ps_o = [pst.enter_context(nc.psum_tensor(f"po{variant}{i}", [128, 512],
                                                     F32)) for i in range(2)]
            AO = 2048
            evac_ev = [0, 0]
            wo_mm_done = {}
            for c in range(4):
                par = c % 2
                wv = [(s_pe, wo_mm_done[c - 2])] if c - 2 in wo_mm_done else []
                wled = s_wdma
                weng = "pool"
                ev_w = DMA(p, AP(big16, par * 8192, [[16384, 128], [512, 16], [1, 512]]),
                           AP(wo.tensor, 512 * c, [[D, 128], [128 * D, 16], [1, 512]]),
                           led=wled, eng=weng, waits=wv)
                for qb in range(NQB):
                    op = (c * NQB + qb) % 2
                    wv2 = [(wled, ev_w)]
                    if evac_ev[op]:
                        wv2.append((s_dve, evac_ev[op]))
                    ev_mm = None
                    for hh in range(16):
                        lhs = AP(arena, AO + hh * 1024 + qb * 128, [[24576, 128],
                                                                   [1, 128]])
                        rhs = big16[:, par * 8192 + hh * 512:par * 8192 + (hh + 1) * 512]
                        ev_mm = em(p, "pe", lambda t, lhs=lhs, rhs=rhs, hh=hh, op=op:
                                   t.matmul(ps_o[op][:], lhs, rhs, start=(hh == 0),
                                            stop=(hh == 15)),
                                   waits=(wv2 if hh == 0 else []),
                                   inc=(s_pe if hh == 15 else None))
                    wo_mm_done[c] = ev_mm
                    xd = x_sb[:, qb * D + 512 * c:qb * D + 512 * c + 512]
                    evac_ev[op] = em(p, "dve", lambda v, xd=xd, op=op: v.tensor_tensor(
                        xd, xd, ps_o[op][:], ALU.add), waits=[(s_pe, ev_mm)],
                        inc=s_dve)
            p.run(nc)


    ffn_ct = [0]

    ffn_ct = [0]

    def ffn_layer():
        ffi = ffn_ct[0]; ffn_ct[0] += 1
        gssB = a32(0, 2048)
        bssB = a32(4096, 2048)
        wbuf = a32(8192, 2048)
        s2buf = a32(12288, 2048)
        pwbuf = a32(16384, 2048)
        ps2buf = a32(20480, 2048)
        BT_off, ST_off = 16384, 20480
        BT = a16(BT_off, 4096)
        ST = a16(ST_off, 4096)
        bsel16 = qk16[:, 512:768]
        ssel16 = qk16[:, 768:1024]
        ts_all = rsc[:, 0:512]
        OH_all = rsc[:, 512:1024]
        v64 = rsc[:, 1024:1536]

        p = Plan()
        with ExitStack() as pst:
            pp = [pst.enter_context(nc.psum_tensor(f"pp{ffi}_{i}", [128, 1536], F32))
                  for i in range(2)]
            ps_t = pst.enter_context(nc.psum_tensor(f"ps_t{ffi}", [128, 128], F32))
            ps_t16 = pst.enter_context(nc.psum_tensor(f"ps_t16{ffi}", [128, 128], BF16))
            ps_b2 = pp[0]

            prev_mm = 0
            prev_cp = 0
            for ri, dstB in ((0, gssB), (1, bssB)):
                wv = [(s_pe, prev_mm)] if prev_mm else []
                ev_l = DMA(p, wbuf[0:1, :], rows_d[ri:ri + 1, :], led=s_hdma,
                           waits=wv)
                for j in range(4):
                    wvj = [(s_hdma, ev_l)] if j == 0 else []
                    if prev_cp:
                        wvj.append((s_dve, prev_cp))
                    ev_b = em(p, "pe", lambda t, j=j: t.matmul(
                        ps_b2[:, 0:512], ones1[:], wbuf[0:1, j * 512:(j + 1) * 512],
                        start=True, stop=True), waits=wvj, inc=s_pe)
                    prev_cp = em(p, "dve", lambda v, j=j, dstB=dstB: v.tensor_copy(
                        dstB[:, j * 512:(j + 1) * 512], ps_b2[:, 0:512]),
                        waits=[(s_pe, ev_b)], inc=s_dve)
                prev_mm = s_pe.n
            ev_bc = s_dve.n

            ev_acc = None
            for qb in range(NQB):
                xs_ap = x_sb[:, qb * D:(qb + 1) * D]
                em(p, "act", lambda s, xs_ap=xs_ap, qb=qb: s.activation(
                    y16[:], xs_ap, AF.Square, accum_out=stats[:, 32 + qb:33 + qb]))
                ev_acc = em(p, "act", lambda s, xs_ap=xs_ap, qb=qb: s.activation(
                    y16[:], xs_ap, AF.Identity, accum_out=stats[:, qb:qb + 1]),
                    inc=s_act)
            mu8 = small[:, 224:232]
            t8 = small[:, 232:240]
            v8 = small[:, 240:248]
            i2b = AP(consts, 2, [[8, 128], [0, 8]])
            e1b = AP(consts, 1, [[8, 128], [0, 8]])
            em(p, "dve", lambda v: v.tensor_tensor(mu8, stats[:, 0:8], i2b, ALU.mult),
               waits=[(s_act, ev_acc)])
            em(p, "dve", lambda v: v.drain())
            em(p, "dve", lambda v: v.tensor_single_scalar(stats[:, 64:72], mu8, -1.0,
                                                          ALU.mult))
            em(p, "dve", lambda v: v.tensor_tensor(t8, mu8, mu8, ALU.mult))
            em(p, "dve", lambda v: v.drain())
            em(p, "dve", lambda v: v.tensor_tensor(v8, stats[:, 32:40], i2b, ALU.mult))
            em(p, "dve", lambda v: v.drain())
            ev1 = em(p, "dve", lambda v: v.tensor_sub(v8, v8, t8), inc=s_dve)
            em(p, "act", lambda s: s.activation(t8, v8, AF.Ln, bias=consts[:, 1:2]),
               waits=[(s_dve, ev1)])
            em(p, "act", lambda s: s.activation(stats[:, 96:104], t8, AF.Exp,
                                                scale=-0.5))
            ev2 = em(p, "act", lambda s: s.activation(small[:, 248:256], t8, AF.Exp,
                                                      scale=-1.0), inc=s_act)
            em(p, "dve", lambda v: v.tensor_tensor(small[:, 248:256], v8,
                                                   small[:, 248:256], ALU.mult),
               waits=[(s_act, ev2)])
            ev3 = em(p, "dve", lambda v: v.drain(), inc=s_dve)
            em(p, "act", lambda s: s.activation(small[:, 248:256], small[:, 248:256],
                                                AF.Ln, bias=consts[:, 1:2]),
               waits=[(s_dve, ev3)])
            ev4 = em(p, "act", lambda s: s.activation(small[:, 248:256],
                                                      small[:, 248:256], AF.Exp,
                                                      scale=-0.5), inc=s_act)
            em(p, "dve", lambda v: v.tensor_tensor(stats[:, 128:136], stats[:, 96:104],
                                                   small[:, 248:256], ALU.mult),
               waits=[(s_act, ev4)])
            ev_C = em(p, "dve", lambda v: v.drain(), inc=s_dve)
            em(p, "dve", lambda v: v.tensor_tensor(stats[:, 8:16], stats[:, 64:72],
                                                   stats[:, 96:104], ALU.mult))
            em(p, "dve", lambda v: v.tensor_tensor(stats[:, 16:24], stats[:, 64:72],
                                                   stats[:, 128:136], ALU.mult))
            ev_bias8 = em(p, "dve", lambda v: v.drain(), inc=s_dve)

            POOL_QBS = ()
            first_eng = {"dve": True, "pool": True}
            red_ev = {}
            pend1 = []

            def flush1(ent):
                qb, s2, ev_s2, sem_e = ent
                ev_sgn = em(p, "act", lambda s, s2=s2: s.activation(
                    s2[:], s2[:], AF.Sign), waits=[(sem_e, ev_s2)], inc=s_act)
                red_ev[qb] = em(p, "dve", lambda v, qb=qb, s2=s2: v.tensor_reduce(
                    ts_all[:, qb * 64:(qb + 1) * 64],
                    s2.rearrange("p (t j) -> p t j", t=64), AX.X, ALU.add),
                    waits=[(s_act, ev_sgn)], inc=s_dve)

            for qb in range(NQB):
                eng = "pool" if qb in POOL_QBS else "dve"
                sem_e = s_pool if eng == "pool" else s_dve
                if eng == "pool":
                    wb, s2 = pwbuf, ps2buf
                else:
                    wb = wbuf if qb % 2 == 0 else pwbuf
                    s2 = s2buf if qb % 2 == 0 else ps2buf
                xs_ap = x_sb[:, qb * D:(qb + 1) * D]
                wv0 = ([(s_dve, ev_C), (s_dve, ev_bc)]
                       if first_eng[eng] else [])
                first_eng[eng] = False
                if eng == "pool":
                    nmb = AP(stats, 64 + qb, [[156, 128], [0, D]])
                    cb = AP(stats, 128 + qb, [[156, 128], [0, D]])
                    rb = AP(stats, 96 + qb, [[156, 128], [0, D]])
                    em(p, eng, lambda v, xs_ap=xs_ap, wb=wb, nmb=nmb:
                       v.tensor_tensor(wb[:], xs_ap, nmb, ALU.add), waits=wv0)
                    em(p, eng, lambda v, wb=wb, s2=s2, rb=rb: v.tensor_tensor(
                        s2[:], wb[:], rb, ALU.mult))
                    em(p, eng, lambda v, xs_ap=xs_ap, s2=s2: v.tensor_tensor(
                        xs_ap, xs_ap, s2[:], ALU.add))
                    em(p, eng, lambda v, wb=wb, cb=cb: v.tensor_tensor(
                        wb[:], wb[:], cb, ALU.mult))
                    em(p, eng, lambda v, wb=wb, s2=s2: v.tensor_tensor(
                        s2[:], wb[:], gssB[:], ALU.mult))
                else:
                    em(p, "act", lambda s, xs_ap=xs_ap, qb=qb, s2=s2: s.activation(
                        s2[:], xs_ap, AF.Identity,
                        scale=stats[:, 96 + qb:97 + qb],
                        bias=stats[:, 8 + qb:9 + qb]),
                        waits=([(s_dve, ev_bias8)] if qb < 2 else
                               [(s_dve, red_ev[qb - 2])]))
                    ev_w = em(p, "act", lambda s, xs_ap=xs_ap, qb=qb, wb=wb:
                              s.activation(
                        wb[:], xs_ap, AF.Identity,
                        scale=stats[:, 128 + qb:129 + qb],
                        bias=stats[:, 16 + qb:17 + qb]), inc=s_act)
                    em(p, "dve", lambda v, xs_ap=xs_ap, s2=s2: v.tensor_tensor(
                        xs_ap, xs_ap, s2[:], ALU.add), waits=[(s_act, ev_w)])
                    em(p, "dve", lambda v, wb=wb, s2=s2: v.tensor_tensor(
                        s2[:], wb[:], gssB[:], ALU.mult))
                ev_s2 = em(p, eng, lambda v, s2=s2: v.tensor_tensor(
                    s2[:], s2[:], bssB[:], ALU.add), inc=sem_e)
                pend1.append((qb, s2, ev_s2, sem_e))
                if len(pend1) >= 2:
                    flush1(pend1.pop(0))
            while pend1:
                flush1(pend1.pop(0))
            ev_ts_pool = s_pool.n

            cs_all = small[:, 0:64]
            m8 = small[:, 64:72]
            bc8 = small[:, 72:80]
            em(p, "dve", lambda v: v.tensor_reduce(
                cs_all, ts_all.rearrange("p (q c t) -> p (q c) t", q=8, c=8),
                AX.X, ALU.add), waits=[(s_pool, ev_ts_pool)])
            em(p, "dve", lambda v: v.drain())
            em(p, "dve", lambda v: v.tensor_reduce(
                m8, cs_all.rearrange("p (q c) -> p q c", q=8), AX.X, ALU.max))
            em(p, "dve", lambda v: v.drain())
            m8b = AP(small, 64, [[256, 128], [1, 8], [0, 8]])
            em(p, "dve", lambda v, m8b=m8b: v.tensor_tensor(
                v64[:, 0:64].rearrange("p (q c) -> p q c", q=8),
                cs_all.rearrange("p (q c) -> p q c", q=8), m8b, ALU.is_ge))
            em(p, "dve", lambda v: v.drain())
            i8t = AP(iota8m, 0, [[8, 128], [0, 8], [1, 8]])
            em(p, "dve", lambda v, i8t=i8t: v.tensor_tensor(
                v64[:, 0:64].rearrange("p (q c) -> p q c", q=8),
                v64[:, 0:64].rearrange("p (q c) -> p q c", q=8), i8t, ALU.mult))
            em(p, "dve", lambda v: v.drain())
            em(p, "dve", lambda v: v.tensor_reduce(
                bc8, v64[:, 0:64].rearrange("p (q c) -> p q c", q=8), AX.X, ALU.min))
            em(p, "dve", lambda v: v.drain())
            em(p, "dve", lambda v: v.tensor_single_scalar(bc8, bc8, BIGV, ALU.add))
            em(p, "dve", lambda v: v.drain())
            ca_t = AP(ca64, 0, [[64, 128], [0, 8], [1, 64]])
            bc_b = AP(small, 72, [[256, 128], [1, 8], [0, 64]])
            em(p, "dve", lambda v, ca_t=ca_t, bc_b=bc_b: v.tensor_tensor(
                v64.rearrange("p (q t) -> p q t", q=8), ca_t, bc_b, ALU.is_equal))
            em(p, "dve", lambda v: v.drain())
            em(p, "dve", lambda v: v.tensor_single_scalar(ts_all[:], ts_all[:], BIGV,
                                                          ALU.add))
            em(p, "dve", lambda v: v.drain())
            em(p, "dve", lambda v: v.tensor_tensor(v64[:], v64[:], ts_all[:], ALU.mult))
            em(p, "dve", lambda v: v.drain())
            em(p, "dve", lambda v: v.tensor_single_scalar(v64[:], v64[:], BIGV,
                                                          ALU.subtract))
            em(p, "dve", lambda v: v.drain())
            em(p, "dve", lambda v: v.tensor_reduce(
                m8, v64.rearrange("p (q t) -> p q t", q=8), AX.X, ALU.max))
            em(p, "dve", lambda v: v.drain())
            m8b2 = AP(small, 64, [[256, 128], [1, 8], [0, 64]])
            em(p, "dve", lambda v, m8b2=m8b2: v.tensor_tensor(
                OH_all.rearrange("p (q t) -> p q t", q=8),
                v64.rearrange("p (q t) -> p q t", q=8), m8b2, ALU.is_ge))
            em(p, "dve", lambda v: v.drain())
            i64t = AP(iota64m, 0, [[64, 128], [0, 8], [1, 64]])
            em(p, "dve", lambda v, i64t=i64t: v.tensor_tensor(
                v64.rearrange("p (q t) -> p q t", q=8),
                OH_all.rearrange("p (q t) -> p q t", q=8), i64t, ALU.mult))
            em(p, "dve", lambda v: v.drain())
            em(p, "dve", lambda v: v.tensor_reduce(
                ffsc[:, FC_TIDX:FC_TIDX + 8],
                v64.rearrange("p (q t) -> p q t", q=8), AX.X, ALU.min))
            em(p, "dve", lambda v: v.drain())
            em(p, "dve", lambda v: v.tensor_single_scalar(
                ffsc[:, FC_TIDX:FC_TIDX + 8], ffsc[:, FC_TIDX:FC_TIDX + 8], BIGV,
                ALU.add))
            em(p, "dve", lambda v: v.drain())
            io64t = AP(iota64, 0, [[64, 128], [0, 8], [1, 64]])
            ti_b = AP(ffsc, FC_TIDX, [[816, 128], [1, 8], [0, 64]])
            ev_oh_all = em(p, "dve", lambda v, io64t=io64t, ti_b=ti_b: v.tensor_tensor(
                OH_all.rearrange("p (q t) -> p q t", q=8), io64t, ti_b, ALU.is_equal),
                inc=s_dve)
            em(p, "dve", lambda v: v.drain())

            em(p, "dve", lambda v: v.tensor_single_scalar(small[:, 224:232],
                                                          stats[:, 96:104], 1.0,
                                                          ALU.add))
            em(p, "dve", lambda v: v.drain())
            em(p, "dve", lambda v: v.reciprocal(small[:, 232:240], small[:, 224:232]))
            em(p, "dve", lambda v: v.drain())
            em(p, "dve", lambda v: v.tensor_tensor(small[:, 232:240], small[:, 232:240],
                                                   stats[:, 128:136], ALU.mult))
            em(p, "dve", lambda v: v.drain())
            em(p, "dve", lambda v: v.tensor_tensor(stats[:, 24:32], stats[:, 64:72],
                                                   small[:, 232:240], ALU.mult))
            ev_cc8 = em(p, "dve", lambda v: v.drain(), inc=s_dve)
            gt_evac = 0
            prev_g3 = 0
            stt_ev = {}
            tm_in = AP(iota64, 0, [[64, 128], [1, 64], [0, 32]])
            for qb in range(NQB):
                xs_ap = x_sb[:, qb * D:(qb + 1) * D]
                wb6 = wbuf if qb % 2 == 0 else pwbuf
                ev_wact = em(p, "act", lambda s, xs_ap=xs_ap, qb=qb, wb6=wb6:
                             s.activation(
                    wb6[:], xs_ap, AF.Identity,
                    scale=small[:, 232 + qb:233 + qb],
                    bias=stats[:, 24 + qb:25 + qb]),
                    waits=([(s_dve, ev_cc8)] if qb < 2 else
                           [(s_dve, stt_ev[qb - 2])]), inc=s_act)
                stt_ev[qb] = em(p, "dve", lambda v, qb=qb, wb6=wb6:
                                v.scalar_tensor_tensor(
                    s2buf.rearrange("p (t j) -> p t j", t=64), tm_in,
                    ffsc[:, FC_TIDX + qb:FC_TIDX + qb + 1],
                    wb6.rearrange("p (t j) -> p t j", t=64),
                    ALU.is_equal, ALU.mult), waits=[(s_act, ev_wact)], inc=s_dve)
                em(p, "dve", lambda v: v.drain())
                ev_wsel = em(p, "dve", lambda v, qb=qb: v.tensor_reduce(
                    small[:, 0:32], s2buf.rearrange("p (t j) -> p j t", t=64),
                    AX.X, ALU.add), inc=s_dve)
                ohs = OH_all[:, qb * 64:(qb + 1) * 64]
                wv_t = [(s_dve, ev_wsel)]
                if prev_g3:
                    wv_t.append((s_pe, prev_g3))
                ev_trf = em(p, "pe", lambda t, ohs=ohs: t.transpose(
                    ps_t[0:64, :], ohs[:, 0:64], identf[:]), waits=wv_t, inc=s_pe)
                wv_e = [(s_pe, ev_trf)]
                if gt_evac:
                    wv_e.append((s_dve, gt_evac))
                gt_evac = em(p, "dve", lambda v: v.tensor_copy(
                    ohTf[0:64, 0:128], ps_t[0:64, :]), waits=wv_e, inc=s_dve)
                prev_g3 = ev_g3 = em(p, "pe", lambda t: t.matmul(
                    pp[0][:, 1024:1090], ohTf[0:64, 0:128],
                    tableT[0:64, 0:66], start=True, stop=True),
                    waits=[(s_dve, gt_evac)], inc=s_pe)
                em(p, "dve", lambda v: v.tensor_copy(
                    gsel[:, 0:66], pp[0][:, 1024:1090]),
                    waits=[(s_pe, ev_g3)])
                em(p, "dve", lambda v: v.drain())
                em(p, "dve", lambda v: v.tensor_tensor(
                    small[:, 0:32], small[:, 0:32], gsel[:, 0:32], ALU.mult))
                em(p, "dve", lambda v: v.drain())
                em(p, "dve", lambda v, qb=qb: v.tensor_tensor(
                    ffsc[:, FC_XS + qb * 32:FC_XS + (qb + 1) * 32], small[:, 0:32],
                    gsel[:, 32:64], ALU.add))
                em(p, "dve", lambda v, qb=qb: v.tensor_copy(
                    ffsc[:, FC_OSEL + qb:FC_OSEL + qb + 1], gsel[:, 64:65]))
                em(p, "dve", lambda v: v.drain())
            em(p, "dve", lambda v: v.tensor_reduce(
                ffsc[:, FC_TMIN:FC_TMIN + 8],
                ffsc[:, FC_XS:FC_XS + 256].rearrange("p (q j) -> p q j", q=8),
                AX.X, ALU.min))
            em(p, "dve", lambda v: v.tensor_reduce(
                ffsc[:, FC_TMAX:FC_TMAX + 8],
                ffsc[:, FC_XS:FC_XS + 256].rearrange("p (q j) -> p q j", q=8),
                AX.X, ALU.max))
            em(p, "dve", lambda v: v.drain())
            ev_u8 = em(p, "dve", lambda v: v.tensor_copy(ohall_u8[:], OH_all),
                       inc=s_dve)
            em(p, "dve", lambda v: v.drain())
            em(p, "dve", lambda v: v.tensor_single_scalar(small[:, 76:77],
                                                          consts[:, 7:8], -1.0,
                                                          ALU.mult))
            em(p, "dve", lambda v: v.drain())
            u8m = ohall_u8.ap()
            tminb = AP(ffsc, FC_TMIN, [[816, 128], [1, 8], [0, 64]])
            tmaxb = AP(ffsc, FC_TMAX, [[816, 128], [1, 8], [0, 64]])
            bigp = AP(consts, 7, [[8, 128], [0, 8], [0, 64]])
            bign = AP(small, 76, [[256, 128], [0, 8], [0, 64]])
            ev_selmin = em(p, "dve", lambda v: v.select(
                v64.rearrange("p (q t) -> p q t", q=8),
                u8m.rearrange("p (q t) -> p q t", q=8), tminb, bigp), inc=s_dve)
            em(p, "dve", lambda v: v.drain())
            red_d = [0, 0]
            ev_sel = ev_selmin
            for mode, col0, alu in ((0, 0, ALU.min), (1, 10, ALU.max)):
                if mode == 1:
                    ev_sel = em(p, "dve", lambda v: v.select(
                        v64.rearrange("p (q t) -> p q t", q=8),
                        u8m.rearrange("p (q t) -> p q t", q=8), tmaxb, bign),
                        waits=[(s_dve, red_d[0]), (s_dve, red_d[1])], inc=s_dve)
                    em(p, "dve", lambda v: v.drain())
                for qb in range(NQB):
                    par = qb % 2
                    dst = ps_t if par == 0 else pp[1]
                    wv = [(s_dve, ev_sel)] if qb < 2 else []
                    if red_d[par]:
                        wv.append((s_dve, red_d[par]))
                    ev_tt = em(p, "pe", lambda t, qb=qb, dst=dst: t.transpose(
                        dst[0:64, 0:128], v64[:, qb * 64:(qb + 1) * 64], identf[:]),
                        waits=wv, inc=s_pe)
                    red_d[par] = em(p, "dve", lambda v, qb=qb, col0=col0, alu=alu,
                                    dst=dst: v.tensor_reduce(
                        tmin_s[0:64, col0 + qb:col0 + qb + 1], dst[0:64, 0:128],
                        AX.X, alu), waits=[(s_pe, ev_tt)], inc=s_dve)
            red_last = em(p, "dve", lambda v: v.drain(),
                          waits=[(s_dve, red_d[0]), (s_dve, red_d[1])], inc=s_dve)
            pp1_guard = red_d[1]
            em(p, "dve", lambda v: v.tensor_reduce(agr[0:64, 0:1], tmin_s[0:64, 0:8],
                                                   AX.X, ALU.min))
            em(p, "dve", lambda v: v.drain())
            em(p, "dve", lambda v: v.tensor_single_scalar(agr[0:64, 0:1],
                                                          agr[0:64, 0:1], -1.0,
                                                          ALU.mult))
            em(p, "dve", lambda v: v.tensor_reduce(agr[0:64, 1:2], tmin_s[0:64, 10:18],
                                                   AX.X, ALU.max))
            ev_agr = em(p, "dve", lambda v: v.drain(), inc=s_dve)
            ev_st = em(p, "sync", lambda sy: sy.dma_start(out=ar_in,
                                                          in_=agr[0:64, 0:2]),
                       waits=[(s_dve, ev_agr)], inc=s_dma, k=16)
            ev_cc = em(p, "pool", lambda g: g.collective_compute(
                "AllReduce", ALU.max, RG, [ar_in], [ar_out]),
                waits=[(s_dma, ev_st)], inc=s_cc, k=1)
            t16_evac = 0
            mm_done = [0, 0]
            prev_tr16 = 0
            for qb in range(NQB):
                par = qb % 2
                ohs = OH_all[:, qb * 64:(qb + 1) * 64]
                wv_oh = [(s_pe, prev_tr16)] if prev_tr16 else []
                ev_oh16 = em(p, "dve", lambda v, ohs=ohs: v.tensor_copy(
                    ohq16[:, 0:64], ohs), waits=wv_oh, inc=s_dve)
                wv_t2 = [(s_dve, ev_oh16)]
                if t16_evac:
                    wv_t2.append((s_dve, t16_evac))
                prev_tr16 = ev_tr16 = em(p, "pe", lambda t: t.transpose(
                    ps_t16[0:64, :], ohq16[:, 0:64], ident16[:]), waits=wv_t2,
                    inc=s_pe)
                t16_evac = em(p, "dve", lambda v: v.tensor_copy(
                    ohT16[0:64, 0:128], ps_t16[0:64, :]),
                    waits=[(s_pe, ev_tr16)], inc=s_dve)
                wv_mm = [(s_dve, t16_evac)]
                if mm_done[par]:
                    wv_mm.append((s_dve, mm_done[par]))
                elif par == 1:
                    wv_mm.append((s_dve, pp1_guard))
                em(p, "pe", lambda t, par=par: t.matmul(
                    pp[par][:, 0:512], ohT16[0:64, 0:128],
                    table16[0:64, 0:512], start=True, stop=True), waits=wv_mm)
                ev_g2 = em(p, "pe", lambda t, par=par: t.matmul(
                    pp[par][:, 512:1024], ohT16[0:64, 0:128],
                    table16[0:64, 512:1024], start=True, stop=True), inc=s_pe)
                em(p, "dve", lambda v, qb=qb, par=par: v.tensor_copy(
                    BT[:, qb * 512:(qb + 1) * 512], pp[par][:, 0:512]),
                    waits=[(s_pe, ev_g2)])
                mm_done[par] = em(p, "dve", lambda v, qb=qb, par=par: v.tensor_copy(
                    ST[:, qb * 512:(qb + 1) * 512], pp[par][:, 512:1024]),
                    inc=s_dve)
            ev_bst = s_dve.n
            ev_ld = em(p, "pool", lambda g: g.dma_start(out=agr[0:64, 2:4], in_=ar_out),
                       waits=[(s_cc, ev_cc)], inc=s_hdma, k=16)
            em(p, "dve", lambda v: v.tensor_single_scalar(
                agr[0:64, 2:3], agr[0:64, 2:3], -1.0, ALU.mult),
                waits=[(s_hdma, ev_ld)])
            ev_n = em(p, "dve", lambda v: v.drain(), inc=s_dve)
            ev_t1 = em(p, "pe", lambda t: t.transpose(
                ps_t[0:1, 0:64], agr[0:64, 2:3], identf[0:64, 0:64]),
                waits=[(s_dve, ev_n)], inc=s_pe)
            ev_t2 = em(p, "pe", lambda t: t.transpose(
                ps_t[0:1, 64:128], agr[0:64, 3:4], identf[0:64, 0:64]), inc=s_pe)
            em(p, "dve", lambda v: v.tensor_copy(small[0:1, 0:64], ps_t[0:1, 0:64]),
               waits=[(s_pe, ev_t2)])
            ev_rw = em(p, "dve", lambda v: v.tensor_copy(small[0:1, 64:128],
                                                         ps_t[0:1, 64:128]),
                       inc=s_dve)
            ev_b1 = em(p, "pe", lambda t: t.matmul(
                ps_t[:, 0:64], ones1[:], small[0:1, 0:64], start=True, stop=True),
                waits=[(s_dve, ev_rw)], inc=s_pe)
            ev_b2 = em(p, "pe", lambda t: t.matmul(
                ps_t[:, 64:128], ones1[:], small[0:1, 64:128], start=True,
                stop=True), inc=s_pe)
            em(p, "dve", lambda v: v.tensor_copy(small[:, 88:152], ps_t[:, 0:64]),
               waits=[(s_pe, ev_b2)])
            ev_gb = em(p, "dve", lambda v: v.tensor_copy(small[:, 152:216],
                                                         ps_t[:, 64:128]),
                       inc=s_dve)
            em(p, "dve", lambda v: v.drain())
            tgb = AP(small, 88, [[256, 128], [0, 8], [1, 64]])
            txb = AP(small, 152, [[256, 128], [0, 8], [1, 64]])
            em(p, "dve", lambda v, tgb=tgb: v.tensor_tensor(
                v64.rearrange("p (q t) -> p q t", q=8),
                OH_all.rearrange("p (q t) -> p q t", q=8), tgb, ALU.mult))
            em(p, "dve", lambda v: v.drain())
            em(p, "dve", lambda v: v.tensor_reduce(
                small[:, 96:104], v64.rearrange("p (q t) -> p q t", q=8),
                AX.X, ALU.add))
            em(p, "dve", lambda v, txb=txb: v.tensor_tensor(
                v64.rearrange("p (q t) -> p q t", q=8),
                OH_all.rearrange("p (q t) -> p q t", q=8), txb, ALU.mult))
            em(p, "dve", lambda v: v.drain())
            em(p, "dve", lambda v: v.tensor_reduce(
                small[:, 104:112], v64.rearrange("p (q t) -> p q t", q=8),
                AX.X, ALU.add))
            em(p, "dve", lambda v: v.drain())
            em(p, "dve", lambda v: v.tensor_sub(small[:, 112:120], small[:, 104:112],
                                                small[:, 96:104]))
            em(p, "dve", lambda v: v.drain())
            e8b = AP(consts, 6, [[8, 128], [0, 8]])
            em(p, "dve", lambda v, e8b=e8b: v.tensor_tensor(
                small[:, 112:120], small[:, 112:120], e8b, ALU.add))
            em(p, "dve", lambda v: v.drain())
            em(p, "dve", lambda v: v.reciprocal(small[:, 120:128], small[:, 112:120]))
            em(p, "dve", lambda v: v.drain())
            em(p, "dve", lambda v: v.tensor_single_scalar(
                small[:, 120:128], small[:, 120:128], 16.0, ALU.mult))
            em(p, "dve", lambda v: v.drain())
            U16MAX = float(np.float32(1.0 - 1e-6) * 16)
            mnb = AP(small, 96, [[256, 128], [1, 8], [0, 32]])
            invb = AP(small, 120, [[256, 128], [1, 8], [0, 32]])
            GI = ffsc[:, FC_GI:FC_GI + 256]
            XL = ffsc[:, FC_XL:FC_XL + 256]
            XS = ffsc[:, FC_XS:FC_XS + 256]
            em(p, "dve", lambda v, mnb=mnb: v.tensor_tensor(
                GI.rearrange("p (q j) -> p q j", q=8),
                XS.rearrange("p (q j) -> p q j", q=8), mnb, ALU.subtract))
            em(p, "dve", lambda v: v.drain())
            em(p, "dve", lambda v, invb=invb: v.tensor_tensor(
                GI.rearrange("p (q j) -> p q j", q=8),
                GI.rearrange("p (q j) -> p q j", q=8), invb, ALU.mult))
            em(p, "dve", lambda v: v.drain())
            em(p, "dve", lambda v: v.tensor_scalar(GI, GI, 0.0, U16MAX,
                                                   ALU.max, ALU.min))
            em(p, "dve", lambda v: v.drain())
            em(p, "dve", lambda v: v.tensor_scalar(XL, GI, MAGIC, MAGIC,
                                                   ALU.add, ALU.subtract))
            em(p, "dve", lambda v: v.drain())
            em(p, "dve", lambda v: v.tensor_tensor(small[:, 0:256], XL, GI,
                                                   ALU.is_gt))
            em(p, "dve", lambda v: v.drain())
            em(p, "dve", lambda v: v.tensor_sub(XL, XL, small[:, 0:256]))
            em(p, "dve", lambda v: v.drain())
            em(p, "dve", lambda v: v.tensor_sub(GI, GI, XL))
            em(p, "dve", lambda v: v.drain())
            eq16 = a16(8192, 4096)
            pbt16 = a16(0, 4096)
            pst16 = a16(4096, 4096)
            eq3v = eq16.rearrange("p (qj g) -> p qj g", g=16)
            GIb = AP(ffsc, FC_XL, [[816, 128], [1, 256], [0, 16]])
            g16 = AP(iota64, 0, [[64, 128], [0, 256], [1, 16]])
            ev_eq = em(p, "dve", lambda v: v.tensor_tensor(
                eq3v, GIb, g16, ALU.is_equal), inc=s_dve)
            em(p, "dve", lambda v: v.drain())
            BTg3 = AP(arena, BT_off, [[24576, 128], [16, 256], [1, 16]])
            STg3 = AP(arena, ST_off, [[24576, 128], [16, 256], [1, 16]])
            pb3 = pbt16.rearrange("p (qj g) -> p qj g", g=16)
            ps3 = pst16.rearrange("p (qj g) -> p qj g", g=16)
            bselF = cosg[:, 0:256]
            sselF = cosg[:, 256:512]
            USE_POOL_SPLINE = False
            if USE_POOL_SPLINE:
                ev_psTT = em(p, "pool", lambda v: v.tensor_tensor(
                    ps3, eq3v, STg3, ALU.mult), waits=[(s_dve, ev_eq)], inc=s_pool)
            else:
                ev_psTT = em(p, "dve", lambda v: v.tensor_tensor(
                    ps3, eq3v, STg3, ALU.mult), inc=s_dve)
            em(p, "dve", lambda v: v.tensor_tensor(pb3, eq3v, BTg3, ALU.mult))
            em(p, "dve", lambda v: v.drain())
            em(p, "dve", lambda v: v.tensor_reduce(bselF, pb3, AX.X, ALU.add))
            em(p, "dve", lambda v: v.tensor_reduce(sselF, ps3, AX.X, ALU.add),
               waits=([(s_pool, ev_psTT)] if USE_POOL_SPLINE else []))
            em(p, "dve", lambda v: v.drain())
            em(p, "dve", lambda v: v.tensor_tensor(small[:, 0:256], sselF,
                                                   GI, ALU.mult))
            em(p, "dve", lambda v: v.drain())
            em(p, "dve", lambda v: v.tensor_tensor(small[:, 0:256], small[:, 0:256],
                                                   bselF, ALU.add))
            em(p, "dve", lambda v: v.drain())
            osb = AP(ffsc, FC_OSEL, [[816, 128], [1, 8], [0, 32]])
            ev_delta = em(p, "dve", lambda v, osb=osb: v.tensor_tensor(
                small[:, 0:256].rearrange("p (q j) -> p q j", q=8),
                small[:, 0:256].rearrange("p (q j) -> p q j", q=8), osb, ALU.mult),
                inc=s_dve)
            em(p, "dve", lambda v: v.drain())
            pbufs = [wbuf, s2buf]
            pfree = [0, 0]
            for qb in range(NQB):
                to_pool = qb in POOL_QBS
                wb = pbufs[qb % 2] if to_pool else wbuf
                dexp = AP(small, qb * 32, [[256, 128], [0, 64], [1, 32]])
                wv = [(s_pool, pfree[qb % 2])] if (to_pool and pfree[qb % 2]) \
                    else []
                ev_w3 = em(p, "dve", lambda v, qb=qb, wb=wb, dexp=dexp:
                           v.scalar_tensor_tensor(
                    wb.rearrange("p (t j) -> p t j", t=64), tm_in,
                    ffsc[:, FC_TIDX + qb:FC_TIDX + qb + 1], dexp,
                    ALU.is_equal, ALU.mult), waits=wv, inc=s_dve)
                em(p, "dve", lambda v: v.drain())
                if to_pool:
                    pfree[qb % 2] = em(p, "pool", lambda v, qb=qb, wb=wb:
                                       v.tensor_tensor(
                        x_sb[:, qb * D:(qb + 1) * D],
                        x_sb[:, qb * D:(qb + 1) * D], wb[:], ALU.add),
                        waits=[(s_dve, ev_w3)], inc=s_pool)
                else:
                    em(p, "dve", lambda v, qb=qb, wb=wb: v.tensor_tensor(
                        x_sb[:, qb * D:(qb + 1) * D],
                        x_sb[:, qb * D:(qb + 1) * D], wb[:], ALU.add),
                        inc=s_dve)
            p.run(nc)

    attn_layer("a")
    if N_SUBLAYERS >= 2:
        ffn_layer()
    if DEBUG_FFN:
        p = Plan()
        DMA(p, ffsc_dbg, ffsc[:, :], eng="sync")
        DMA(p, tmin_dbg, tmin_s[0:64, :], eng="sync")
        DMA(p, agr_dbg, agr[0:64, :], eng="sync")
        DMA(p, bt_dbg, arena[:, 16384:20480], eng="sync")
        ev0 = DMA(p, st_dbg, arena[:, 20480:24576], eng="sync")
        em(p, "sync", lambda sy: sy.nop(), waits=[(s_dma, ev0)])
        p.run(nc)
    if N_SUBLAYERS >= 3:
        attn_layer("b")
    if N_SUBLAYERS >= 4:
        ffn_layer()

    p = Plan()
    ev = em(p, "sync", lambda sy: sy.dma_start(
        out=AP(x_out.tensor, 0, [[D, 128], [128 * D, NQB], [1, D]]),
        in_=AP(x_sb, 0, [[NQB * D, 128], [D, NQB], [1, D]])),
        waits=[(s_dve, s_dve.n), (s_pool, s_pool.n)], inc=s_dma, k=16)
    em(p, "sync", lambda sy: sy.nop(), waits=[(s_dma, ev)])
    p.run(nc)

    st.close()
    return nc


_NC_CACHE = None


def kernel(L0_attn_a_x, L0_attn_a_wq, L0_attn_a_wk, L0_attn_a_wv, L0_attn_a_wo,
           L0_attn_b_wq, L0_attn_b_wk, L0_attn_b_wv, L0_attn_b_wo,
           L0_ffn_ln_gamma, L0_ffn_ln_beta, L0_ffn_spline_bases,
           L0_ffn_spline_slopes, L0_ffn_output_scale):
    global _NC_CACHE
    x = np.ascontiguousarray(np.asarray(L0_attn_a_x, dtype=np.float32).reshape(8192, D))
    wqkv_a_h = np.ascontiguousarray(
        np.concatenate([L0_attn_a_wq, L0_attn_a_wk, L0_attn_a_wv], axis=1)
        .astype(np.float32))
    wqkv_b_h = np.ascontiguousarray(
        np.concatenate([L0_attn_b_wq, L0_attn_b_wk, L0_attn_b_wv], axis=1)
        .astype(np.float32))
    wo_a_h = np.ascontiguousarray(np.asarray(L0_attn_a_wo, dtype=np.float32))
    wo_b_h = np.ascontiguousarray(np.asarray(L0_attn_b_wo, dtype=np.float32))
    tb = np.ascontiguousarray(np.asarray(L0_ffn_spline_bases, dtype=np.float32))
    ts_ = np.ascontiguousarray(np.asarray(L0_ffn_spline_slopes, dtype=np.float32))
    lg = np.ascontiguousarray(np.asarray(L0_ffn_ln_gamma, dtype=np.float32))
    lb = np.ascontiguousarray(np.asarray(L0_ffn_ln_beta, dtype=np.float32))
    osc = np.ascontiguousarray(np.asarray(L0_ffn_output_scale, dtype=np.float32))

    if _NC_CACHE is None:
        _NC_CACHE = build_nc()
    nc = _NC_CACHE

    in_maps = []
    for c in range(8):
        lo = c * T - 512
        if lo < 0:
            xs = np.concatenate([np.zeros((512, D), np.float32), x[0:c * T + T]],
                                axis=0)
        else:
            xs = x[lo:c * T + T]
        hb_a = np.zeros(NKB_A, np.float32)
        hb_b = np.zeros(NKB_B, np.float32)
        if c == 0:
            hb_a[:HALO_A] = HBIAS
            hb_b[:8] = HBIAS
        posb = (np.arange(NKB_A, dtype=np.float32) * 128.0) + float(c * T - 512)
        if c == 0:
            posb[:HALO_A] = 0.0
        in_maps.append({
            "x_in": np.ascontiguousarray(xs),
            "wqkv_a": wqkv_a_h, "wo_a": wo_a_h,
            "wqkv_b": wqkv_b_h, "wo_b": wo_b_h,
            "tbases": tb, "tslopes": ts_, "lgamma": lg, "lbeta": lb, "oscale": osc,
            "hb_a_in": hb_a, "hb_b_in": hb_b, "posb_in": posb,
        })
    res = run_bass_kernel_spmd(nc, in_maps, core_ids=list(range(8)))
    out = np.concatenate([res.results[c]["x_out"] for c in range(8)], axis=0)
    return out.reshape(1, 8192, D).astype(np.float32)


# version 27
# speedup vs baseline: 1.0001x; 1.0001x over previous
import numpy as np
from contextlib import ExitStack

import concourse.bass as bass
import concourse.mybir as mybir
from concourse.bass_utils import run_bass_kernel_spmd

F32 = mybir.dt.float32
BF16 = mybir.dt.bfloat16
AF = mybir.ActivationFunctionType
ALU = mybir.AluOpType
AX = mybir.AxisListType
AP = bass.AP

T = 1024
D = 2048
NQB = 8
HALO_A = 4
NKB_A = 12
NKB_B = 16
NKV = 4
SCALE = float(1.0 / np.sqrt(128.0))
WIN_A = 512
WIN_B = 1025
MAGIC = 8388608.0 + 4194304.0
BIGV = 1.0e4
HBIAS = -30000.0
TWO_PI = float(2 * np.pi)
TWO_PI_HI = 6.28125
TWO_PI_LO = float(2 * np.pi - 6.28125)
LN10K = float(np.log(10000.0))
RG = [[0, 1, 2, 3, 4, 5, 6, 7]]

N_SUBLAYERS = 4
DEBUG_PROJ = False
DEBUG_FFN = False


class Led:
    def __init__(self, handle):
        self.h = handle
        self.n = 0


class Plan:
    ENGS = ("pe", "dve", "act", "pool", "sync")

    def __init__(self):
        self.ops = {e: [] for e in self.ENGS}

    def op(self, eng, emit, waits=(), inc=None, k=1):
        if inc is not None:
            inc.n += k
        self.ops[eng].append((list(waits), emit, inc, k))
        return inc.n if inc is not None else None

    def _replay(self, eng, e):
        for waits, emit, inc, k in self.ops[eng]:
            for led, val in waits:
                e.wait_ge(led.h, val)
            ins = emit(e)
            if inc is not None:
                assert ins is not None, f"op on {eng} returned None but has inc"
                ins.then_inc(inc.h, k)

    def run(self, nc):
        with nc.Block() as blk:
            for eng, meth in (("pe", blk.tensor), ("dve", blk.vector),
                              ("act", blk.scalar), ("pool", blk.gpsimd),
                              ("sync", blk.sync)):
                if self.ops[eng]:
                    meth(lambda e, eng=eng: self._replay(eng, e))


def qk_block_class(c, W):
    if c < 0 or c > W + 126:
        return None
    if c == 0:
        return "causal"
    if -126 <= c - (W - 1) <= 127:
        return "upper"
    return "full"


def runs_for(is_a, grp, kb):
    W = WIN_A if is_a else WIN_B
    qoff = 512 if is_a else 1024
    runs = []
    cur = None
    for qs in (0, 1):
        c = (qoff + grp * 256 + qs * 128) - kb * 128
        cls = qk_block_class(c, W)
        if cls is None:
            if cur:
                runs.append(tuple(cur))
                cur = None
            continue
        m = None if cls == "full" else cls
        if cur is None:
            cur = [qs * 128, 128, [(0, m)] if m else []]
        else:
            if m:
                cur[2].append((cur[1], m))
            cur[1] += 128
    if cur:
        runs.append(tuple(cur))
    return runs


def build_nc():
    nc = bass.Bass("TRN2", target_bir_lowering=False)

    x_in = nc.dram_tensor("x_in", [T + 512, D], F32, kind="ExternalInput").ap()
    wqkv_a = nc.dram_tensor("wqkv_a", [D, 3072], F32, kind="ExternalInput").ap()
    wo_a = nc.dram_tensor("wo_a", [D, D], F32, kind="ExternalInput").ap()
    wqkv_b = nc.dram_tensor("wqkv_b", [D, 3072], F32, kind="ExternalInput").ap()
    wo_b = nc.dram_tensor("wo_b", [D, D], F32, kind="ExternalInput").ap()
    tbases = nc.dram_tensor("tbases", [64, 32, 16], F32, kind="ExternalInput").ap()
    tslopes = nc.dram_tensor("tslopes", [64, 32, 16], F32, kind="ExternalInput").ap()
    lgamma = nc.dram_tensor("lgamma", [D], F32, kind="ExternalInput").ap()
    lbeta = nc.dram_tensor("lbeta", [D], F32, kind="ExternalInput").ap()
    oscale = nc.dram_tensor("oscale", [64], F32, kind="ExternalInput").ap()
    hb_a_in = nc.dram_tensor("hb_a_in", [NKB_A], F32, kind="ExternalInput").ap()
    hb_b_in = nc.dram_tensor("hb_b_in", [NKB_B], F32, kind="ExternalInput").ap()
    posb_in = nc.dram_tensor("posb_in", [NKB_A], F32, kind="ExternalInput").ap()
    x_out = nc.dram_tensor("x_out", [T, D], F32, kind="ExternalOutput").ap()

    if DEBUG_FFN:
        ffsc_dbg = nc.dram_tensor("ffsc_dbg", [128, 816], F32, kind="ExternalOutput").ap()
        tmin_dbg = nc.dram_tensor("tmin_dbg", [64, 18], F32, kind="ExternalOutput").ap()
        agr_dbg = nc.dram_tensor("agr_dbg", [64, 4], F32, kind="ExternalOutput").ap()
        bt_dbg = nc.dram_tensor("bt_dbg", [128, 4096], BF16, kind="ExternalOutput").ap()
        st_dbg = nc.dram_tensor("st_dbg", [128, 4096], BF16, kind="ExternalOutput").ap()
    if DEBUG_PROJ:
        kT_dbg = nc.dram_tensor("kT_dbg", [128, 8192], BF16, kind="ExternalOutput").ap()
        v_dbg = nc.dram_tensor("v_dbg", [128, 8192], BF16, kind="ExternalOutput").ap()
        q_dbg = nc.dram_tensor("q_dbg", [128, 16384], BF16, kind="ExternalOutput").ap()
    rows_d = nc.dram_tensor("rows_d", [2, D], F32, kind="Internal").ap()
    rope_d = nc.dram_tensor("rope_d", [5, 128, 1024], F32, kind="Internal").ap()
    ag_in = nc.dram_tensor("ag_in", [128, 8192], BF16, kind="Internal").ap()
    ag_out = nc.dram_tensor("ag_out", [1024, 8192], BF16, kind="Internal").ap()
    ar_in = nc.dram_tensor("ar_in", [64, 2], F32, kind="Internal").ap()
    ar_out = nc.dram_tensor("ar_out", [64, 2], F32, kind="Internal").ap()

    st = ExitStack()

    def sb(name, shape, dt=F32):
        return st.enter_context(nc.sbuf_tensor(name, shape, dt))

    def sem(name):
        return Led(st.enter_context(nc.semaphore(name)))

    x_sb = sb("x_sb", [128, NQB * D])
    big16 = sb("big16", [128, 16384], BF16)
    kT = sb("kT", [128, 8192], BF16)
    v_sb = sb("v_sb", [128, 8192], BF16)
    arena = sb("arena", [128, 24576], BF16)
    y16 = sb("y16", [128, D], BF16)
    qk16 = sb("qk16", [128, 1024], BF16)
    rsc = sb("rsc", [128, 1792])
    cosg = sb("cosg", [128, 512])
    sing = sb("sing", [128, 512])
    tableT = sb("tableT", [64, 68])
    table16 = sb("table16", [64, 1024], BF16)
    ffsc = sb("ffsc", [128, 816])
    stats = sb("stats", [128, 156])
    small = sb("small", [128, 256])
    consts = sb("consts", [128, 8])
    iota64 = sb("iota64", [128, 64])
    iota64m = sb("iota64m", [128, 64])
    iota8m = sb("iota8m", [128, 8])
    ca64 = sb("ca64", [128, 64])
    mcaus = sb("mcaus", [128, 128], BF16)
    mstrU = sb("mstrU", [128, 128], BF16)
    minclU = sb("minclU", [128, 128], BF16)
    ident16 = sb("ident16", [128, 128], BF16)
    identf = sb("identf", [128, 128])
    ones_bf = sb("ones_bf", [128, 128], BF16)
    ones1 = sb("ones1", [1, 128])
    hbB = sb("hbB", [128, NKB_A + NKB_B])
    pidx = sb("pidx", [128, 1])
    invfA = sb("invfA", [128, 128])
    invfB = sb("invfB", [128, 128])
    sgnA = sb("sgnA", [128, 128])
    sgnB = sb("sgnB", [128, 128])
    tmin_s = sb("tmin_s", [64, 18])
    agr = sb("agr", [64, 4])
    ohq = sb("ohq", [128, 64])
    ohq16 = sb("ohq16", [128, 64], BF16)
    ohq_u8 = sb("ohq_u8", [128, 64], mybir.dt.uint8)
    ohall_u8 = sb("ohall_u8", [128, 512], mybir.dt.uint8)
    ohT16 = sb("ohT16", [64, 128], BF16)
    ohTf = sb("ohTf", [64, 128])
    gsel = sb("gsel", [128, 66])

    FC_TMIN, FC_TMAX, FC_TIDX, FC_OSEL = 0, 8, 16, 24
    FC_GI, FC_XL, FC_XS = 32, 288, 544

    s_dma = sem("s_dma")
    s_wdma = sem("s_wdma")
    s_wdma2 = sem("s_wdma2")
    s_hdma = sem("s_hdma")
    s_pe = sem("s_pe")
    s_dve = sem("s_dve")
    s_act = sem("s_act")
    s_pool = sem("s_pool")
    s_cc = sem("s_cc")
    ALL_SEMS = [s_dma, s_wdma, s_wdma2, s_hdma, s_pe, s_dve, s_act, s_pool, s_cc]

    with nc.Block() as b0:
        @b0.gpsimd
        def _(g):
            for s in ALL_SEMS:
                g.sem_clear(s.h)
    nc.all_engine_barrier()

    def a16(off, n):
        return arena[:, off:off + n]

    def a32(off_bf16, n_f32):
        return arena[:, off_bf16:off_bf16 + 2 * n_f32].bitcast(F32)

    P = Plan()

    def em(pl, eng, fn, waits=(), inc=None, k=1):
        return pl.op(eng, fn, waits=waits, inc=inc, k=k)

    def rope_tables_for_group(p, is_a, group, base_waits):
        invf = invfA if is_a else invfB
        sgn = sgnA if is_a else sgnB
        ng = len(group)
        W = 128 * ng
        posB = small[:, 224:225]
        ang = rsc[:, 0:512]
        ev_cos = ev_sin = None
        for si, u in enumerate(group):
            pb_col = 144 + (u if is_a else 4 + u)
            em(p, "dve", lambda v, pb_col=pb_col: v.tensor_single_scalar(
                posB, pidx[:], stats[:, pb_col:pb_col + 1], ALU.add),
                waits=(list(base_waits) if si == 0 else []))
            em(p, "dve", lambda v: v.drain())
            em(p, "dve", lambda v, si=si: v.tensor_single_scalar(
                ang[:, si * 128:(si + 1) * 128], invf[:], posB, ALU.mult))
            em(p, "dve", lambda v: v.drain())
        for trig in ("sin", "cos"):
            aa = rsc[:, 512:512 + W] if trig == "sin" else rsc[:, 1280:1280 + W]
            kk = rsc[:, 1280:1280 + W] if trig == "sin" else rsc[:, 512:512 + W]
            if trig == "sin":
                em(p, "dve", lambda v, aa=aa: v.tensor_copy(aa, ang[:, 0:W]))
            else:
                em(p, "dve", lambda v, aa=aa: v.tensor_single_scalar(
                    aa, ang[:, 0:W], float(0.5 * np.pi), ALU.add))
            em(p, "dve", lambda v: v.drain())
            em(p, "dve", lambda v, aa=aa, kk=kk: v.tensor_single_scalar(
                kk, aa, float(1.0 / TWO_PI), ALU.mult))
            em(p, "dve", lambda v: v.drain())
            em(p, "dve", lambda v, kk=kk: v.tensor_single_scalar(
                kk, kk, consts[:, 0:1], ALU.add))
            em(p, "dve", lambda v: v.drain())
            em(p, "dve", lambda v, kk=kk: v.tensor_single_scalar(
                kk, kk, MAGIC, ALU.subtract))
            em(p, "dve", lambda v: v.drain())
            em(p, "dve", lambda v, aa=aa, kk=kk: v.scalar_tensor_tensor(
                aa, kk, consts[:, 3:4], aa, ALU.mult, ALU.add))
            em(p, "dve", lambda v: v.drain())
            ev = em(p, "dve", lambda v, aa=aa, kk=kk: v.scalar_tensor_tensor(
                aa, kk, consts[:, 4:5], aa, ALU.mult, ALU.add), inc=s_dve)
            if trig == "sin":
                eva = em(p, "act", lambda s, aa=aa: s.activation(
                    rsc[:, 1280:1280 + W], aa, AF.Sin), waits=[(s_dve, ev)],
                    inc=s_act)
                sgb = AP(sgn, 0, [[128, 128], [0, ng], [1, 128]])
                ev_sin = em(p, "dve", lambda v, W=W, sgb=sgb: v.tensor_tensor(
                    sing[:, 0:W].rearrange("p (s w) -> p s w", w=128),
                    rsc[:, 1280:1280 + W].rearrange("p (s w) -> p s w", w=128),
                    sgb, ALU.mult), waits=[(s_act, eva)], inc=s_dve)
            else:
                ev_cos = em(p, "act", lambda s, aa=aa, W=W: s.activation(
                    cosg[:, 0:W], aa, AF.Sin), waits=[(s_dve, ev)], inc=s_act)
        return ev_cos, ev_sin

    def DMA(pl, dst, src, led=s_dma, eng="pool", waits=()):
        return pl.op(eng, lambda e, d=dst, s=src: e.dma_start(out=d, in_=s),
                     waits=waits, inc=led, k=16)

    DMA(P, AP(x_sb, 0, [[NQB * D, 128], [D, 4], [1, D]]),
        AP(x_in.tensor, 512 * D, [[D, 128], [128 * D, 4], [1, D]]), led=s_hdma)
    DMA(P, AP(x_sb, 4 * D, [[NQB * D, 128], [D, 4], [1, D]]),
        AP(x_in.tensor, (512 + 4 * 128) * D, [[D, 128], [128 * D, 4], [1, D]]),
        eng="sync")
    DMA(P, a32(4096, 512)[0:64, :], tslopes.rearrange("t j g -> t (j g)"),
        led=s_hdma)
    DMA(P, tableT[0:64, 0:32], lgamma.rearrange("(t j) -> t j", t=64), led=s_hdma)
    DMA(P, tableT[0:64, 32:64], lbeta.rearrange("(t j) -> t j", t=64), led=s_hdma)
    DMA(P, tableT[0:64, 64:65], oscale.rearrange("(t o) -> t o", o=1), led=s_hdma)
    DMA(P, table16[0:64, 0:512], tbases.rearrange("t j g -> t (j g)"), led=s_hdma)
    DMA(P, table16[0:64, 512:1024], tslopes.rearrange("t j g -> t (j g)"),
        led=s_hdma)
    DMA(P, small[0:1, 0:NKB_A], hb_a_in.rearrange("(o k) -> o k", o=1), led=s_hdma)
    DMA(P, small[0:1, 32:32 + NKB_B], hb_b_in.rearrange("(o k) -> o k", o=1),
        led=s_hdma)
    DMA(P, small[0:1, 64:64 + NKB_A], posb_in.rearrange("(o k) -> o k", o=1),
        led=s_hdma)
    n_setup_dma = s_dma.n
    n_setup_hdma = s_hdma.n

    qqB = a32(0, 128)
    kkB = a32(256, 128)
    jA = a32(512, 128)
    jB = a32(768, 128)
    h01 = a32(1024, 128)
    p01 = a32(1280, 128)
    em(P, "pool", lambda g: g.iota(small[:, 128:136], [[0, 8]], channel_multiplier=1,
                                   allow_small_or_imprecise_dtypes=True))
    em(P, "pool", lambda g: g.iota(iota64[:], [[1, 64]], channel_multiplier=0,
                                   allow_small_or_imprecise_dtypes=True))
    em(P, "pool", lambda g: g.iota(ca64[:], [[1, 8], [0, 8]], channel_multiplier=0,
                                   allow_small_or_imprecise_dtypes=True))
    em(P, "pool", lambda g: g.iota(qqB, [[1, 128]], channel_multiplier=0,
                                   allow_small_or_imprecise_dtypes=True))
    em(P, "pool", lambda g: g.iota(kkB, [[0, 128]], channel_multiplier=1,
                                   allow_small_or_imprecise_dtypes=True))
    em(P, "pool", lambda g: g.iota(jA, [[0, 2], [1, 64]], channel_multiplier=0,
                                   allow_small_or_imprecise_dtypes=True))
    em(P, "pool", lambda g: g.iota(jB, [[1, 64], [0, 2]], channel_multiplier=0,
                                   allow_small_or_imprecise_dtypes=True))
    em(P, "pool", lambda g: g.iota(h01, [[1, 2], [0, 64]], channel_multiplier=0,
                                   allow_small_or_imprecise_dtypes=True))
    em(P, "pool", lambda g: g.iota(p01, [[0, 64], [1, 2]], channel_multiplier=0,
                                   allow_small_or_imprecise_dtypes=True))
    em(P, "pool", lambda g: g.memset(ones1[:], 1.0))
    em(P, "pool", lambda g: g.memset(ones_bf[:], 1.0))
    for i, val in ((0, MAGIC), (1, 1e-5), (2, 1.0 / 2048.0), (3, -TWO_PI_HI),
                   (4, -TWO_PI_LO), (5, 1.0), (6, 1e-8), (7, BIGV)):
        em(P, "pool", lambda g, i=i, val=val: g.memset(consts[:, i:i + 1],
                                                       float(val)),
           inc=(s_pool if i == 7 else None))
    pl_set = s_pool.n

    w0 = [(s_dma, n_setup_dma), (s_hdma, n_setup_hdma),
          (s_pool, pl_set)]
    em(P, "dve", lambda v: v.tensor_copy(pidx[:], small[:, 128:129]), waits=w0)
    em(P, "dve", lambda v: v.drain())
    em(P, "dve", lambda v: v.tensor_tensor(ident16[:], qqB, kkB, ALU.is_equal))
    em(P, "dve", lambda v: v.tensor_tensor(identf[:], qqB, kkB, ALU.is_equal))
    em(P, "dve", lambda v: v.tensor_tensor(mcaus[:], qqB, kkB, ALU.is_ge))
    em(P, "dve", lambda v: v.tensor_tensor(minclU[:], qqB, kkB, ALU.is_le))
    em(P, "dve", lambda v: v.tensor_tensor(mstrU[:], qqB, kkB, ALU.is_lt))
    em(P, "dve", lambda v: v.tensor_single_scalar(iota64m[:], iota64[:], BIGV, ALU.subtract))
    em(P, "dve", lambda v: v.tensor_single_scalar(iota8m[:], iota64[:, 0:8], BIGV, ALU.subtract))
    em(P, "dve", lambda v: v.tensor_single_scalar(sgnA[:], h01, 2.0, ALU.mult))
    em(P, "dve", lambda v: v.tensor_single_scalar(sgnB[:], p01, 2.0, ALU.mult))
    em(P, "dve", lambda v: v.drain())
    em(P, "dve", lambda v: v.tensor_single_scalar(sgnA[:], sgnA[:], 1.0, ALU.subtract))
    em(P, "dve", lambda v: v.tensor_single_scalar(sgnB[:], sgnB[:], 1.0, ALU.subtract),
       inc=s_dve)
    dv_set1 = s_dve.n

    em(P, "act", lambda s: s.activation(invfA[:], jA, AF.Exp, scale=-LN10K / 64.0),
       waits=w0)
    em(P, "act", lambda s: s.activation(invfB[:], jB, AF.Exp, scale=-LN10K / 64.0),
       inc=s_act)
    ac_set1 = s_act.n

    with nc.psum_tensor("ps_b", [128, 512], F32) as ps_b:
        em(P, "pe", lambda t: t.matmul(ps_b[:, 0:NKB_A], ones1[:], small[0:1, 0:NKB_A],
                                       start=True, stop=True),
           waits=[(s_hdma, n_setup_hdma), (s_pool, pl_set)])
        em(P, "pe", lambda t: t.matmul(ps_b[:, 64:64 + NKB_B], ones1[:],
                                       small[0:1, 32:32 + NKB_B], start=True, stop=True))
        em(P, "pe", lambda t: t.matmul(ps_b[:, 128:128 + NKB_A], ones1[:],
                                       small[0:1, 64:64 + NKB_A], start=True, stop=True),
           inc=s_pe)
        pe_set1 = s_pe.n
        em(P, "dve", lambda v: v.tensor_copy(hbB[:, 0:NKB_A], ps_b[:, 0:NKB_A]),
           waits=[(s_pe, pe_set1)])
        em(P, "dve", lambda v: v.tensor_copy(hbB[:, NKB_A:NKB_A + NKB_B],
                                             ps_b[:, 64:64 + NKB_B]))
        em(P, "dve", lambda v: v.tensor_copy(stats[:, 144:144 + NKB_A],
                                             ps_b[:, 128:128 + NKB_A]), inc=s_dve)
        dv_set2 = s_dve.n

        em(P, "dve", lambda v: v.tensor_reduce(
            small[0:64, 224:256],
            a32(4096, 512)[0:64, :].rearrange("p (j g) -> p j g", j=32),
            AX.X, ALU.add), waits=[(s_hdma, n_setup_hdma)], inc=s_dve)
        ev_ss = s_dve.n
        em(P, "act", lambda s: s.activation(small[0:64, 224:256], small[0:64, 224:256],
                                            AF.Sign), waits=[(s_dve, ev_ss)], inc=s_act)
        ev_sg = s_act.n
        em(P, "dve", lambda v: v.tensor_tensor(small[0:64, 0:32], tableT[0:64, 0:32],
                                               small[0:64, 224:256], ALU.mult),
           waits=[(s_act, ev_sg)])
        em(P, "dve", lambda v: v.tensor_tensor(small[0:64, 32:64], tableT[0:64, 32:64],
                                               small[0:64, 224:256], ALU.mult),
           inc=s_dve)
        ev_rows = s_dve.n
        em(P, "sync", lambda sy: sy.dma_start(
            out=rows_d[0:1, :].rearrange("r (t j) -> (r t) j", t=64),
            in_=small[0:64, 0:32]), waits=[(s_dve, ev_rows)], inc=s_dma, k=16)
        em(P, "sync", lambda sy: sy.dma_start(
            out=rows_d[1:2, :].rearrange("r (t j) -> (r t) j", t=64),
            in_=small[0:64, 32:64]), inc=s_dma, k=16)
        s1 = big16[:, 0:3072].bitcast(F32)
        s2 = big16[:, 3072:6144].bitcast(F32)
        s3 = big16[:, 6144:9216].bitcast(F32)
        s4 = big16[:, 9216:12288].bitcast(F32)
        prev_stash = 0
        for nsets, nunits, invf, sgn, pb0, base in (
                (3, 12, invfA, sgnA, 144, 0), (2, 8, invfB, sgnB, 148, 3)):
            W = nsets * 512
            ang = rsc[:, 0:W]
            a3 = ang.rearrange("p (u w) -> p u w", w=128)
            pidxb = AP(pidx, 0, [[1, 128], [0, nunits], [0, 128]])
            posbb = AP(stats, pb0, [[156, 128], [1, nunits], [0, 128]])
            invfb_ = AP(invf, 0, [[128, 128], [0, nunits], [1, 128]])
            sgnb_ = AP(sgn, 0, [[128, 128], [0, nunits], [1, 128]])
            bw0 = [(s_dve, dv_set2), (s_act, ac_set1)]
            if prev_stash:
                bw0.append((s_dma, prev_stash))
            em(P, "dve", lambda v, a3=a3, pidxb=pidxb, posbb=posbb:
               v.tensor_tensor(a3, pidxb, posbb, ALU.add), waits=bw0)
            em(P, "dve", lambda v: v.drain())
            em(P, "dve", lambda v, a3=a3, invfb_=invfb_: v.tensor_tensor(
                a3, a3, invfb_, ALU.mult))
            em(P, "dve", lambda v: v.drain())
            evt = {}
            for trig, aa, kk in (("sin", s1, s2), ("cos", s3, s4)):
                av = aa[:, 0:W]
                kv = kk[:, 0:W]
                if trig == "sin":
                    em(P, "dve", lambda v, av=av, W=W, ang=ang: v.tensor_copy(
                        av, ang[:, 0:W]))
                else:
                    em(P, "dve", lambda v, av=av, W=W, ang=ang:
                       v.tensor_single_scalar(
                        av, ang[:, 0:W], float(0.5 * np.pi), ALU.add))
                em(P, "dve", lambda v: v.drain())
                em(P, "dve", lambda v, av=av, kv=kv: v.tensor_single_scalar(
                    kv, av, float(1.0 / TWO_PI), ALU.mult))
                em(P, "dve", lambda v: v.drain())
                em(P, "dve", lambda v, kv=kv: v.tensor_scalar(
                    kv, kv, consts[:, 0:1], MAGIC, ALU.add, ALU.subtract))
                em(P, "dve", lambda v: v.drain())
                em(P, "dve", lambda v, av=av, kv=kv: v.scalar_tensor_tensor(
                    av, kv, consts[:, 3:4], av, ALU.mult, ALU.add))
                em(P, "dve", lambda v: v.drain())
                ev = em(P, "dve", lambda v, av=av, kv=kv: v.scalar_tensor_tensor(
                    av, kv, consts[:, 4:5], av, ALU.mult, ALU.add), inc=s_dve)
                if trig == "sin":
                    eva = em(P, "act", lambda s, kv=kv, av=av: s.activation(
                        kv, av, AF.Sin), waits=[(s_dve, ev)], inc=s_act)
                    evt["sin"] = em(P, "dve", lambda v, av=av, kv=kv, W=W,
                                    sgnb_=sgnb_: v.tensor_tensor(
                        av.rearrange("p (u w) -> p u w", w=128),
                        kv.rearrange("p (u w) -> p u w", w=128),
                        sgnb_, ALU.mult), waits=[(s_act, eva)], inc=s_dve)
                else:
                    evt["cos"] = em(P, "act", lambda s, kv=kv, av=av: s.activation(
                        kv, av, AF.Sin), waits=[(s_dve, ev)], inc=s_act)
            em(P, "sync", lambda sy, base=base, nsets=nsets, W=W: sy.dma_start(
                out=AP(rope_d.tensor, base * 131072,
                       [[1024, 128], [131072, nsets], [1, 512]]),
                in_=s4[:, 0:W].rearrange("p (s c) -> p s c", c=512)),
                waits=[(s_act, evt["cos"])], inc=s_dma, k=16)
            prev_stash = em(P, "sync", lambda sy, base=base, nsets=nsets, W=W:
                            sy.dma_start(
                out=AP(rope_d.tensor, base * 131072 + 512,
                       [[1024, 128], [131072, nsets], [1, 512]]),
                in_=s1[:, 0:W].rearrange("p (s c) -> p s c", c=512)),
                waits=[(s_dve, evt["sin"])], inc=s_dma, k=16)
        P.run(nc)


    def scalar_chain_ln(p, u, xs_ap, want_C, extra_waits=()):
        sums = stats[:, u:u + 1]
        sumsq = stats[:, 32 + u:33 + u]
        negmu = stats[:, 64 + u:65 + u]
        rstd = stats[:, 96 + u:97 + u]
        Ccol = stats[:, 128 + u:129 + u]
        mu = small[:, 240:241]
        var = small[:, 241:242]
        t0 = small[:, 242:243]
        v2 = small[:, 243:244]
        em(p, "act", lambda s: s.activation(y16[:], xs_ap, AF.Square,
                                            accum_out=sumsq),
           waits=list(extra_waits))
        eva = em(p, "act", lambda s: s.activation(y16[:], xs_ap, AF.Identity,
                                                  accum_out=sums), inc=s_act)
        em(p, "dve", lambda v: v.tensor_single_scalar(mu, sums, consts[:, 2:3], ALU.mult),
           waits=[(s_act, eva)])
        em(p, "dve", lambda v: v.drain())
        em(p, "dve", lambda v: v.tensor_single_scalar(negmu, mu, -1.0, ALU.mult))
        em(p, "dve", lambda v: v.tensor_single_scalar(t0, mu, mu, ALU.mult),
           waits=[(s_act, eva)])
        em(p, "dve", lambda v: v.drain())
        em(p, "dve", lambda v: v.scalar_tensor_tensor(var, sumsq, consts[:, 2:3], t0,
                                                      ALU.mult, ALU.subtract))
        ev1 = em(p, "dve", lambda v: v.drain(), inc=s_dve)
        em(p, "act", lambda s: s.activation(t0, var, AF.Ln, bias=consts[:, 1:2]),
           waits=[(s_dve, ev1)])
        ev2 = em(p, "act", lambda s: s.activation(rstd, t0, AF.Exp, scale=-0.5),
                 inc=s_act)
        if not want_C:
            return ev2
        em(p, "act", lambda s: s.activation(v2, t0, AF.Exp, scale=-1.0))
        ev3 = em(p, "act", lambda s: s.nop(), inc=s_act)
        em(p, "dve", lambda v: v.tensor_single_scalar(v2, var, v2, ALU.mult),
           waits=[(s_act, ev3)])
        ev4 = em(p, "dve", lambda v: v.drain(), inc=s_dve)
        em(p, "act", lambda s: s.activation(v2, v2, AF.Ln, bias=consts[:, 1:2]),
           waits=[(s_dve, ev4)])
        ev5 = em(p, "act", lambda s: s.activation(v2, v2, AF.Exp, scale=-0.5),
                 inc=s_act)
        em(p, "dve", lambda v: v.tensor_single_scalar(Ccol, rstd, v2, ALU.mult),
           waits=[(s_act, ev5)])
        ev6 = em(p, "dve", lambda v: v.drain(), inc=s_dve)
        return ev2, ev6


    def attn_layer(variant):
        is_a = variant == "a"
        wqkv = wqkv_a if is_a else wqkv_b
        wo = wo_a if is_a else wo_b
        nkb = NKB_A if is_a else NKB_B
        hb_off = 0 if is_a else NKB_A
        units = list(range(NKB_A)) if is_a else list(range(NQB))
        groups = [units[i:i + 4] for i in range(0, len(units), 4)]

        CW = 512

        def uinfo(u):
            if is_a:
                return (u < HALO_A), (None if u < HALO_A else u - HALO_A), u
            return False, u, 8 + u

        def slots_of(c):
            out = []
            for i in range(CW // 128):
                col = CW * c + 128 * i
                if col < 2048:
                    out.append(("q", col // 128, i))
                elif col < 2560:
                    out.append(("k", (col - 2048) // 128, i))
                else:
                    out.append(("v", (col - 2560) // 128, i))
            return out

        p = Plan()
        with ExitStack() as pst:
            ps_tr = [pst.enter_context(nc.psum_tensor(f"pt{variant}{i}", [128, 2048],
                                                      BF16)) for i in range(2)]
            ps_qkv = [pst.enter_context(nc.psum_tensor(f"pq{variant}{i}", [128, 512],
                                                       F32)) for i in range(2)]
            wbuf = [arena[:, 0:8192], arena[:, 8192:16384]]
            yTg_off = 16384

            tr_evac = [0, 0]
            qkv_done = [0, 0]
            chunk_mm_done = {}
            citer = 0
            wchunk = 0
            pending_tr = []
            trctr = [0]
            tr_read = {0: None, 1: None}
            prev_tr_pe = 0
            yT_ev = {}
            ag_pending = [None]
            first_ln = [True]
            if is_a:
                passes = [("all", 0), ("all", 1), ("all", 2)]
            else:
                passes = [("kv", 0), ("kv", 1), ("AG", None), ("qq", None)]

            def load_ropetab(rset):
                wpre = [(s_dve, s_dve.n), (s_act, s_act.n)]
                em(p, "sync", lambda sy, rset=rset: sy.dma_start(
                    out=cosg[:, 0:512],
                    in_=AP(rope_d.tensor, rset * 128 * 1024,
                           [[1024, 128], [1, 512]])),
                   waits=wpre, inc=s_dma, k=16)
                ev_rld = em(p, "sync", lambda sy, rset=rset: sy.dma_start(
                    out=sing[:, 0:512],
                    in_=AP(rope_d.tensor, rset * 128 * 1024 + 512,
                           [[1024, 128], [1, 512]])), inc=s_dma, k=16)
                return ((s_dma, ev_rld), (s_dma, ev_rld))

            for ph, gi in passes:
                if ph == "AG":
                    for fn in pending_tr:
                        fn()
                    pending_tr = []
                    wpk = [(s_act, s_act.n), (s_dve, s_dve.n), (s_pe, s_pe.n)]
                    DMA(p, AP(ag_in.tensor, 0,
                              [[8192, 128], [1024, 4], [1, 1024]]),
                        AP(kT, 1024, [[8192, 128], [2048, 4], [1, 1024]]),
                        eng="sync", waits=wpk)
                    ev_pack = DMA(p, AP(ag_in.tensor, 4096,
                                        [[8192, 128], [1024, 4], [1, 1024]]),
                                  AP(v_sb, 1024, [[8192, 128], [2048, 4],
                                                  [1, 1024]]), eng="sync")
                    ag_pending[0] = ev_pack
                    continue
                if ph == "qq":
                    group = []
                    units_iter = [(gi2, si, si, u) for gi2 in (0, 1)
                                  for si, u in enumerate(groups[gi2])]
                    cos_ev = {}
                else:
                    group = groups[gi]
                    units_iter = [(gi, si, si, u) for si, u in enumerate(group)]
                    gtab = load_ropetab(gi if is_a else 3 + gi)
                    cos_ev = {u: gtab for u in group}
                for si, u in (enumerate(group) if ph != "qq" else []):
                    halo, qb, kb = uinfo(u)
                    xw = []
                    if first_ln[0]:
                        xw += [(s_dve, s_dve.n), (s_pool, s_pool.n)]
                        first_ln[0] = False
                    if halo:
                        hw = [(s_dve, s_dve.n), (s_act, s_act.n)]
                        hx_ap = a32(0, 2048)
                        ev_hx = DMA(p, hx_ap, AP(x_in.tensor, u * 128 * D,
                                                 [[D, 128], [1, D]]),
                                    led=s_hdma, waits=hw)
                        xs_ap = hx_ap
                        xw = [(s_hdma, ev_hx)]
                    else:
                        xs_ap = x_sb[:, qb * D:(qb + 1) * D]
                    ev_rstd = scalar_chain_ln(p, u % 32, xs_ap, False, extra_waits=xw)
                    wv_y = [(s_act, ev_rstd)]
                    if prev_tr_pe:
                        wv_y.append((s_pe, prev_tr_pe))
                    evy = em(p, "dve", lambda v, xs_ap=xs_ap, u=u: v.tensor_scalar(
                        y16[:], xs_ap, stats[:, 64 + (u % 32):65 + (u % 32)],
                        stats[:, 96 + (u % 32):97 + (u % 32)], ALU.add, ALU.mult),
                        waits=wv_y, inc=s_dve)
                    par = si % 2
                    wv = [(s_dve, evy)]
                    if tr_evac[par]:
                        wv.append((s_act, tr_evac[par]))
                    ev_tr = None
                    for i in range(16):
                        ev_tr = em(p, "pe", lambda t, i=i, par=par: t.transpose(
                            ps_tr[par][:, i * 128:(i + 1) * 128],
                            y16[:, i * 128:(i + 1) * 128], ident16[:]),
                            waits=(wv if i == 0 else []),
                            inc=(s_pe if i == 15 else None))
                    if (not is_a) and gi == 0:
                        d1 = kT[:, si * 2048:si * 2048 + 1024]
                        d2 = v_sb[:, si * 2048:si * 2048 + 1024]
                        em(p, "act", lambda s, d1=d1, par=par: s.activation(
                            d1, ps_tr[par][:, 0:1024], AF.Identity),
                            waits=[(s_pe, ev_tr)])
                        tr_evac[par] = em(p, "act", lambda s, d2=d2, par=par:
                                          s.activation(d2, ps_tr[par][:, 1024:2048],
                                                       AF.Identity), inc=s_act)
                    else:
                        dst_yT = a16(yTg_off + si * 2048, 2048)
                        tr_evac[par] = em(p, "act", lambda s, dst_yT=dst_yT, par=par:
                                          s.activation(dst_yT, ps_tr[par][:],
                                                       AF.Identity),
                                          waits=[(s_pe, ev_tr)], inc=s_act)
                    yT_ev[u] = tr_evac[par]
                    prev_tr_pe = ev_tr

                first_chunk = True
                if ph == "all":
                    clist = [4, 5] if group[0] < HALO_A else [0, 1, 2, 3, 4, 5]
                else:
                    clist = {"kv": [4, 5], "qq": [0, 1, 2, 3]}[ph]
                cur_g = [None]
                for c in clist:
                    par = wchunk % 2
                    wvw = ([(s_pe, chunk_mm_done[wchunk - 2])]
                           if wchunk - 2 in chunk_mm_done else [])
                    if first_chunk:
                        wvw += [(s_dve, s_dve.n), (s_act, s_act.n)]
                        first_chunk = False
                    wled = s_wdma
                    weng = "pool"
                    ev_w = DMA(p, AP(arena, par * (16 * CW),
                                     [[24576, 128], [CW, 16], [1, CW]]),
                               AP(wqkv.tensor, CW * c,
                                  [[3072, 128], [128 * 3072, 16], [1, CW]]),
                               led=wled, eng=weng, waits=wvw)
                    if ag_pending[0] is not None:
                        em(p, "pool", lambda g: g.collective_compute(
                            "AllGather", ALU.bypass, RG, [ag_in], [ag_out]),
                            waits=[(s_dma, ag_pending[0])], inc=s_cc, k=1)
                        ag_pending[0] = None
                    for gi2, jslot, si, u in units_iter:
                        if ph == "qq" and cur_g[0] != gi2:
                            gtab = load_ropetab(gi2 if is_a else 3 + gi2)
                            for uu in groups[gi2]:
                                cos_ev[uu] = gtab
                            cur_g[0] = gi2
                        halo, qb, kb = uinfo(u)
                        qp = citer % 2
                        wv = [(wled, ev_w), (s_act, yT_ev[u])]
                        if qkv_done[qp]:
                            wv.append((s_dve, qkv_done[qp]))
                        ev_mm = None
                        for dc in range(16):
                            if (not is_a) and gi2 == 0:
                                if dc < 8:
                                    lhs = kT[:, jslot * 2048 + dc * 128:
                                             jslot * 2048 + dc * 128 + 128]
                                else:
                                    lhs = v_sb[:, jslot * 2048 + (dc - 8) * 128:
                                               jslot * 2048 + (dc - 8) * 128 + 128]
                            else:
                                lhs = a16(yTg_off + jslot * 2048 + dc * 128, 128)
                            rhs = wbuf[par][:, dc * CW:(dc + 1) * CW]
                            ev_mm = em(p, "pe", lambda t, lhs=lhs, rhs=rhs, dc=dc,
                                       qp=qp: t.matmul(ps_qkv[qp][:, 0:CW], lhs, rhs,
                                                       start=(dc == 0),
                                                       stop=(dc == 15)),
                                       waits=(wv if dc == 0 else []),
                                       inc=(s_pe if dc == 15 else None))
                        chunk_mm_done[wchunk] = ev_mm
                        if len(pending_tr) > 1:
                            pending_tr.pop(0)()
                        sl = slots_of(c)
                        qk_slots = [s0 for s0 in sl if s0[0] in ("q", "k")]
                        v_slots = [s0 for s0 in sl if s0[0] == "v"]
                        last_consume = None
                        if qk_slots:
                            ns = len(qk_slots)
                            lo = qk_slots[0][2] * 128
                            if is_a:
                                off2, stp = 64, 1
                            else:
                                off2, stp = 1, 2
                            P1 = AP(ps_qkv[qp], lo, [[512, 128], [128, ns], [stp, 64]])
                            P2 = AP(ps_qkv[qp], lo + off2,
                                    [[512, 128], [128, ns], [stp, 64]])
                            C1 = AP(cosg, si * 128, [[512, 128], [0, ns], [stp, 64]])
                            C2 = AP(cosg, si * 128 + off2,
                                    [[512, 128], [0, ns], [stp, 64]])
                            S1 = AP(sing, si * 128, [[512, 128], [0, ns], [stp, 64]])
                            S2 = AP(sing, si * 128 + off2,
                                    [[512, 128], [0, ns], [stp, 64]])
                            m1 = AP(rsc, 768, [[1792, 128], [64, ns], [1, 64]])
                            m2 = AP(rsc, 1024, [[1792, 128], [64, ns], [1, 64]])
                            kp = citer % 2
                            if is_a:
                                o1 = AP(rsc, 1280, [[1792, 128], [128, ns], [stp, 64]])
                                o2 = AP(rsc, 1280 + off2,
                                        [[1792, 128], [128, ns], [stp, 64]])
                            else:
                                o1 = AP(qk16, kp * 512, [[1024, 128], [128, ns],
                                                         [stp, 64]])
                                o2 = AP(qk16, kp * 512 + off2,
                                        [[1024, 128], [128, ns], [stp, 64]])
                            qkr = rsc[:, 1280:1280 + 128 * ns]
                            wvr = [(s_pe, ev_mm), cos_ev[u][0], cos_ev[u][1]]
                            if tr_read[kp]:
                                wvr.append((s_pe, tr_read[kp]))
                            em(p, "dve", lambda v, P1=P1, C1=C1, m1=m1: v.tensor_tensor(
                                m1, P1, C1, ALU.mult), waits=wvr)
                            em(p, "dve", lambda v, P2=P2, S1=S1, m2=m2: v.tensor_tensor(
                                m2, P2, S1, ALU.mult))
                            em(p, "dve", lambda v: v.drain())
                            em(p, "dve", lambda v, o1=o1, m1=m1, m2=m2: v.tensor_tensor(
                                o1, m1, m2, ALU.add))
                            em(p, "dve", lambda v, P2=P2, C2=C2, m1=m1: v.tensor_tensor(
                                m1, P2, C2, ALU.mult))
                            em(p, "dve", lambda v, P1=P1, S2=S2, m2=m2: v.tensor_tensor(
                                m2, P1, S2, ALU.mult))
                            em(p, "dve", lambda v: v.drain())
                            ev_rope = em(p, "dve", lambda v, o2=o2, m1=m1, m2=m2:
                                         v.tensor_tensor(o2, m1, m2, ALU.add),
                                         inc=s_dve)
                            if is_a:
                                ev_sq = em(p, "act", lambda s, qkr=qkr, ns=ns:
                                           s.activation(rsc[:, 768:768 + 128 * ns], qkr,
                                                        AF.Square),
                                           waits=[(s_dve, ev_rope)], inc=s_act)
                                em(p, "dve", lambda v, ns=ns: v.tensor_reduce(
                                    small[:, 244:244 + ns],
                                    AP(rsc, 768, [[1792, 128], [128, ns], [1, 128]]),
                                    AX.X, ALU.add), waits=[(s_act, ev_sq)])
                                em(p, "dve", lambda v: v.drain())
                                em(p, "dve", lambda v, ns=ns: v.tensor_single_scalar(
                                    small[:, 244:244 + ns], small[:, 244:244 + ns],
                                    1e-24, ALU.max))
                                ev_ns = em(p, "dve", lambda v: v.drain(), inc=s_dve)
                                em(p, "act", lambda s, ns=ns: s.activation(
                                    small[:, 248:248 + ns], small[:, 244:244 + ns],
                                    AF.Ln), waits=[(s_dve, ev_ns)])
                                ev_rn = em(p, "act", lambda s, ns=ns: s.activation(
                                    small[:, 248:248 + ns], small[:, 248:248 + ns],
                                    AF.Exp, scale=-0.5), inc=s_act)
                                evq = None
                                for j in range(ns):
                                    evq = em(p, "dve", lambda v, j=j, kp=kp:
                                             v.tensor_single_scalar(
                                        qk16[:, kp * 512 + j * 128:
                                             kp * 512 + (j + 1) * 128],
                                        rsc[:, 1280 + j * 128:1280 + (j + 1) * 128],
                                        small[:, 248 + j:249 + j], ALU.mult),
                                        waits=([(s_act, ev_rn)] if j == 0 else []),
                                        inc=(s_dve if j == ns - 1 else None))
                            else:
                                evq = ev_rope
                            last_consume = evq

                            def mk_tr(evq=evq, kp=kp, qk_slots=tuple(qk_slots),
                                      qb=qb, kb=kb):
                                tp = trctr[0] % 2
                                trctr[0] += 1
                                wv2 = [(s_dve, evq)]
                                if tr_evac[tp]:
                                    wv2.append((s_act, tr_evac[tp]))
                                ev_t2 = None
                                for j in range(len(qk_slots)):
                                    ev_t2 = em(p, "pe", lambda t, j=j, tp=tp, kp=kp:
                                               t.transpose(
                                        ps_tr[tp][:, j * 128:(j + 1) * 128],
                                        qk16[:, kp * 512 + j * 128:
                                             kp * 512 + (j + 1) * 128],
                                        ident16[:]),
                                        waits=(wv2 if j == 0 else []),
                                        inc=(s_pe if j == len(qk_slots) - 1 else None))
                                tr_read[kp] = ev_t2
                                last_evac = None
                                for j, (kind, idx, slot_i) in enumerate(qk_slots):
                                    if kind == "q":
                                        dst = big16[:, idx * 1024 + qb * 128:
                                                    idx * 1024 + qb * 128 + 128]
                                    else:
                                        dst = kT[:, (idx * 16 + kb) * 128:
                                                 (idx * 16 + kb) * 128 + 128]
                                    last_evac = em(p, "act",
                                                   lambda s, j=j, tp=tp, dst=dst:
                                                   s.activation(
                                        dst, ps_tr[tp][:, j * 128:(j + 1) * 128],
                                        AF.Identity),
                                        waits=([(s_pe, ev_t2)] if j == 0 else []),
                                        inc=(s_act if j == len(qk_slots) - 1
                                             else None))
                                tr_evac[tp] = last_evac
                            pending_tr.append(mk_tr)
                        if v_slots:
                            lo = v_slots[0][2] * 128
                            nv = len(v_slots)
                            kvh0 = v_slots[0][1]
                            vdst = AP(v_sb, (kvh0 * 16 + kb) * 128,
                                      [[8192, 128], [2048, nv], [1, 128]])
                            vsrc = AP(ps_qkv[qp], lo, [[512, 128], [128, nv], [1, 128]])
                            last_consume = em(p, "dve", lambda v, vdst=vdst, vsrc=vsrc:
                                              v.tensor_copy(vdst, vsrc),
                                              waits=[(s_pe, ev_mm)], inc=s_dve)
                        qkv_done[qp] = s_dve.n
                        citer += 1
                    wchunk += 1
            for fn in pending_tr:
                fn()
            pending_tr = []
            p.run(nc)

        if DEBUG_PROJ and is_a:
            p = Plan()
            DMA(p, kT_dbg, kT[:, :], eng="sync")
            DMA(p, v_dbg, v_sb[:, :], eng="sync")
            ev0 = DMA(p, q_dbg, big16[:, :], eng="sync")
            em(p, "sync", lambda sy: sy.nop(), waits=[(s_dma, ev0)])
            p.run(nc)

        if not is_a:
            def unpack_k(g):
                with g.register("rk") as rk:
                    g.cc_rank_ld(rk, replica_groups=RG)
                    g.reg_add(rk, rk, 7)
                    g.reg_mod(rk, rk, 8)
                    g.reg_mul(rk, rk, 128 * 8192)
                    return g.dma_start(
                        out=AP(kT, 0, [[8192, 128], [2048, 4], [1, 1024]]),
                        in_=AP(ag_out.tensor, rk, [[8192, 128], [1024, 4], [1, 1024]]))

            def unpack_v(g):
                with g.register("rk2") as rk:
                    g.cc_rank_ld(rk, replica_groups=RG)
                    g.reg_add(rk, rk, 7)
                    g.reg_mod(rk, rk, 8)
                    g.reg_mul(rk, rk, 128 * 8192)
                    g.reg_add(rk, rk, 4096)
                    return g.dma_start(
                        out=AP(v_sb, 0, [[8192, 128], [2048, 4], [1, 1024]]),
                        in_=AP(ag_out.tensor, rk, [[8192, 128], [1024, 4], [1, 1024]]))

        p = Plan()
        upk_ev = None
        if not is_a:
            em(p, "pool", unpack_k, waits=[(s_cc, s_cc.n)], inc=s_hdma, k=16)
            upk_ev = em(p, "pool", unpack_v, inc=s_hdma, k=16)
        with ExitStack() as pst:
            ps_l = [pst.enter_context(nc.psum_tensor(f"pl{variant}{i}", [128, 1024],
                                                     F32)) for i in range(2)]
            ps_ao = pst.enter_context(nc.psum_tensor(f"pa{variant}", [128, 1024], F32))
            ps_s = pst.enter_context(nc.psum_tensor(f"pss{variant}", [128, 1024], F32))
            probs = [a16(0, 1024), a16(1024, 1024)]
            AO = 2048
            rbuf_v = a32(18432, 1024)
            lnr_v = a32(20480, 1024)

            exp_done = [0, 0]
            mask_done = [0, 0]
            pv_done = [0, 0]
            aodiv_ev = 0
            r_ev = 0
            for grp in range(4):
                for kvh in range(NKV):
                    kbs = [kb for kb in range(nkb) if runs_for(is_a, grp, kb)]
                    if not is_a:
                        kbs = [kb for kb in kbs if kb >= 8] + \
                              [kb for kb in kbs if kb < 8]
                    bank_started = [False, False]
                    esm = [0]
                    halo_waited = [False]
                    pend = []

                    def emit_pv(ent, kvh=kvh, bank_started=bank_started, esm=esm):
                        runs, lp, kb, wv_pv, first_kvh = ent
                        first = True
                        for qp2 in range(2):
                            for (q0, ln, masks) in runs:
                                stA = not bank_started[qp2]
                                wv = list(wv_pv) if first else []
                                if first and first_kvh:
                                    if aodiv_ev:
                                        wv.append((s_dve, aodiv_ev))
                                    if r_ev:
                                        wv.append((s_act, r_ev))
                                first = False
                                vt = v_sb[:, (kvh * 16 + kb) * 128:
                                          (kvh * 16 + kb) * 128 + 128]
                                pr = AP(arena, lp * 1024 + qp2 * 512 + q0,
                                        [[24576, 128], [256, 2], [1, ln]])
                                aod = AP(ps_ao, qp2 * 512 + q0,
                                         [[1024, 128], [256, 2], [1, ln]])
                                smd = AP(ps_s, qp2 * 512 + q0,
                                         [[1024, 128], [256, 2], [1, ln]])
                                em(p, "pe", lambda t, aod=aod, vt=vt, pr=pr,
                                   stA=stA: t.matmul(aod, vt, pr, start=stA,
                                                     stop=False),
                                   waits=wv, inc=s_pe)
                                esm[0] = em(p, "pe", lambda t, smd=smd, pr=pr,
                                            stA=stA: t.matmul(smd, ones_bf[:], pr,
                                                              start=stA, stop=False),
                                            inc=s_pe)
                                bank_started[qp2] = True
                        pv_done[lp] = s_pe.n

                    kiter = 0
                    for kb in kbs:
                        runs = runs_for(is_a, grp, kb)
                        lp = kiter % 2
                        ev_mm_l = None
                        first = True
                        for qp2 in range(2):
                            h = kvh * 4 + 2 * qp2
                            for (q0, ln, masks) in runs:
                                wq = []
                                if first:
                                    if exp_done[lp]:
                                        wq.append((s_act, exp_done[lp]))
                                    if (not is_a) and kb < 8 and not halo_waited[0]:
                                        wq.append((s_hdma, upk_ev))
                                        halo_waited[0] = True
                                    first = False
                                lhs = kT[:, (kvh * 16 + kb) * 128:
                                         (kvh * 16 + kb) * 128 + 128]
                                rhs = AP(big16, h * 1024 + grp * 256 + q0,
                                         [[16384, 128], [1024, 2], [1, ln]])
                                dst = AP(ps_l[lp], qp2 * 512 + q0,
                                         [[1024, 128], [256, 2], [1, ln]])
                                ev_mm_l = em(p, "pe", lambda t, dst=dst, lhs=lhs,
                                             rhs=rhs: t.matmul(dst, lhs, rhs,
                                                               start=True, stop=True),
                                             waits=wq, inc=s_pe)
                        ev_exp = None
                        first = True
                        for (q0, ln, masks) in runs:
                            wv = [(s_pe, ev_mm_l)]
                            if first:
                                if pv_done[lp]:
                                    wv.append((s_pe, pv_done[lp]))
                                if mask_done[lp]:
                                    wv.append((s_dve, mask_done[lp]))
                                first = False
                            pdst = AP(arena, (lp * 1024) + q0, [[24576, 128], [256, 4],
                                                               [1, ln]])
                            psrc = AP(ps_l[lp], q0, [[1024, 128], [256, 4], [1, ln]])
                            bias = hbB[:, hb_off + kb:hb_off + kb + 1]
                            ev_exp = em(p, "act", lambda s, pdst=pdst, psrc=psrc,
                                        bias=bias: s.activation(
                                pdst, psrc, AF.Exp, scale=SCALE, bias=bias),
                                waits=wv, inc=s_act)
                        exp_done[lp] = ev_exp
                        nmask = 0
                        for (q0, ln, masks) in runs:
                            for (off, mcls) in masks:
                                mt = {"causal": mcaus,
                                      "upper": (mstrU if is_a else minclU)}[mcls]
                                pap = AP(arena, lp * 1024 + q0 + off,
                                         [[24576, 128], [256, 4], [1, 128]])
                                mb = AP(mt, 0, [[128, 128], [0, 4], [1, 128]])
                                em(p, "dve", lambda g, pap=pap, mb=mb:
                                   g.tensor_tensor(pap, pap, mb, ALU.mult),
                                   waits=([(s_act, ev_exp)] if nmask == 0 else []),
                                   inc=s_dve)
                                nmask += 1
                        if nmask:
                            mask_done[lp] = s_dve.n
                        wv_pv = [(s_act, ev_exp)]
                        if nmask:
                            wv_pv.append((s_dve, mask_done[lp]))
                        pend.append((runs, lp, kb, wv_pv, kiter == 0))
                        if len(pend) >= 2:
                            emit_pv(pend.pop(0))
                        kiter += 1
                    while pend:
                        emit_pv(pend.pop(0))
                    em(p, "act", lambda s: s.activation(lnr_v, ps_s[:], AF.Ln),
                       waits=[(s_pe, esm[0])])
                    r_ev = em(p, "act", lambda s: s.activation(
                        rbuf_v, lnr_v, AF.Exp, scale=-1.0), inc=s_act)
                    ao_dst = AP(arena, AO + (kvh * 4) * 1024 + grp * 256,
                                [[24576, 128], [1024, 4], [1, 256]])
                    ao_src = AP(ps_ao, 0, [[1024, 128], [256, 4], [1, 256]])
                    rr = rbuf_v.rearrange("p (q c) -> p q c", q=4)
                    aodiv_ev = em(p, "dve", lambda v, ao_dst=ao_dst, ao_src=ao_src,
                                  rr=rr: v.tensor_tensor(ao_dst, ao_src, rr, ALU.mult),
                                  waits=[(s_act, r_ev)], inc=s_dve)
            p.run(nc)

        p = Plan()
        with ExitStack() as pst:
            ps_o = [pst.enter_context(nc.psum_tensor(f"po{variant}{i}", [128, 512],
                                                     F32)) for i in range(2)]
            AO = 2048
            evac_ev = [0, 0]
            wo_mm_done = {}
            for c in range(4):
                par = c % 2
                wv = [(s_pe, wo_mm_done[c - 2])] if c - 2 in wo_mm_done else []
                wled = s_wdma
                weng = "pool"
                ev_w = DMA(p, AP(big16, par * 8192, [[16384, 128], [512, 16], [1, 512]]),
                           AP(wo.tensor, 512 * c, [[D, 128], [128 * D, 16], [1, 512]]),
                           led=wled, eng=weng, waits=wv)
                for qb in range(NQB):
                    op = (c * NQB + qb) % 2
                    wv2 = [(wled, ev_w)]
                    if evac_ev[op]:
                        wv2.append((s_dve, evac_ev[op]))
                    ev_mm = None
                    for hh in range(16):
                        lhs = AP(arena, AO + hh * 1024 + qb * 128, [[24576, 128],
                                                                   [1, 128]])
                        rhs = big16[:, par * 8192 + hh * 512:par * 8192 + (hh + 1) * 512]
                        ev_mm = em(p, "pe", lambda t, lhs=lhs, rhs=rhs, hh=hh, op=op:
                                   t.matmul(ps_o[op][:], lhs, rhs, start=(hh == 0),
                                            stop=(hh == 15)),
                                   waits=(wv2 if hh == 0 else []),
                                   inc=(s_pe if hh == 15 else None))
                    wo_mm_done[c] = ev_mm
                    xd = x_sb[:, qb * D + 512 * c:qb * D + 512 * c + 512]
                    evac_ev[op] = em(p, "dve", lambda v, xd=xd, op=op: v.tensor_tensor(
                        xd, xd, ps_o[op][:], ALU.add), waits=[(s_pe, ev_mm)],
                        inc=s_dve)
            p.run(nc)


    ffn_ct = [0]

    ffn_ct = [0]

    def ffn_layer():
        ffi = ffn_ct[0]; ffn_ct[0] += 1
        gssB = a32(0, 2048)
        bssB = a32(4096, 2048)
        wbuf = a32(8192, 2048)
        s2buf = a32(12288, 2048)
        pwbuf = a32(16384, 2048)
        ps2buf = a32(20480, 2048)
        BT_off, ST_off = 16384, 20480
        BT = a16(BT_off, 4096)
        ST = a16(ST_off, 4096)
        bsel16 = qk16[:, 512:768]
        ssel16 = qk16[:, 768:1024]
        ts_all = rsc[:, 0:512]
        OH_all = rsc[:, 512:1024]
        v64 = rsc[:, 1024:1536]

        p = Plan()
        with ExitStack() as pst:
            pp = [pst.enter_context(nc.psum_tensor(f"pp{ffi}_{i}", [128, 1536], F32))
                  for i in range(2)]
            ps_t = pst.enter_context(nc.psum_tensor(f"ps_t{ffi}", [128, 128], F32))
            ps_t16 = pst.enter_context(nc.psum_tensor(f"ps_t16{ffi}", [128, 128], BF16))
            ps_b2 = pp[0]

            prev_mm = 0
            prev_cp = 0
            for ri, dstB in ((0, gssB), (1, bssB)):
                wv = [(s_pe, prev_mm)] if prev_mm else []
                ev_l = DMA(p, wbuf[0:1, :], rows_d[ri:ri + 1, :], led=s_hdma,
                           waits=wv)
                for j in range(4):
                    wvj = [(s_hdma, ev_l)] if j == 0 else []
                    if prev_cp:
                        wvj.append((s_dve, prev_cp))
                    ev_b = em(p, "pe", lambda t, j=j: t.matmul(
                        ps_b2[:, 0:512], ones1[:], wbuf[0:1, j * 512:(j + 1) * 512],
                        start=True, stop=True), waits=wvj, inc=s_pe)
                    prev_cp = em(p, "dve", lambda v, j=j, dstB=dstB: v.tensor_copy(
                        dstB[:, j * 512:(j + 1) * 512], ps_b2[:, 0:512]),
                        waits=[(s_pe, ev_b)], inc=s_dve)
                prev_mm = s_pe.n
            ev_bc = s_dve.n

            ev_acc = None
            for qb in range(NQB):
                xs_ap = x_sb[:, qb * D:(qb + 1) * D]
                em(p, "act", lambda s, xs_ap=xs_ap, qb=qb: s.activation(
                    y16[:], xs_ap, AF.Square, accum_out=stats[:, 32 + qb:33 + qb]))
                ev_acc = em(p, "act", lambda s, xs_ap=xs_ap, qb=qb: s.activation(
                    y16[:], xs_ap, AF.Identity, accum_out=stats[:, qb:qb + 1]),
                    inc=s_act)
            mu8 = small[:, 224:232]
            t8 = small[:, 232:240]
            v8 = small[:, 240:248]
            i2b = AP(consts, 2, [[8, 128], [0, 8]])
            e1b = AP(consts, 1, [[8, 128], [0, 8]])
            em(p, "dve", lambda v: v.tensor_tensor(mu8, stats[:, 0:8], i2b, ALU.mult),
               waits=[(s_act, ev_acc)])
            em(p, "dve", lambda v: v.drain())
            em(p, "dve", lambda v: v.tensor_single_scalar(stats[:, 64:72], mu8, -1.0,
                                                          ALU.mult))
            em(p, "dve", lambda v: v.tensor_tensor(t8, mu8, mu8, ALU.mult))
            em(p, "dve", lambda v: v.drain())
            em(p, "dve", lambda v: v.tensor_tensor(v8, stats[:, 32:40], i2b, ALU.mult))
            em(p, "dve", lambda v: v.drain())
            ev1 = em(p, "dve", lambda v: v.tensor_sub(v8, v8, t8), inc=s_dve)
            em(p, "act", lambda s: s.activation(t8, v8, AF.Ln, bias=consts[:, 1:2]),
               waits=[(s_dve, ev1)])
            em(p, "act", lambda s: s.activation(stats[:, 96:104], t8, AF.Exp,
                                                scale=-0.5))
            ev2 = em(p, "act", lambda s: s.activation(small[:, 248:256], t8, AF.Exp,
                                                      scale=-1.0), inc=s_act)
            em(p, "dve", lambda v: v.tensor_tensor(small[:, 248:256], v8,
                                                   small[:, 248:256], ALU.mult),
               waits=[(s_act, ev2)])
            ev3 = em(p, "dve", lambda v: v.drain(), inc=s_dve)
            em(p, "act", lambda s: s.activation(small[:, 248:256], small[:, 248:256],
                                                AF.Ln, bias=consts[:, 1:2]),
               waits=[(s_dve, ev3)])
            ev4 = em(p, "act", lambda s: s.activation(small[:, 248:256],
                                                      small[:, 248:256], AF.Exp,
                                                      scale=-0.5), inc=s_act)
            em(p, "dve", lambda v: v.tensor_tensor(stats[:, 128:136], stats[:, 96:104],
                                                   small[:, 248:256], ALU.mult),
               waits=[(s_act, ev4)])
            ev_C = em(p, "dve", lambda v: v.drain(), inc=s_dve)
            em(p, "dve", lambda v: v.tensor_tensor(stats[:, 8:16], stats[:, 64:72],
                                                   stats[:, 96:104], ALU.mult))
            em(p, "dve", lambda v: v.tensor_tensor(stats[:, 16:24], stats[:, 64:72],
                                                   stats[:, 128:136], ALU.mult))
            ev_bias8 = em(p, "dve", lambda v: v.drain(), inc=s_dve)

            POOL_QBS = ()
            first_eng = {"dve": True, "pool": True}
            red_ev = {}
            pend1 = []

            def flush1(ent):
                qb, s2, ev_s2, sem_e = ent
                ev_sgn = em(p, "act", lambda s, s2=s2: s.activation(
                    s2[:], s2[:], AF.Sign), waits=[(sem_e, ev_s2)], inc=s_act)
                red_ev[qb] = em(p, "dve", lambda v, qb=qb, s2=s2: v.tensor_reduce(
                    ts_all[:, qb * 64:(qb + 1) * 64],
                    s2.rearrange("p (t j) -> p t j", t=64), AX.X, ALU.add),
                    waits=[(s_act, ev_sgn)], inc=s_dve)

            for qb in range(NQB):
                eng = "pool" if qb in POOL_QBS else "dve"
                sem_e = s_pool if eng == "pool" else s_dve
                if eng == "pool":
                    wb, s2 = pwbuf, ps2buf
                else:
                    wb = wbuf if qb % 2 == 0 else pwbuf
                    s2 = s2buf if qb % 2 == 0 else ps2buf
                xs_ap = x_sb[:, qb * D:(qb + 1) * D]
                wv0 = ([(s_dve, ev_C), (s_dve, ev_bc)]
                       if first_eng[eng] else [])
                first_eng[eng] = False
                if eng == "pool":
                    nmb = AP(stats, 64 + qb, [[156, 128], [0, D]])
                    cb = AP(stats, 128 + qb, [[156, 128], [0, D]])
                    rb = AP(stats, 96 + qb, [[156, 128], [0, D]])
                    em(p, eng, lambda v, xs_ap=xs_ap, wb=wb, nmb=nmb:
                       v.tensor_tensor(wb[:], xs_ap, nmb, ALU.add), waits=wv0)
                    em(p, eng, lambda v, wb=wb, s2=s2, rb=rb: v.tensor_tensor(
                        s2[:], wb[:], rb, ALU.mult))
                    em(p, eng, lambda v, xs_ap=xs_ap, s2=s2: v.tensor_tensor(
                        xs_ap, xs_ap, s2[:], ALU.add))
                    em(p, eng, lambda v, wb=wb, cb=cb: v.tensor_tensor(
                        wb[:], wb[:], cb, ALU.mult))
                    em(p, eng, lambda v, wb=wb, s2=s2: v.tensor_tensor(
                        s2[:], wb[:], gssB[:], ALU.mult))
                else:
                    em(p, "act", lambda s, xs_ap=xs_ap, qb=qb, s2=s2: s.activation(
                        s2[:], xs_ap, AF.Identity,
                        scale=stats[:, 96 + qb:97 + qb],
                        bias=stats[:, 8 + qb:9 + qb]),
                        waits=([(s_dve, ev_bias8)] if qb < 2 else
                               [(s_dve, red_ev[qb - 2])]))
                    ev_w = em(p, "act", lambda s, xs_ap=xs_ap, qb=qb, wb=wb:
                              s.activation(
                        wb[:], xs_ap, AF.Identity,
                        scale=stats[:, 128 + qb:129 + qb],
                        bias=stats[:, 16 + qb:17 + qb]), inc=s_act)
                    em(p, "dve", lambda v, xs_ap=xs_ap, s2=s2: v.tensor_tensor(
                        xs_ap, xs_ap, s2[:], ALU.add), waits=[(s_act, ev_w)])
                    em(p, "dve", lambda v, wb=wb, s2=s2: v.tensor_tensor(
                        s2[:], wb[:], gssB[:], ALU.mult))
                ev_s2 = em(p, eng, lambda v, s2=s2: v.tensor_tensor(
                    s2[:], s2[:], bssB[:], ALU.add), inc=sem_e)
                pend1.append((qb, s2, ev_s2, sem_e))
                if len(pend1) >= 2:
                    flush1(pend1.pop(0))
            while pend1:
                flush1(pend1.pop(0))
            ev_ts_pool = s_pool.n

            cs_all = small[:, 0:64]
            m8 = small[:, 64:72]
            bc8 = small[:, 72:80]
            em(p, "dve", lambda v: v.tensor_reduce(
                cs_all, ts_all.rearrange("p (q c t) -> p (q c) t", q=8, c=8),
                AX.X, ALU.add), waits=[(s_pool, ev_ts_pool)])
            em(p, "dve", lambda v: v.drain())
            em(p, "dve", lambda v: v.tensor_reduce(
                m8, cs_all.rearrange("p (q c) -> p q c", q=8), AX.X, ALU.max))
            em(p, "dve", lambda v: v.drain())
            m8b = AP(small, 64, [[256, 128], [1, 8], [0, 8]])
            em(p, "dve", lambda v, m8b=m8b: v.tensor_tensor(
                v64[:, 0:64].rearrange("p (q c) -> p q c", q=8),
                cs_all.rearrange("p (q c) -> p q c", q=8), m8b, ALU.is_ge))
            em(p, "dve", lambda v: v.drain())
            i8t = AP(iota8m, 0, [[8, 128], [0, 8], [1, 8]])
            em(p, "dve", lambda v, i8t=i8t: v.tensor_tensor(
                v64[:, 0:64].rearrange("p (q c) -> p q c", q=8),
                v64[:, 0:64].rearrange("p (q c) -> p q c", q=8), i8t, ALU.mult))
            em(p, "dve", lambda v: v.drain())
            em(p, "dve", lambda v: v.tensor_reduce(
                bc8, v64[:, 0:64].rearrange("p (q c) -> p q c", q=8), AX.X, ALU.min))
            em(p, "dve", lambda v: v.drain())
            em(p, "dve", lambda v: v.tensor_single_scalar(bc8, bc8, BIGV, ALU.add))
            em(p, "dve", lambda v: v.drain())
            ca_t = AP(ca64, 0, [[64, 128], [0, 8], [1, 64]])
            bc_b = AP(small, 72, [[256, 128], [1, 8], [0, 64]])
            em(p, "dve", lambda v, ca_t=ca_t, bc_b=bc_b: v.tensor_tensor(
                v64.rearrange("p (q t) -> p q t", q=8), ca_t, bc_b, ALU.is_equal))
            em(p, "dve", lambda v: v.drain())
            em(p, "dve", lambda v: v.tensor_single_scalar(ts_all[:], ts_all[:], BIGV,
                                                          ALU.add))
            em(p, "dve", lambda v: v.drain())
            em(p, "dve", lambda v: v.tensor_tensor(v64[:], v64[:], ts_all[:], ALU.mult))
            em(p, "dve", lambda v: v.drain())
            em(p, "dve", lambda v: v.tensor_single_scalar(v64[:], v64[:], BIGV,
                                                          ALU.subtract))
            em(p, "dve", lambda v: v.drain())
            em(p, "dve", lambda v: v.tensor_reduce(
                m8, v64.rearrange("p (q t) -> p q t", q=8), AX.X, ALU.max))
            em(p, "dve", lambda v: v.drain())
            m8b2 = AP(small, 64, [[256, 128], [1, 8], [0, 64]])
            em(p, "dve", lambda v, m8b2=m8b2: v.tensor_tensor(
                OH_all.rearrange("p (q t) -> p q t", q=8),
                v64.rearrange("p (q t) -> p q t", q=8), m8b2, ALU.is_ge))
            em(p, "dve", lambda v: v.drain())
            i64t = AP(iota64m, 0, [[64, 128], [0, 8], [1, 64]])
            em(p, "dve", lambda v, i64t=i64t: v.tensor_tensor(
                v64.rearrange("p (q t) -> p q t", q=8),
                OH_all.rearrange("p (q t) -> p q t", q=8), i64t, ALU.mult))
            em(p, "dve", lambda v: v.drain())
            em(p, "dve", lambda v: v.tensor_reduce(
                ffsc[:, FC_TIDX:FC_TIDX + 8],
                v64.rearrange("p (q t) -> p q t", q=8), AX.X, ALU.min))
            em(p, "dve", lambda v: v.drain())
            em(p, "dve", lambda v: v.tensor_single_scalar(
                ffsc[:, FC_TIDX:FC_TIDX + 8], ffsc[:, FC_TIDX:FC_TIDX + 8], BIGV,
                ALU.add))
            em(p, "dve", lambda v: v.drain())
            io64t = AP(iota64, 0, [[64, 128], [0, 8], [1, 64]])
            ti_b = AP(ffsc, FC_TIDX, [[816, 128], [1, 8], [0, 64]])
            ev_oh_all = em(p, "dve", lambda v, io64t=io64t, ti_b=ti_b: v.tensor_tensor(
                OH_all.rearrange("p (q t) -> p q t", q=8), io64t, ti_b, ALU.is_equal),
                inc=s_dve)
            em(p, "dve", lambda v: v.drain())

            em(p, "dve", lambda v: v.tensor_single_scalar(small[:, 224:232],
                                                          stats[:, 96:104], 1.0,
                                                          ALU.add))
            em(p, "dve", lambda v: v.drain())
            em(p, "dve", lambda v: v.reciprocal(small[:, 232:240], small[:, 224:232]))
            em(p, "dve", lambda v: v.drain())
            em(p, "dve", lambda v: v.tensor_tensor(small[:, 232:240], small[:, 232:240],
                                                   stats[:, 128:136], ALU.mult))
            em(p, "dve", lambda v: v.drain())
            em(p, "dve", lambda v: v.tensor_tensor(stats[:, 24:32], stats[:, 64:72],
                                                   small[:, 232:240], ALU.mult))
            ev_cc8 = em(p, "dve", lambda v: v.drain(), inc=s_dve)
            gt_evac = 0
            prev_g3 = 0
            stt_ev = {}
            tm_in = AP(iota64, 0, [[64, 128], [1, 64], [0, 32]])
            for qb in range(NQB):
                xs_ap = x_sb[:, qb * D:(qb + 1) * D]
                wb6 = wbuf if qb % 2 == 0 else pwbuf
                ev_wact = em(p, "act", lambda s, xs_ap=xs_ap, qb=qb, wb6=wb6:
                             s.activation(
                    wb6[:], xs_ap, AF.Identity,
                    scale=small[:, 232 + qb:233 + qb],
                    bias=stats[:, 24 + qb:25 + qb]),
                    waits=([(s_dve, ev_cc8)] if qb < 2 else
                           [(s_dve, stt_ev[qb - 2])]), inc=s_act)
                stt_ev[qb] = em(p, "dve", lambda v, qb=qb, wb6=wb6:
                                v.scalar_tensor_tensor(
                    s2buf.rearrange("p (t j) -> p t j", t=64), tm_in,
                    ffsc[:, FC_TIDX + qb:FC_TIDX + qb + 1],
                    wb6.rearrange("p (t j) -> p t j", t=64),
                    ALU.is_equal, ALU.mult), waits=[(s_act, ev_wact)], inc=s_dve)
                em(p, "dve", lambda v: v.drain())
                ev_wsel = em(p, "dve", lambda v, qb=qb: v.tensor_reduce(
                    small[:, 0:32], s2buf.rearrange("p (t j) -> p j t", t=64),
                    AX.X, ALU.add), inc=s_dve)
                ohs = OH_all[:, qb * 64:(qb + 1) * 64]
                wv_t = [(s_dve, ev_wsel)]
                if prev_g3:
                    wv_t.append((s_pe, prev_g3))
                ev_trf = em(p, "pe", lambda t, ohs=ohs: t.transpose(
                    ps_t[0:64, :], ohs[:, 0:64], identf[:]), waits=wv_t, inc=s_pe)
                wv_e = [(s_pe, ev_trf)]
                if gt_evac:
                    wv_e.append((s_dve, gt_evac))
                gt_evac = em(p, "dve", lambda v: v.tensor_copy(
                    ohTf[0:64, 0:128], ps_t[0:64, :]), waits=wv_e, inc=s_dve)
                prev_g3 = ev_g3 = em(p, "pe", lambda t: t.matmul(
                    pp[0][:, 1024:1090], ohTf[0:64, 0:128],
                    tableT[0:64, 0:66], start=True, stop=True),
                    waits=[(s_dve, gt_evac)], inc=s_pe)
                em(p, "dve", lambda v: v.tensor_copy(
                    gsel[:, 0:66], pp[0][:, 1024:1090]),
                    waits=[(s_pe, ev_g3)])
                em(p, "dve", lambda v: v.drain())
                em(p, "dve", lambda v: v.tensor_tensor(
                    small[:, 0:32], small[:, 0:32], gsel[:, 0:32], ALU.mult))
                em(p, "dve", lambda v: v.drain())
                em(p, "dve", lambda v, qb=qb: v.tensor_tensor(
                    ffsc[:, FC_XS + qb * 32:FC_XS + (qb + 1) * 32], small[:, 0:32],
                    gsel[:, 32:64], ALU.add))
                em(p, "dve", lambda v, qb=qb: v.tensor_copy(
                    ffsc[:, FC_OSEL + qb:FC_OSEL + qb + 1], gsel[:, 64:65]))
                em(p, "dve", lambda v: v.drain())
            em(p, "dve", lambda v: v.tensor_reduce(
                ffsc[:, FC_TMIN:FC_TMIN + 8],
                ffsc[:, FC_XS:FC_XS + 256].rearrange("p (q j) -> p q j", q=8),
                AX.X, ALU.min))
            em(p, "dve", lambda v: v.tensor_reduce(
                ffsc[:, FC_TMAX:FC_TMAX + 8],
                ffsc[:, FC_XS:FC_XS + 256].rearrange("p (q j) -> p q j", q=8),
                AX.X, ALU.max))
            em(p, "dve", lambda v: v.drain())
            ev_u8 = em(p, "dve", lambda v: v.tensor_copy(ohall_u8[:], OH_all),
                       inc=s_dve)
            em(p, "dve", lambda v: v.drain())
            em(p, "dve", lambda v: v.tensor_single_scalar(small[:, 76:77],
                                                          consts[:, 7:8], -1.0,
                                                          ALU.mult))
            em(p, "dve", lambda v: v.drain())
            u8m = ohall_u8.ap()
            tminb = AP(ffsc, FC_TMIN, [[816, 128], [1, 8], [0, 64]])
            tmaxb = AP(ffsc, FC_TMAX, [[816, 128], [1, 8], [0, 64]])
            bigp = AP(consts, 7, [[8, 128], [0, 8], [0, 64]])
            bign = AP(small, 76, [[256, 128], [0, 8], [0, 64]])
            ev_selmin = em(p, "dve", lambda v: v.select(
                v64.rearrange("p (q t) -> p q t", q=8),
                u8m.rearrange("p (q t) -> p q t", q=8), tminb, bigp), inc=s_dve)
            em(p, "dve", lambda v: v.drain())
            red_d = [0, 0]
            ev_sel = ev_selmin
            for mode, col0, alu in ((0, 0, ALU.min), (1, 10, ALU.max)):
                if mode == 1:
                    ev_sel = em(p, "dve", lambda v: v.select(
                        v64.rearrange("p (q t) -> p q t", q=8),
                        u8m.rearrange("p (q t) -> p q t", q=8), tmaxb, bign),
                        waits=[(s_dve, red_d[0]), (s_dve, red_d[1])], inc=s_dve)
                    em(p, "dve", lambda v: v.drain())
                for qb in range(NQB):
                    par = qb % 2
                    dst = ps_t if par == 0 else pp[1]
                    wv = [(s_dve, ev_sel)] if qb < 2 else []
                    if red_d[par]:
                        wv.append((s_dve, red_d[par]))
                    ev_tt = em(p, "pe", lambda t, qb=qb, dst=dst: t.transpose(
                        dst[0:64, 0:128], v64[:, qb * 64:(qb + 1) * 64], identf[:]),
                        waits=wv, inc=s_pe)
                    red_d[par] = em(p, "dve", lambda v, qb=qb, col0=col0, alu=alu,
                                    dst=dst: v.tensor_reduce(
                        tmin_s[0:64, col0 + qb:col0 + qb + 1], dst[0:64, 0:128],
                        AX.X, alu), waits=[(s_pe, ev_tt)], inc=s_dve)
            red_last = em(p, "dve", lambda v: v.drain(),
                          waits=[(s_dve, red_d[0]), (s_dve, red_d[1])], inc=s_dve)
            pp1_guard = red_d[1]
            em(p, "dve", lambda v: v.tensor_reduce(agr[0:64, 0:1], tmin_s[0:64, 0:8],
                                                   AX.X, ALU.min))
            em(p, "dve", lambda v: v.drain())
            em(p, "dve", lambda v: v.tensor_single_scalar(agr[0:64, 0:1],
                                                          agr[0:64, 0:1], -1.0,
                                                          ALU.mult))
            em(p, "dve", lambda v: v.tensor_reduce(agr[0:64, 1:2], tmin_s[0:64, 10:18],
                                                   AX.X, ALU.max))
            ev_agr = em(p, "dve", lambda v: v.drain(), inc=s_dve)
            ev_st = em(p, "sync", lambda sy: sy.dma_start(out=ar_in,
                                                          in_=agr[0:64, 0:2]),
                       waits=[(s_dve, ev_agr)], inc=s_dma, k=16)
            ev_cc = em(p, "pool", lambda g: g.collective_compute(
                "AllReduce", ALU.max, RG, [ar_in], [ar_out]),
                waits=[(s_dma, ev_st)], inc=s_cc, k=1)
            t16_evac = 0
            mm_done = [0, 0]
            prev_tr16 = 0
            for qb in range(NQB):
                par = qb % 2
                ohs = OH_all[:, qb * 64:(qb + 1) * 64]
                wv_oh = [(s_pe, prev_tr16)] if prev_tr16 else []
                ev_oh16 = em(p, "dve", lambda v, ohs=ohs: v.tensor_copy(
                    ohq16[:, 0:64], ohs), waits=wv_oh, inc=s_dve)
                wv_t2 = [(s_dve, ev_oh16)]
                if t16_evac:
                    wv_t2.append((s_dve, t16_evac))
                prev_tr16 = ev_tr16 = em(p, "pe", lambda t: t.transpose(
                    ps_t16[0:64, :], ohq16[:, 0:64], ident16[:]), waits=wv_t2,
                    inc=s_pe)
                t16_evac = em(p, "dve", lambda v: v.tensor_copy(
                    ohT16[0:64, 0:128], ps_t16[0:64, :]),
                    waits=[(s_pe, ev_tr16)], inc=s_dve)
                wv_mm = [(s_dve, t16_evac)]
                if mm_done[par]:
                    wv_mm.append((s_dve, mm_done[par]))
                elif par == 1:
                    wv_mm.append((s_dve, pp1_guard))
                em(p, "pe", lambda t, par=par: t.matmul(
                    pp[par][:, 0:512], ohT16[0:64, 0:128],
                    table16[0:64, 0:512], start=True, stop=True), waits=wv_mm)
                ev_g2 = em(p, "pe", lambda t, par=par: t.matmul(
                    pp[par][:, 512:1024], ohT16[0:64, 0:128],
                    table16[0:64, 512:1024], start=True, stop=True), inc=s_pe)
                em(p, "dve", lambda v, qb=qb, par=par: v.tensor_copy(
                    BT[:, qb * 512:(qb + 1) * 512], pp[par][:, 0:512]),
                    waits=[(s_pe, ev_g2)])
                mm_done[par] = em(p, "dve", lambda v, qb=qb, par=par: v.tensor_copy(
                    ST[:, qb * 512:(qb + 1) * 512], pp[par][:, 512:1024]),
                    inc=s_dve)
            ev_bst = s_dve.n
            ev_ld = em(p, "pool", lambda g: g.dma_start(out=agr[0:64, 2:4], in_=ar_out),
                       waits=[(s_cc, ev_cc)], inc=s_hdma, k=16)
            em(p, "dve", lambda v: v.tensor_single_scalar(
                agr[0:64, 2:3], agr[0:64, 2:3], -1.0, ALU.mult),
                waits=[(s_hdma, ev_ld)])
            ev_n = em(p, "dve", lambda v: v.drain(), inc=s_dve)
            ev_t1 = em(p, "pe", lambda t: t.transpose(
                ps_t[0:1, 0:64], agr[0:64, 2:3], identf[0:64, 0:64]),
                waits=[(s_dve, ev_n)], inc=s_pe)
            ev_t2 = em(p, "pe", lambda t: t.transpose(
                ps_t[0:1, 64:128], agr[0:64, 3:4], identf[0:64, 0:64]), inc=s_pe)
            em(p, "dve", lambda v: v.tensor_copy(small[0:1, 0:64], ps_t[0:1, 0:64]),
               waits=[(s_pe, ev_t2)])
            ev_rw = em(p, "dve", lambda v: v.tensor_copy(small[0:1, 64:128],
                                                         ps_t[0:1, 64:128]),
                       inc=s_dve)
            ev_b1 = em(p, "pe", lambda t: t.matmul(
                ps_t[:, 0:64], ones1[:], small[0:1, 0:64], start=True, stop=True),
                waits=[(s_dve, ev_rw)], inc=s_pe)
            ev_b2 = em(p, "pe", lambda t: t.matmul(
                ps_t[:, 64:128], ones1[:], small[0:1, 64:128], start=True,
                stop=True), inc=s_pe)
            em(p, "dve", lambda v: v.tensor_copy(small[:, 88:152], ps_t[:, 0:64]),
               waits=[(s_pe, ev_b2)])
            ev_gb = em(p, "dve", lambda v: v.tensor_copy(small[:, 152:216],
                                                         ps_t[:, 64:128]),
                       inc=s_dve)
            em(p, "dve", lambda v: v.drain())
            tgb = AP(small, 88, [[256, 128], [0, 8], [1, 64]])
            txb = AP(small, 152, [[256, 128], [0, 8], [1, 64]])
            em(p, "dve", lambda v, tgb=tgb: v.tensor_tensor(
                v64.rearrange("p (q t) -> p q t", q=8),
                OH_all.rearrange("p (q t) -> p q t", q=8), tgb, ALU.mult))
            em(p, "dve", lambda v: v.drain())
            em(p, "dve", lambda v: v.tensor_reduce(
                small[:, 96:104], v64.rearrange("p (q t) -> p q t", q=8),
                AX.X, ALU.add))
            em(p, "dve", lambda v, txb=txb: v.tensor_tensor(
                v64.rearrange("p (q t) -> p q t", q=8),
                OH_all.rearrange("p (q t) -> p q t", q=8), txb, ALU.mult))
            em(p, "dve", lambda v: v.drain())
            em(p, "dve", lambda v: v.tensor_reduce(
                small[:, 104:112], v64.rearrange("p (q t) -> p q t", q=8),
                AX.X, ALU.add))
            em(p, "dve", lambda v: v.drain())
            em(p, "dve", lambda v: v.tensor_sub(small[:, 112:120], small[:, 104:112],
                                                small[:, 96:104]))
            em(p, "dve", lambda v: v.drain())
            e8b = AP(consts, 6, [[8, 128], [0, 8]])
            em(p, "dve", lambda v, e8b=e8b: v.tensor_tensor(
                small[:, 112:120], small[:, 112:120], e8b, ALU.add))
            em(p, "dve", lambda v: v.drain())
            em(p, "dve", lambda v: v.reciprocal(small[:, 120:128], small[:, 112:120]))
            em(p, "dve", lambda v: v.drain())
            em(p, "dve", lambda v: v.tensor_single_scalar(
                small[:, 120:128], small[:, 120:128], 16.0, ALU.mult))
            em(p, "dve", lambda v: v.drain())
            U16MAX = float(np.float32(1.0 - 1e-6) * 16)
            mnb = AP(small, 96, [[256, 128], [1, 8], [0, 32]])
            invb = AP(small, 120, [[256, 128], [1, 8], [0, 32]])
            GI = ffsc[:, FC_GI:FC_GI + 256]
            XL = ffsc[:, FC_XL:FC_XL + 256]
            XS = ffsc[:, FC_XS:FC_XS + 256]
            em(p, "dve", lambda v, mnb=mnb: v.tensor_tensor(
                GI.rearrange("p (q j) -> p q j", q=8),
                XS.rearrange("p (q j) -> p q j", q=8), mnb, ALU.subtract))
            em(p, "dve", lambda v: v.drain())
            em(p, "dve", lambda v, invb=invb: v.tensor_tensor(
                GI.rearrange("p (q j) -> p q j", q=8),
                GI.rearrange("p (q j) -> p q j", q=8), invb, ALU.mult))
            em(p, "dve", lambda v: v.drain())
            em(p, "dve", lambda v: v.tensor_scalar(GI, GI, 0.0, U16MAX,
                                                   ALU.max, ALU.min))
            em(p, "dve", lambda v: v.drain())
            em(p, "dve", lambda v: v.tensor_scalar(XL, GI, MAGIC, MAGIC,
                                                   ALU.add, ALU.subtract))
            em(p, "dve", lambda v: v.drain())
            em(p, "dve", lambda v: v.tensor_tensor(small[:, 0:256], XL, GI,
                                                   ALU.is_gt))
            em(p, "dve", lambda v: v.drain())
            em(p, "dve", lambda v: v.tensor_sub(XL, XL, small[:, 0:256]))
            em(p, "dve", lambda v: v.drain())
            em(p, "dve", lambda v: v.tensor_sub(GI, GI, XL))
            em(p, "dve", lambda v: v.drain())
            eq16 = a16(8192, 4096)
            pbt16 = a16(0, 4096)
            pst16 = a16(4096, 4096)
            eq3v = eq16.rearrange("p (qj g) -> p qj g", g=16)
            GIb = AP(ffsc, FC_XL, [[816, 128], [1, 256], [0, 16]])
            g16 = AP(iota64, 0, [[64, 128], [0, 256], [1, 16]])
            ev_eq = em(p, "dve", lambda v: v.tensor_tensor(
                eq3v, GIb, g16, ALU.is_equal), inc=s_dve)
            em(p, "dve", lambda v: v.drain())
            BTg3 = AP(arena, BT_off, [[24576, 128], [16, 256], [1, 16]])
            STg3 = AP(arena, ST_off, [[24576, 128], [16, 256], [1, 16]])
            pb3 = pbt16.rearrange("p (qj g) -> p qj g", g=16)
            ps3 = pst16.rearrange("p (qj g) -> p qj g", g=16)
            bselF = cosg[:, 0:256]
            sselF = cosg[:, 256:512]
            USE_POOL_SPLINE = False
            if USE_POOL_SPLINE:
                ev_psTT = em(p, "pool", lambda v: v.tensor_tensor(
                    ps3, eq3v, STg3, ALU.mult), waits=[(s_dve, ev_eq)], inc=s_pool)
            else:
                ev_psTT = em(p, "dve", lambda v: v.tensor_tensor(
                    ps3, eq3v, STg3, ALU.mult), inc=s_dve)
            em(p, "dve", lambda v: v.tensor_tensor(pb3, eq3v, BTg3, ALU.mult))
            em(p, "dve", lambda v: v.drain())
            em(p, "dve", lambda v: v.tensor_reduce(bselF, pb3, AX.X, ALU.add))
            em(p, "dve", lambda v: v.tensor_reduce(sselF, ps3, AX.X, ALU.add),
               waits=([(s_pool, ev_psTT)] if USE_POOL_SPLINE else []))
            em(p, "dve", lambda v: v.drain())
            em(p, "dve", lambda v: v.tensor_tensor(small[:, 0:256], sselF,
                                                   GI, ALU.mult))
            em(p, "dve", lambda v: v.drain())
            em(p, "dve", lambda v: v.tensor_tensor(small[:, 0:256], small[:, 0:256],
                                                   bselF, ALU.add))
            em(p, "dve", lambda v: v.drain())
            osb = AP(ffsc, FC_OSEL, [[816, 128], [1, 8], [0, 32]])
            ev_delta = em(p, "dve", lambda v, osb=osb: v.tensor_tensor(
                small[:, 0:256].rearrange("p (q j) -> p q j", q=8),
                small[:, 0:256].rearrange("p (q j) -> p q j", q=8), osb, ALU.mult),
                inc=s_dve)
            em(p, "dve", lambda v: v.drain())
            pbufs = [wbuf, s2buf]
            pfree = [0, 0]
            for qb in range(NQB):
                to_pool = qb in POOL_QBS
                wb = pbufs[qb % 2] if to_pool else wbuf
                dexp = AP(small, qb * 32, [[256, 128], [0, 64], [1, 32]])
                wv = [(s_pool, pfree[qb % 2])] if (to_pool and pfree[qb % 2]) \
                    else []
                ev_w3 = em(p, "dve", lambda v, qb=qb, wb=wb, dexp=dexp:
                           v.scalar_tensor_tensor(
                    wb.rearrange("p (t j) -> p t j", t=64), tm_in,
                    ffsc[:, FC_TIDX + qb:FC_TIDX + qb + 1], dexp,
                    ALU.is_equal, ALU.mult), waits=wv, inc=s_dve)
                em(p, "dve", lambda v: v.drain())
                if to_pool:
                    pfree[qb % 2] = em(p, "pool", lambda v, qb=qb, wb=wb:
                                       v.tensor_tensor(
                        x_sb[:, qb * D:(qb + 1) * D],
                        x_sb[:, qb * D:(qb + 1) * D], wb[:], ALU.add),
                        waits=[(s_dve, ev_w3)], inc=s_pool)
                else:
                    em(p, "dve", lambda v, qb=qb, wb=wb: v.tensor_tensor(
                        x_sb[:, qb * D:(qb + 1) * D],
                        x_sb[:, qb * D:(qb + 1) * D], wb[:], ALU.add),
                        inc=s_dve)
            p.run(nc)

    attn_layer("a")
    if N_SUBLAYERS >= 2:
        ffn_layer()
    if DEBUG_FFN:
        p = Plan()
        DMA(p, ffsc_dbg, ffsc[:, :], eng="sync")
        DMA(p, tmin_dbg, tmin_s[0:64, :], eng="sync")
        DMA(p, agr_dbg, agr[0:64, :], eng="sync")
        DMA(p, bt_dbg, arena[:, 16384:20480], eng="sync")
        ev0 = DMA(p, st_dbg, arena[:, 20480:24576], eng="sync")
        em(p, "sync", lambda sy: sy.nop(), waits=[(s_dma, ev0)])
        p.run(nc)
    if N_SUBLAYERS >= 3:
        attn_layer("b")
    if N_SUBLAYERS >= 4:
        ffn_layer()

    p = Plan()
    ev = em(p, "sync", lambda sy: sy.dma_start(
        out=AP(x_out.tensor, 0, [[D, 128], [128 * D, NQB], [1, D]]),
        in_=AP(x_sb, 0, [[NQB * D, 128], [D, NQB], [1, D]])),
        waits=[(s_dve, s_dve.n), (s_pool, s_pool.n)], inc=s_dma, k=16)
    em(p, "sync", lambda sy: sy.nop(), waits=[(s_dma, ev)])
    p.run(nc)

    st.close()
    return nc


_NC_CACHE = None


def kernel(L0_attn_a_x, L0_attn_a_wq, L0_attn_a_wk, L0_attn_a_wv, L0_attn_a_wo,
           L0_attn_b_wq, L0_attn_b_wk, L0_attn_b_wv, L0_attn_b_wo,
           L0_ffn_ln_gamma, L0_ffn_ln_beta, L0_ffn_spline_bases,
           L0_ffn_spline_slopes, L0_ffn_output_scale):
    global _NC_CACHE
    x = np.ascontiguousarray(np.asarray(L0_attn_a_x, dtype=np.float32).reshape(8192, D))
    wqkv_a_h = np.ascontiguousarray(
        np.concatenate([L0_attn_a_wq, L0_attn_a_wk, L0_attn_a_wv], axis=1)
        .astype(np.float32))
    wqkv_b_h = np.ascontiguousarray(
        np.concatenate([L0_attn_b_wq, L0_attn_b_wk, L0_attn_b_wv], axis=1)
        .astype(np.float32))
    wo_a_h = np.ascontiguousarray(np.asarray(L0_attn_a_wo, dtype=np.float32))
    wo_b_h = np.ascontiguousarray(np.asarray(L0_attn_b_wo, dtype=np.float32))
    tb = np.ascontiguousarray(np.asarray(L0_ffn_spline_bases, dtype=np.float32))
    ts_ = np.ascontiguousarray(np.asarray(L0_ffn_spline_slopes, dtype=np.float32))
    lg = np.ascontiguousarray(np.asarray(L0_ffn_ln_gamma, dtype=np.float32))
    lb = np.ascontiguousarray(np.asarray(L0_ffn_ln_beta, dtype=np.float32))
    osc = np.ascontiguousarray(np.asarray(L0_ffn_output_scale, dtype=np.float32))

    if _NC_CACHE is None:
        _NC_CACHE = build_nc()
    nc = _NC_CACHE

    in_maps = []
    for c in range(8):
        lo = c * T - 512
        if lo < 0:
            xs = np.concatenate([np.zeros((512, D), np.float32), x[0:c * T + T]],
                                axis=0)
        else:
            xs = x[lo:c * T + T]
        hb_a = np.zeros(NKB_A, np.float32)
        hb_b = np.zeros(NKB_B, np.float32)
        if c == 0:
            hb_a[:HALO_A] = HBIAS
            hb_b[:8] = HBIAS
        posb = (np.arange(NKB_A, dtype=np.float32) * 128.0) + float(c * T - 512)
        if c == 0:
            posb[:HALO_A] = 0.0
        in_maps.append({
            "x_in": np.ascontiguousarray(xs),
            "wqkv_a": wqkv_a_h, "wo_a": wo_a_h,
            "wqkv_b": wqkv_b_h, "wo_b": wo_b_h,
            "tbases": tb, "tslopes": ts_, "lgamma": lg, "lbeta": lb, "oscale": osc,
            "hb_a_in": hb_a, "hb_b_in": hb_b, "posb_in": posb,
        })
    res = run_bass_kernel_spmd(nc, in_maps, core_ids=list(range(8)))
    out = np.concatenate([res.results[c]["x_out"] for c in range(8)], axis=0)
    return out.reshape(1, 8192, D).astype(np.float32)


# version 28
# speedup vs baseline: 1.0073x; 1.0072x over previous
import numpy as np
from contextlib import ExitStack

import concourse.bass as bass
import concourse.mybir as mybir
from concourse.bass_utils import run_bass_kernel_spmd

F32 = mybir.dt.float32
BF16 = mybir.dt.bfloat16
AF = mybir.ActivationFunctionType
ALU = mybir.AluOpType
AX = mybir.AxisListType
AP = bass.AP

T = 1024
D = 2048
NQB = 8
HALO_A = 4
NKB_A = 12
NKB_B = 16
NKV = 4
SCALE = float(1.0 / np.sqrt(128.0))
WIN_A = 512
WIN_B = 1025
MAGIC = 8388608.0 + 4194304.0
BIGV = 1.0e4
HBIAS = -30000.0
TWO_PI = float(2 * np.pi)
TWO_PI_HI = 6.28125
TWO_PI_LO = float(2 * np.pi - 6.28125)
LN10K = float(np.log(10000.0))
RG = [[0, 1, 2, 3, 4, 5, 6, 7]]

N_SUBLAYERS = 4
DEBUG_PROJ = False
DEBUG_FFN = False


class Led:
    def __init__(self, handle):
        self.h = handle
        self.n = 0


class Plan:
    ENGS = ("pe", "dve", "act", "pool", "sync")

    def __init__(self):
        self.ops = {e: [] for e in self.ENGS}

    def op(self, eng, emit, waits=(), inc=None, k=1):
        if inc is not None:
            inc.n += k
        self.ops[eng].append((list(waits), emit, inc, k))
        return inc.n if inc is not None else None

    def _replay(self, eng, e):
        for waits, emit, inc, k in self.ops[eng]:
            for led, val in waits:
                e.wait_ge(led.h, val)
            ins = emit(e)
            if inc is not None:
                assert ins is not None, f"op on {eng} returned None but has inc"
                ins.then_inc(inc.h, k)

    def run(self, nc):
        with nc.Block() as blk:
            for eng, meth in (("pe", blk.tensor), ("dve", blk.vector),
                              ("act", blk.scalar), ("pool", blk.gpsimd),
                              ("sync", blk.sync)):
                if self.ops[eng]:
                    meth(lambda e, eng=eng: self._replay(eng, e))


def qk_block_class(c, W):
    if c < 0 or c > W + 126:
        return None
    if c == 0:
        return "causal"
    if -126 <= c - (W - 1) <= 127:
        return "upper"
    return "full"


def runs_for(is_a, grp, kb):
    W = WIN_A if is_a else WIN_B
    qoff = 512 if is_a else 1024
    runs = []
    cur = None
    for qs in (0, 1):
        c = (qoff + grp * 256 + qs * 128) - kb * 128
        cls = qk_block_class(c, W)
        if cls is None:
            if cur:
                runs.append(tuple(cur))
                cur = None
            continue
        m = None if cls == "full" else cls
        if cur is None:
            cur = [qs * 128, 128, [(0, m)] if m else []]
        else:
            if m:
                cur[2].append((cur[1], m))
            cur[1] += 128
    if cur:
        runs.append(tuple(cur))
    return runs


def build_nc():
    nc = bass.Bass("TRN2", target_bir_lowering=False)

    x_in = nc.dram_tensor("x_in", [T + 512, D], F32, kind="ExternalInput").ap()
    wqkv_a = nc.dram_tensor("wqkv_a", [D, 3072], F32, kind="ExternalInput").ap()
    wo_a = nc.dram_tensor("wo_a", [D, D], F32, kind="ExternalInput").ap()
    wqkv_b = nc.dram_tensor("wqkv_b", [D, 3072], F32, kind="ExternalInput").ap()
    wo_b = nc.dram_tensor("wo_b", [D, D], F32, kind="ExternalInput").ap()
    tbases = nc.dram_tensor("tbases", [64, 32, 16], F32, kind="ExternalInput").ap()
    tslopes = nc.dram_tensor("tslopes", [64, 32, 16], F32, kind="ExternalInput").ap()
    lgamma = nc.dram_tensor("lgamma", [D], F32, kind="ExternalInput").ap()
    lbeta = nc.dram_tensor("lbeta", [D], F32, kind="ExternalInput").ap()
    oscale = nc.dram_tensor("oscale", [64], F32, kind="ExternalInput").ap()
    hb_a_in = nc.dram_tensor("hb_a_in", [NKB_A], F32, kind="ExternalInput").ap()
    hb_b_in = nc.dram_tensor("hb_b_in", [NKB_B], F32, kind="ExternalInput").ap()
    posb_in = nc.dram_tensor("posb_in", [NKB_A], F32, kind="ExternalInput").ap()
    x_out = nc.dram_tensor("x_out", [T, D], F32, kind="ExternalOutput").ap()

    if DEBUG_FFN:
        ffsc_dbg = nc.dram_tensor("ffsc_dbg", [128, 816], F32, kind="ExternalOutput").ap()
        tmin_dbg = nc.dram_tensor("tmin_dbg", [64, 18], F32, kind="ExternalOutput").ap()
        agr_dbg = nc.dram_tensor("agr_dbg", [64, 4], F32, kind="ExternalOutput").ap()
        bt_dbg = nc.dram_tensor("bt_dbg", [128, 4096], BF16, kind="ExternalOutput").ap()
        st_dbg = nc.dram_tensor("st_dbg", [128, 4096], BF16, kind="ExternalOutput").ap()
    if DEBUG_PROJ:
        kT_dbg = nc.dram_tensor("kT_dbg", [128, 8192], BF16, kind="ExternalOutput").ap()
        v_dbg = nc.dram_tensor("v_dbg", [128, 8192], BF16, kind="ExternalOutput").ap()
        q_dbg = nc.dram_tensor("q_dbg", [128, 16384], BF16, kind="ExternalOutput").ap()
    rows_d = nc.dram_tensor("rows_d", [2, D], F32, kind="Internal").ap()
    rope_d = nc.dram_tensor("rope_d", [5, 128, 1024], F32, kind="Internal").ap()
    ag_in = nc.dram_tensor("ag_in", [128, 8192], BF16, kind="Internal").ap()
    ag_out = nc.dram_tensor("ag_out", [1024, 8192], BF16, kind="Internal").ap()
    ar_in = nc.dram_tensor("ar_in", [64, 2], F32, kind="Internal").ap()
    ar_out = nc.dram_tensor("ar_out", [64, 2], F32, kind="Internal").ap()

    st = ExitStack()

    def sb(name, shape, dt=F32):
        return st.enter_context(nc.sbuf_tensor(name, shape, dt))

    def sem(name):
        return Led(st.enter_context(nc.semaphore(name)))

    x_sb = sb("x_sb", [128, NQB * D])
    big16 = sb("big16", [128, 16384], BF16)
    kT = sb("kT", [128, 8192], BF16)
    v_sb = sb("v_sb", [128, 8192], BF16)
    arena = sb("arena", [128, 24576], BF16)
    y16 = sb("y16", [128, D], BF16)
    qk16 = sb("qk16", [128, 1024], BF16)
    rsc = sb("rsc", [128, 1792])
    cosg = sb("cosg", [128, 512])
    sing = sb("sing", [128, 512])
    tableT = sb("tableT", [64, 68])
    table16 = sb("table16", [64, 1024], BF16)
    ffsc = sb("ffsc", [128, 816])
    stats = sb("stats", [128, 156])
    small = sb("small", [128, 256])
    consts = sb("consts", [128, 8])
    iota64 = sb("iota64", [128, 64])
    iota64m = sb("iota64m", [128, 64])
    iota8m = sb("iota8m", [128, 8])
    ca64 = sb("ca64", [128, 64])
    mcaus = sb("mcaus", [128, 128], BF16)
    mstrU = sb("mstrU", [128, 128], BF16)
    minclU = sb("minclU", [128, 128], BF16)
    ident16 = sb("ident16", [128, 128], BF16)
    identf = sb("identf", [128, 128])
    ones_bf = sb("ones_bf", [128, 128], BF16)
    ones1 = sb("ones1", [1, 128])
    hbB = sb("hbB", [128, NKB_A + NKB_B])
    pidx = sb("pidx", [128, 1])
    invfA = sb("invfA", [128, 128])
    invfB = sb("invfB", [128, 128])
    sgnA = sb("sgnA", [128, 128])
    sgnB = sb("sgnB", [128, 128])
    tmin_s = sb("tmin_s", [64, 18])
    agr = sb("agr", [64, 4])
    ohq = sb("ohq", [128, 64])
    ohq16 = sb("ohq16", [128, 64], BF16)
    ohq_u8 = sb("ohq_u8", [128, 64], mybir.dt.uint8)
    ohall_u8 = sb("ohall_u8", [128, 512], mybir.dt.uint8)
    ohT16 = sb("ohT16", [64, 128], BF16)
    ohTf = sb("ohTf", [64, 128])
    gsel = sb("gsel", [128, 66])

    FC_TMIN, FC_TMAX, FC_TIDX, FC_OSEL = 0, 8, 16, 24
    FC_GI, FC_XL, FC_XS = 32, 288, 544

    s_dma = sem("s_dma")
    s_wdma = sem("s_wdma")
    s_wdma2 = sem("s_wdma2")
    s_hdma = sem("s_hdma")
    s_pe = sem("s_pe")
    s_dve = sem("s_dve")
    s_act = sem("s_act")
    s_pool = sem("s_pool")
    s_cc = sem("s_cc")
    ALL_SEMS = [s_dma, s_wdma, s_wdma2, s_hdma, s_pe, s_dve, s_act, s_pool, s_cc]

    with nc.Block() as b0:
        @b0.gpsimd
        def _(g):
            for s in ALL_SEMS:
                g.sem_clear(s.h)
    nc.all_engine_barrier()

    def a16(off, n):
        return arena[:, off:off + n]

    def a32(off_bf16, n_f32):
        return arena[:, off_bf16:off_bf16 + 2 * n_f32].bitcast(F32)

    P = Plan()

    def em(pl, eng, fn, waits=(), inc=None, k=1):
        return pl.op(eng, fn, waits=waits, inc=inc, k=k)

    def rope_tables_for_group(p, is_a, group, base_waits):
        invf = invfA if is_a else invfB
        sgn = sgnA if is_a else sgnB
        ng = len(group)
        W = 128 * ng
        posB = small[:, 224:225]
        ang = rsc[:, 0:512]
        ev_cos = ev_sin = None
        for si, u in enumerate(group):
            pb_col = 144 + (u if is_a else 4 + u)
            em(p, "dve", lambda v, pb_col=pb_col: v.tensor_single_scalar(
                posB, pidx[:], stats[:, pb_col:pb_col + 1], ALU.add),
                waits=(list(base_waits) if si == 0 else []))
            em(p, "dve", lambda v: v.drain())
            em(p, "dve", lambda v, si=si: v.tensor_single_scalar(
                ang[:, si * 128:(si + 1) * 128], invf[:], posB, ALU.mult))
            em(p, "dve", lambda v: v.drain())
        for trig in ("sin", "cos"):
            aa = rsc[:, 512:512 + W] if trig == "sin" else rsc[:, 1280:1280 + W]
            kk = rsc[:, 1280:1280 + W] if trig == "sin" else rsc[:, 512:512 + W]
            if trig == "sin":
                em(p, "dve", lambda v, aa=aa: v.tensor_copy(aa, ang[:, 0:W]))
            else:
                em(p, "dve", lambda v, aa=aa: v.tensor_single_scalar(
                    aa, ang[:, 0:W], float(0.5 * np.pi), ALU.add))
            em(p, "dve", lambda v: v.drain())
            em(p, "dve", lambda v, aa=aa, kk=kk: v.tensor_single_scalar(
                kk, aa, float(1.0 / TWO_PI), ALU.mult))
            em(p, "dve", lambda v: v.drain())
            em(p, "dve", lambda v, kk=kk: v.tensor_single_scalar(
                kk, kk, consts[:, 0:1], ALU.add))
            em(p, "dve", lambda v: v.drain())
            em(p, "dve", lambda v, kk=kk: v.tensor_single_scalar(
                kk, kk, MAGIC, ALU.subtract))
            em(p, "dve", lambda v: v.drain())
            em(p, "dve", lambda v, aa=aa, kk=kk: v.scalar_tensor_tensor(
                aa, kk, consts[:, 3:4], aa, ALU.mult, ALU.add))
            em(p, "dve", lambda v: v.drain())
            ev = em(p, "dve", lambda v, aa=aa, kk=kk: v.scalar_tensor_tensor(
                aa, kk, consts[:, 4:5], aa, ALU.mult, ALU.add), inc=s_dve)
            if trig == "sin":
                eva = em(p, "act", lambda s, aa=aa: s.activation(
                    rsc[:, 1280:1280 + W], aa, AF.Sin), waits=[(s_dve, ev)],
                    inc=s_act)
                sgb = AP(sgn, 0, [[128, 128], [0, ng], [1, 128]])
                ev_sin = em(p, "dve", lambda v, W=W, sgb=sgb: v.tensor_tensor(
                    sing[:, 0:W].rearrange("p (s w) -> p s w", w=128),
                    rsc[:, 1280:1280 + W].rearrange("p (s w) -> p s w", w=128),
                    sgb, ALU.mult), waits=[(s_act, eva)], inc=s_dve)
            else:
                ev_cos = em(p, "act", lambda s, aa=aa, W=W: s.activation(
                    cosg[:, 0:W], aa, AF.Sin), waits=[(s_dve, ev)], inc=s_act)
        return ev_cos, ev_sin

    def DMA(pl, dst, src, led=s_dma, eng="pool", waits=()):
        return pl.op(eng, lambda e, d=dst, s=src: e.dma_start(out=d, in_=s),
                     waits=waits, inc=led, k=16)

    DMA(P, AP(x_sb, 0, [[NQB * D, 128], [D, 4], [1, D]]),
        AP(x_in.tensor, 512 * D, [[D, 128], [128 * D, 4], [1, D]]), led=s_hdma)
    DMA(P, AP(x_sb, 4 * D, [[NQB * D, 128], [D, 4], [1, D]]),
        AP(x_in.tensor, (512 + 4 * 128) * D, [[D, 128], [128 * D, 4], [1, D]]),
        eng="sync")
    DMA(P, a32(4096, 512)[0:64, :], tslopes.rearrange("t j g -> t (j g)"),
        led=s_hdma)
    DMA(P, tableT[0:64, 0:32], lgamma.rearrange("(t j) -> t j", t=64), led=s_hdma)
    DMA(P, tableT[0:64, 32:64], lbeta.rearrange("(t j) -> t j", t=64), led=s_hdma)
    DMA(P, tableT[0:64, 64:65], oscale.rearrange("(t o) -> t o", o=1), led=s_hdma)
    DMA(P, table16[0:64, 0:512], tbases.rearrange("t j g -> t (j g)"), led=s_hdma)
    DMA(P, table16[0:64, 512:1024], tslopes.rearrange("t j g -> t (j g)"),
        led=s_hdma)
    DMA(P, small[0:1, 0:NKB_A], hb_a_in.rearrange("(o k) -> o k", o=1), led=s_hdma)
    DMA(P, small[0:1, 32:32 + NKB_B], hb_b_in.rearrange("(o k) -> o k", o=1),
        led=s_hdma)
    DMA(P, small[0:1, 64:64 + NKB_A], posb_in.rearrange("(o k) -> o k", o=1),
        led=s_hdma)
    n_setup_dma = s_dma.n
    n_setup_hdma = s_hdma.n

    qqB = a32(0, 128)
    kkB = a32(256, 128)
    jA = a32(512, 128)
    jB = a32(768, 128)
    h01 = a32(1024, 128)
    p01 = a32(1280, 128)
    em(P, "pool", lambda g: g.iota(small[:, 128:136], [[0, 8]], channel_multiplier=1,
                                   allow_small_or_imprecise_dtypes=True))
    em(P, "pool", lambda g: g.iota(iota64[:], [[1, 64]], channel_multiplier=0,
                                   allow_small_or_imprecise_dtypes=True))
    em(P, "pool", lambda g: g.iota(ca64[:], [[1, 8], [0, 8]], channel_multiplier=0,
                                   allow_small_or_imprecise_dtypes=True))
    em(P, "pool", lambda g: g.iota(qqB, [[1, 128]], channel_multiplier=0,
                                   allow_small_or_imprecise_dtypes=True))
    em(P, "pool", lambda g: g.iota(kkB, [[0, 128]], channel_multiplier=1,
                                   allow_small_or_imprecise_dtypes=True))
    em(P, "pool", lambda g: g.iota(jA, [[0, 2], [1, 64]], channel_multiplier=0,
                                   allow_small_or_imprecise_dtypes=True))
    em(P, "pool", lambda g: g.iota(jB, [[1, 64], [0, 2]], channel_multiplier=0,
                                   allow_small_or_imprecise_dtypes=True))
    em(P, "pool", lambda g: g.iota(h01, [[1, 2], [0, 64]], channel_multiplier=0,
                                   allow_small_or_imprecise_dtypes=True))
    em(P, "pool", lambda g: g.iota(p01, [[0, 64], [1, 2]], channel_multiplier=0,
                                   allow_small_or_imprecise_dtypes=True))
    em(P, "pool", lambda g: g.memset(ones1[:], 1.0))
    em(P, "pool", lambda g: g.memset(ones_bf[:], 1.0))
    for i, val in ((0, MAGIC), (1, 1e-5), (2, 1.0 / 2048.0), (3, -TWO_PI_HI),
                   (4, -TWO_PI_LO), (5, 1.0), (6, 1e-8), (7, BIGV)):
        em(P, "pool", lambda g, i=i, val=val: g.memset(consts[:, i:i + 1],
                                                       float(val)),
           inc=(s_pool if i == 7 else None))
    pl_set = s_pool.n

    w0 = [(s_dma, n_setup_dma), (s_hdma, n_setup_hdma),
          (s_pool, pl_set)]
    em(P, "dve", lambda v: v.tensor_copy(pidx[:], small[:, 128:129]), waits=w0)
    em(P, "dve", lambda v: v.drain())
    em(P, "dve", lambda v: v.tensor_tensor(ident16[:], qqB, kkB, ALU.is_equal))
    em(P, "dve", lambda v: v.tensor_tensor(identf[:], qqB, kkB, ALU.is_equal))
    em(P, "dve", lambda v: v.tensor_tensor(mcaus[:], qqB, kkB, ALU.is_ge))
    em(P, "dve", lambda v: v.tensor_tensor(minclU[:], qqB, kkB, ALU.is_le))
    em(P, "dve", lambda v: v.tensor_tensor(mstrU[:], qqB, kkB, ALU.is_lt))
    em(P, "dve", lambda v: v.tensor_single_scalar(iota64m[:], iota64[:], BIGV, ALU.subtract))
    em(P, "dve", lambda v: v.tensor_single_scalar(iota8m[:], iota64[:, 0:8], BIGV, ALU.subtract))
    em(P, "dve", lambda v: v.tensor_single_scalar(sgnA[:], h01, 2.0, ALU.mult))
    em(P, "dve", lambda v: v.tensor_single_scalar(sgnB[:], p01, 2.0, ALU.mult))
    em(P, "dve", lambda v: v.drain())
    em(P, "dve", lambda v: v.tensor_single_scalar(sgnA[:], sgnA[:], 1.0, ALU.subtract))
    em(P, "dve", lambda v: v.tensor_single_scalar(sgnB[:], sgnB[:], 1.0, ALU.subtract),
       inc=s_dve)
    dv_set1 = s_dve.n

    em(P, "act", lambda s: s.activation(invfA[:], jA, AF.Exp, scale=-LN10K / 64.0),
       waits=w0)
    em(P, "act", lambda s: s.activation(invfB[:], jB, AF.Exp, scale=-LN10K / 64.0),
       inc=s_act)
    ac_set1 = s_act.n

    with nc.psum_tensor("ps_b", [128, 512], F32) as ps_b:
        em(P, "pe", lambda t: t.matmul(ps_b[:, 0:NKB_A], ones1[:], small[0:1, 0:NKB_A],
                                       start=True, stop=True),
           waits=[(s_hdma, n_setup_hdma), (s_pool, pl_set)])
        em(P, "pe", lambda t: t.matmul(ps_b[:, 64:64 + NKB_B], ones1[:],
                                       small[0:1, 32:32 + NKB_B], start=True, stop=True))
        em(P, "pe", lambda t: t.matmul(ps_b[:, 128:128 + NKB_A], ones1[:],
                                       small[0:1, 64:64 + NKB_A], start=True, stop=True),
           inc=s_pe)
        pe_set1 = s_pe.n
        em(P, "dve", lambda v: v.tensor_copy(hbB[:, 0:NKB_A], ps_b[:, 0:NKB_A]),
           waits=[(s_pe, pe_set1)])
        em(P, "dve", lambda v: v.tensor_copy(hbB[:, NKB_A:NKB_A + NKB_B],
                                             ps_b[:, 64:64 + NKB_B]))
        em(P, "dve", lambda v: v.tensor_copy(stats[:, 144:144 + NKB_A],
                                             ps_b[:, 128:128 + NKB_A]), inc=s_dve)
        dv_set2 = s_dve.n

        em(P, "dve", lambda v: v.tensor_reduce(
            small[0:64, 224:256],
            a32(4096, 512)[0:64, :].rearrange("p (j g) -> p j g", j=32),
            AX.X, ALU.add), waits=[(s_hdma, n_setup_hdma)], inc=s_dve)
        ev_ss = s_dve.n
        em(P, "act", lambda s: s.activation(small[0:64, 224:256], small[0:64, 224:256],
                                            AF.Sign), waits=[(s_dve, ev_ss)], inc=s_act)
        ev_sg = s_act.n
        em(P, "dve", lambda v: v.tensor_tensor(small[0:64, 0:32], tableT[0:64, 0:32],
                                               small[0:64, 224:256], ALU.mult),
           waits=[(s_act, ev_sg)])
        em(P, "dve", lambda v: v.tensor_tensor(small[0:64, 32:64], tableT[0:64, 32:64],
                                               small[0:64, 224:256], ALU.mult),
           inc=s_dve)
        ev_rows = s_dve.n
        em(P, "sync", lambda sy: sy.dma_start(
            out=rows_d[0:1, :].rearrange("r (t j) -> (r t) j", t=64),
            in_=small[0:64, 0:32]), waits=[(s_dve, ev_rows)], inc=s_dma, k=16)
        em(P, "sync", lambda sy: sy.dma_start(
            out=rows_d[1:2, :].rearrange("r (t j) -> (r t) j", t=64),
            in_=small[0:64, 32:64]), inc=s_dma, k=16)
        s1 = big16[:, 0:3072].bitcast(F32)
        s2 = big16[:, 3072:6144].bitcast(F32)
        s3 = big16[:, 6144:9216].bitcast(F32)
        s4 = big16[:, 9216:12288].bitcast(F32)
        prev_stash = 0
        for nsets, nunits, invf, sgn, pb0, base in (
                (3, 12, invfA, sgnA, 144, 0), (2, 8, invfB, sgnB, 148, 3)):
            W = nsets * 512
            ang = rsc[:, 0:W]
            a3 = ang.rearrange("p (u w) -> p u w", w=128)
            pidxb = AP(pidx, 0, [[1, 128], [0, nunits], [0, 128]])
            posbb = AP(stats, pb0, [[156, 128], [1, nunits], [0, 128]])
            invfb_ = AP(invf, 0, [[128, 128], [0, nunits], [1, 128]])
            sgnb_ = AP(sgn, 0, [[128, 128], [0, nunits], [1, 128]])
            bw0 = [(s_dve, dv_set2), (s_act, ac_set1)]
            if prev_stash:
                bw0.append((s_dma, prev_stash))
            em(P, "dve", lambda v, a3=a3, pidxb=pidxb, posbb=posbb:
               v.tensor_tensor(a3, pidxb, posbb, ALU.add), waits=bw0)
            em(P, "dve", lambda v: v.drain())
            em(P, "dve", lambda v, a3=a3, invfb_=invfb_: v.tensor_tensor(
                a3, a3, invfb_, ALU.mult))
            em(P, "dve", lambda v: v.drain())
            evt = {}
            for trig, aa, kk in (("sin", s1, s2), ("cos", s3, s4)):
                av = aa[:, 0:W]
                kv = kk[:, 0:W]
                if trig == "sin":
                    em(P, "dve", lambda v, av=av, W=W, ang=ang: v.tensor_copy(
                        av, ang[:, 0:W]))
                else:
                    em(P, "dve", lambda v, av=av, W=W, ang=ang:
                       v.tensor_single_scalar(
                        av, ang[:, 0:W], float(0.5 * np.pi), ALU.add))
                em(P, "dve", lambda v: v.drain())
                em(P, "dve", lambda v, av=av, kv=kv: v.tensor_single_scalar(
                    kv, av, float(1.0 / TWO_PI), ALU.mult))
                em(P, "dve", lambda v: v.drain())
                em(P, "dve", lambda v, kv=kv: v.tensor_scalar(
                    kv, kv, consts[:, 0:1], MAGIC, ALU.add, ALU.subtract))
                em(P, "dve", lambda v: v.drain())
                em(P, "dve", lambda v, av=av, kv=kv: v.scalar_tensor_tensor(
                    av, kv, consts[:, 3:4], av, ALU.mult, ALU.add))
                em(P, "dve", lambda v: v.drain())
                em(P, "dve", lambda v, av=av, kv=kv: v.scalar_tensor_tensor(
                    av, kv, consts[:, 4:5], av, ALU.mult, ALU.add))
                em(P, "dve", lambda v: v.drain())
                PI_LO = float(np.nextafter(np.float32(np.pi), np.float32(0)))
                ev = em(P, "dve", lambda v, av=av, PI_LO=PI_LO: v.tensor_scalar(
                    av, av, -PI_LO, PI_LO, ALU.max, ALU.min), inc=s_dve)
                if trig == "sin":
                    eva = em(P, "act", lambda s, kv=kv, av=av: s.activation(
                        kv, av, AF.Sin), waits=[(s_dve, ev)], inc=s_act)
                    evt["sin"] = em(P, "dve", lambda v, av=av, kv=kv, W=W,
                                    sgnb_=sgnb_: v.tensor_tensor(
                        av.rearrange("p (u w) -> p u w", w=128),
                        kv.rearrange("p (u w) -> p u w", w=128),
                        sgnb_, ALU.mult), waits=[(s_act, eva)], inc=s_dve)
                else:
                    evt["cos"] = em(P, "act", lambda s, kv=kv, av=av: s.activation(
                        kv, av, AF.Sin), waits=[(s_dve, ev)], inc=s_act)
            em(P, "sync", lambda sy, base=base, nsets=nsets, W=W: sy.dma_start(
                out=AP(rope_d.tensor, base * 131072,
                       [[1024, 128], [131072, nsets], [1, 512]]),
                in_=s4[:, 0:W].rearrange("p (s c) -> p s c", c=512)),
                waits=[(s_act, evt["cos"])], inc=s_dma, k=16)
            prev_stash = em(P, "sync", lambda sy, base=base, nsets=nsets, W=W:
                            sy.dma_start(
                out=AP(rope_d.tensor, base * 131072 + 512,
                       [[1024, 128], [131072, nsets], [1, 512]]),
                in_=s1[:, 0:W].rearrange("p (s c) -> p s c", c=512)),
                waits=[(s_dve, evt["sin"])], inc=s_dma, k=16)
        P.run(nc)


    def scalar_chain_ln(p, u, xs_ap, want_C, extra_waits=()):
        sums = stats[:, u:u + 1]
        sumsq = stats[:, 32 + u:33 + u]
        negmu = stats[:, 64 + u:65 + u]
        rstd = stats[:, 96 + u:97 + u]
        Ccol = stats[:, 128 + u:129 + u]
        mu = small[:, 240:241]
        var = small[:, 241:242]
        t0 = small[:, 242:243]
        v2 = small[:, 243:244]
        em(p, "act", lambda s: s.activation(y16[:], xs_ap, AF.Square,
                                            accum_out=sumsq),
           waits=list(extra_waits))
        eva = em(p, "act", lambda s: s.activation(y16[:], xs_ap, AF.Identity,
                                                  accum_out=sums), inc=s_act)
        em(p, "dve", lambda v: v.tensor_single_scalar(mu, sums, consts[:, 2:3], ALU.mult),
           waits=[(s_act, eva)])
        em(p, "dve", lambda v: v.drain())
        em(p, "dve", lambda v: v.tensor_single_scalar(negmu, mu, -1.0, ALU.mult))
        em(p, "dve", lambda v: v.tensor_single_scalar(t0, mu, mu, ALU.mult),
           waits=[(s_act, eva)])
        em(p, "dve", lambda v: v.drain())
        em(p, "dve", lambda v: v.scalar_tensor_tensor(var, sumsq, consts[:, 2:3], t0,
                                                      ALU.mult, ALU.subtract))
        ev1 = em(p, "dve", lambda v: v.drain(), inc=s_dve)
        em(p, "act", lambda s: s.activation(t0, var, AF.Ln, bias=consts[:, 1:2]),
           waits=[(s_dve, ev1)])
        ev2 = em(p, "act", lambda s: s.activation(rstd, t0, AF.Exp, scale=-0.5),
                 inc=s_act)
        if not want_C:
            return ev2
        em(p, "act", lambda s: s.activation(v2, t0, AF.Exp, scale=-1.0))
        ev3 = em(p, "act", lambda s: s.nop(), inc=s_act)
        em(p, "dve", lambda v: v.tensor_single_scalar(v2, var, v2, ALU.mult),
           waits=[(s_act, ev3)])
        ev4 = em(p, "dve", lambda v: v.drain(), inc=s_dve)
        em(p, "act", lambda s: s.activation(v2, v2, AF.Ln, bias=consts[:, 1:2]),
           waits=[(s_dve, ev4)])
        ev5 = em(p, "act", lambda s: s.activation(v2, v2, AF.Exp, scale=-0.5),
                 inc=s_act)
        em(p, "dve", lambda v: v.tensor_single_scalar(Ccol, rstd, v2, ALU.mult),
           waits=[(s_act, ev5)])
        ev6 = em(p, "dve", lambda v: v.drain(), inc=s_dve)
        return ev2, ev6


    def attn_layer(variant):
        is_a = variant == "a"
        wqkv = wqkv_a if is_a else wqkv_b
        wo = wo_a if is_a else wo_b
        nkb = NKB_A if is_a else NKB_B
        hb_off = 0 if is_a else NKB_A
        units = list(range(NKB_A)) if is_a else list(range(NQB))
        groups = [units[i:i + 4] for i in range(0, len(units), 4)]

        CW = 512

        def uinfo(u):
            if is_a:
                return (u < HALO_A), (None if u < HALO_A else u - HALO_A), u
            return False, u, 8 + u

        def slots_of(c):
            out = []
            for i in range(CW // 128):
                col = CW * c + 128 * i
                if col < 2048:
                    out.append(("q", col // 128, i))
                elif col < 2560:
                    out.append(("k", (col - 2048) // 128, i))
                else:
                    out.append(("v", (col - 2560) // 128, i))
            return out

        p = Plan()
        with ExitStack() as pst:
            ps_tr = [pst.enter_context(nc.psum_tensor(f"pt{variant}{i}", [128, 2048],
                                                      BF16)) for i in range(2)]
            ps_qkv = [pst.enter_context(nc.psum_tensor(f"pq{variant}{i}", [128, 512],
                                                       F32)) for i in range(2)]
            wbuf = [arena[:, 0:8192], arena[:, 8192:16384]]
            yTg_off = 16384

            tr_evac = [0, 0]
            qkv_done = [0, 0]
            chunk_mm_done = {}
            citer = 0
            wchunk = 0
            pending_tr = []
            trctr = [0]
            tr_read = {0: None, 1: None}
            prev_tr_pe = 0
            yT_ev = {}
            ag_pending = [None]
            first_ln = [True]
            if is_a:
                passes = [("all", 0), ("all", 1), ("all", 2)]
            else:
                passes = [("kv", 0), ("kv", 1), ("AG", None), ("qq", None)]

            def load_ropetab(rset):
                wpre = [(s_dve, s_dve.n), (s_act, s_act.n)]
                em(p, "sync", lambda sy, rset=rset: sy.dma_start(
                    out=cosg[:, 0:512],
                    in_=AP(rope_d.tensor, rset * 128 * 1024,
                           [[1024, 128], [1, 512]])),
                   waits=wpre, inc=s_dma, k=16)
                ev_rld = em(p, "sync", lambda sy, rset=rset: sy.dma_start(
                    out=sing[:, 0:512],
                    in_=AP(rope_d.tensor, rset * 128 * 1024 + 512,
                           [[1024, 128], [1, 512]])), inc=s_dma, k=16)
                return ((s_dma, ev_rld), (s_dma, ev_rld))

            for ph, gi in passes:
                if ph == "AG":
                    for fn in pending_tr:
                        fn()
                    pending_tr = []
                    wpk = [(s_act, s_act.n), (s_dve, s_dve.n), (s_pe, s_pe.n)]
                    DMA(p, AP(ag_in.tensor, 0,
                              [[8192, 128], [1024, 4], [1, 1024]]),
                        AP(kT, 1024, [[8192, 128], [2048, 4], [1, 1024]]),
                        eng="sync", waits=wpk)
                    ev_pack = DMA(p, AP(ag_in.tensor, 4096,
                                        [[8192, 128], [1024, 4], [1, 1024]]),
                                  AP(v_sb, 1024, [[8192, 128], [2048, 4],
                                                  [1, 1024]]), eng="sync")
                    ag_pending[0] = ev_pack
                    continue
                if ph == "qq":
                    group = []
                    units_iter = [(gi2, si, si, u) for gi2 in (0, 1)
                                  for si, u in enumerate(groups[gi2])]
                    cos_ev = {}
                else:
                    group = groups[gi]
                    units_iter = [(gi, si, si, u) for si, u in enumerate(group)]
                    gtab = load_ropetab(gi if is_a else 3 + gi)
                    cos_ev = {u: gtab for u in group}
                for si, u in (enumerate(group) if ph != "qq" else []):
                    halo, qb, kb = uinfo(u)
                    xw = []
                    if first_ln[0]:
                        xw += [(s_dve, s_dve.n), (s_pool, s_pool.n)]
                        first_ln[0] = False
                    if halo:
                        hw = [(s_dve, s_dve.n), (s_act, s_act.n)]
                        hx_ap = a32(0, 2048)
                        ev_hx = DMA(p, hx_ap, AP(x_in.tensor, u * 128 * D,
                                                 [[D, 128], [1, D]]),
                                    led=s_hdma, waits=hw)
                        xs_ap = hx_ap
                        xw = [(s_hdma, ev_hx)]
                    else:
                        xs_ap = x_sb[:, qb * D:(qb + 1) * D]
                    ev_rstd = scalar_chain_ln(p, u % 32, xs_ap, False, extra_waits=xw)
                    wv_y = [(s_act, ev_rstd)]
                    if prev_tr_pe:
                        wv_y.append((s_pe, prev_tr_pe))
                    evy = em(p, "dve", lambda v, xs_ap=xs_ap, u=u: v.tensor_scalar(
                        y16[:], xs_ap, stats[:, 64 + (u % 32):65 + (u % 32)],
                        stats[:, 96 + (u % 32):97 + (u % 32)], ALU.add, ALU.mult),
                        waits=wv_y, inc=s_dve)
                    par = si % 2
                    wv = [(s_dve, evy)]
                    if tr_evac[par]:
                        wv.append((s_act, tr_evac[par]))
                    ev_tr = None
                    for i in range(16):
                        ev_tr = em(p, "pe", lambda t, i=i, par=par: t.transpose(
                            ps_tr[par][:, i * 128:(i + 1) * 128],
                            y16[:, i * 128:(i + 1) * 128], ident16[:]),
                            waits=(wv if i == 0 else []),
                            inc=(s_pe if i == 15 else None))
                    if (not is_a) and gi == 0:
                        d1 = kT[:, si * 2048:si * 2048 + 1024]
                        d2 = v_sb[:, si * 2048:si * 2048 + 1024]
                        em(p, "act", lambda s, d1=d1, par=par: s.activation(
                            d1, ps_tr[par][:, 0:1024], AF.Identity),
                            waits=[(s_pe, ev_tr)])
                        tr_evac[par] = em(p, "act", lambda s, d2=d2, par=par:
                                          s.activation(d2, ps_tr[par][:, 1024:2048],
                                                       AF.Identity), inc=s_act)
                    else:
                        dst_yT = a16(yTg_off + si * 2048, 2048)
                        tr_evac[par] = em(p, "act", lambda s, dst_yT=dst_yT, par=par:
                                          s.activation(dst_yT, ps_tr[par][:],
                                                       AF.Identity),
                                          waits=[(s_pe, ev_tr)], inc=s_act)
                    yT_ev[u] = tr_evac[par]
                    prev_tr_pe = ev_tr

                first_chunk = True
                if ph == "all":
                    clist = [4, 5] if group[0] < HALO_A else [0, 1, 2, 3, 4, 5]
                else:
                    clist = {"kv": [4, 5], "qq": [0, 1, 2, 3]}[ph]
                cur_g = [None]
                for c in clist:
                    par = wchunk % 2
                    wvw = ([(s_pe, chunk_mm_done[wchunk - 2])]
                           if wchunk - 2 in chunk_mm_done else [])
                    if first_chunk:
                        wvw += [(s_dve, s_dve.n), (s_act, s_act.n)]
                        first_chunk = False
                    wled = s_wdma
                    weng = "pool"
                    ev_w = DMA(p, AP(arena, par * (16 * CW),
                                     [[24576, 128], [CW, 16], [1, CW]]),
                               AP(wqkv.tensor, CW * c,
                                  [[3072, 128], [128 * 3072, 16], [1, CW]]),
                               led=wled, eng=weng, waits=wvw)
                    if ag_pending[0] is not None:
                        em(p, "pool", lambda g: g.collective_compute(
                            "AllGather", ALU.bypass, RG, [ag_in], [ag_out]),
                            waits=[(s_dma, ag_pending[0])], inc=s_cc, k=1)
                        ag_pending[0] = None
                    for gi2, jslot, si, u in units_iter:
                        if ph == "qq" and cur_g[0] != gi2:
                            gtab = load_ropetab(gi2 if is_a else 3 + gi2)
                            for uu in groups[gi2]:
                                cos_ev[uu] = gtab
                            cur_g[0] = gi2
                        halo, qb, kb = uinfo(u)
                        qp = citer % 2
                        wv = [(wled, ev_w), (s_act, yT_ev[u])]
                        if qkv_done[qp]:
                            wv.append((s_dve, qkv_done[qp]))
                        ev_mm = None
                        for dc in range(16):
                            if (not is_a) and gi2 == 0:
                                if dc < 8:
                                    lhs = kT[:, jslot * 2048 + dc * 128:
                                             jslot * 2048 + dc * 128 + 128]
                                else:
                                    lhs = v_sb[:, jslot * 2048 + (dc - 8) * 128:
                                               jslot * 2048 + (dc - 8) * 128 + 128]
                            else:
                                lhs = a16(yTg_off + jslot * 2048 + dc * 128, 128)
                            rhs = wbuf[par][:, dc * CW:(dc + 1) * CW]
                            ev_mm = em(p, "pe", lambda t, lhs=lhs, rhs=rhs, dc=dc,
                                       qp=qp: t.matmul(ps_qkv[qp][:, 0:CW], lhs, rhs,
                                                       start=(dc == 0),
                                                       stop=(dc == 15)),
                                       waits=(wv if dc == 0 else []),
                                       inc=(s_pe if dc == 15 else None))
                        chunk_mm_done[wchunk] = ev_mm
                        if len(pending_tr) > 1:
                            pending_tr.pop(0)()
                        sl = slots_of(c)
                        qk_slots = [s0 for s0 in sl if s0[0] in ("q", "k")]
                        v_slots = [s0 for s0 in sl if s0[0] == "v"]
                        last_consume = None
                        if qk_slots:
                            ns = len(qk_slots)
                            lo = qk_slots[0][2] * 128
                            if is_a:
                                off2, stp = 64, 1
                            else:
                                off2, stp = 1, 2
                            P1 = AP(ps_qkv[qp], lo, [[512, 128], [128, ns], [stp, 64]])
                            P2 = AP(ps_qkv[qp], lo + off2,
                                    [[512, 128], [128, ns], [stp, 64]])
                            C1 = AP(cosg, si * 128, [[512, 128], [0, ns], [stp, 64]])
                            C2 = AP(cosg, si * 128 + off2,
                                    [[512, 128], [0, ns], [stp, 64]])
                            S1 = AP(sing, si * 128, [[512, 128], [0, ns], [stp, 64]])
                            S2 = AP(sing, si * 128 + off2,
                                    [[512, 128], [0, ns], [stp, 64]])
                            m1 = AP(rsc, 768, [[1792, 128], [64, ns], [1, 64]])
                            m2 = AP(rsc, 1024, [[1792, 128], [64, ns], [1, 64]])
                            kp = citer % 2
                            if is_a:
                                o1 = AP(rsc, 1280, [[1792, 128], [128, ns], [stp, 64]])
                                o2 = AP(rsc, 1280 + off2,
                                        [[1792, 128], [128, ns], [stp, 64]])
                            else:
                                o1 = AP(qk16, kp * 512, [[1024, 128], [128, ns],
                                                         [stp, 64]])
                                o2 = AP(qk16, kp * 512 + off2,
                                        [[1024, 128], [128, ns], [stp, 64]])
                            qkr = rsc[:, 1280:1280 + 128 * ns]
                            wvr = [(s_pe, ev_mm), cos_ev[u][0], cos_ev[u][1]]
                            if tr_read[kp]:
                                wvr.append((s_pe, tr_read[kp]))
                            em(p, "dve", lambda v, P1=P1, C1=C1, m1=m1: v.tensor_tensor(
                                m1, P1, C1, ALU.mult), waits=wvr)
                            em(p, "dve", lambda v, P2=P2, S1=S1, m2=m2: v.tensor_tensor(
                                m2, P2, S1, ALU.mult))
                            em(p, "dve", lambda v: v.drain())
                            em(p, "dve", lambda v, o1=o1, m1=m1, m2=m2: v.tensor_tensor(
                                o1, m1, m2, ALU.add))
                            em(p, "dve", lambda v, P2=P2, C2=C2, m1=m1: v.tensor_tensor(
                                m1, P2, C2, ALU.mult))
                            em(p, "dve", lambda v, P1=P1, S2=S2, m2=m2: v.tensor_tensor(
                                m2, P1, S2, ALU.mult))
                            em(p, "dve", lambda v: v.drain())
                            ev_rope = em(p, "dve", lambda v, o2=o2, m1=m1, m2=m2:
                                         v.tensor_tensor(o2, m1, m2, ALU.add),
                                         inc=s_dve)
                            if is_a:
                                ev_sq = em(p, "act", lambda s, qkr=qkr, ns=ns:
                                           s.activation(rsc[:, 768:768 + 128 * ns], qkr,
                                                        AF.Square),
                                           waits=[(s_dve, ev_rope)], inc=s_act)
                                em(p, "dve", lambda v, ns=ns: v.tensor_reduce(
                                    small[:, 244:244 + ns],
                                    AP(rsc, 768, [[1792, 128], [128, ns], [1, 128]]),
                                    AX.X, ALU.add), waits=[(s_act, ev_sq)])
                                em(p, "dve", lambda v: v.drain())
                                em(p, "dve", lambda v, ns=ns: v.tensor_single_scalar(
                                    small[:, 244:244 + ns], small[:, 244:244 + ns],
                                    1e-24, ALU.max))
                                ev_ns = em(p, "dve", lambda v: v.drain(), inc=s_dve)
                                em(p, "act", lambda s, ns=ns: s.activation(
                                    small[:, 248:248 + ns], small[:, 244:244 + ns],
                                    AF.Ln), waits=[(s_dve, ev_ns)])
                                ev_rn = em(p, "act", lambda s, ns=ns: s.activation(
                                    small[:, 248:248 + ns], small[:, 248:248 + ns],
                                    AF.Exp, scale=-0.5), inc=s_act)
                                evq = None
                                for j in range(ns):
                                    evq = em(p, "dve", lambda v, j=j, kp=kp:
                                             v.tensor_single_scalar(
                                        qk16[:, kp * 512 + j * 128:
                                             kp * 512 + (j + 1) * 128],
                                        rsc[:, 1280 + j * 128:1280 + (j + 1) * 128],
                                        small[:, 248 + j:249 + j], ALU.mult),
                                        waits=([(s_act, ev_rn)] if j == 0 else []),
                                        inc=(s_dve if j == ns - 1 else None))
                            else:
                                evq = ev_rope
                            last_consume = evq

                            def mk_tr(evq=evq, kp=kp, qk_slots=tuple(qk_slots),
                                      qb=qb, kb=kb):
                                tp = trctr[0] % 2
                                trctr[0] += 1
                                wv2 = [(s_dve, evq)]
                                if tr_evac[tp]:
                                    wv2.append((s_act, tr_evac[tp]))
                                ev_t2 = None
                                for j in range(len(qk_slots)):
                                    ev_t2 = em(p, "pe", lambda t, j=j, tp=tp, kp=kp:
                                               t.transpose(
                                        ps_tr[tp][:, j * 128:(j + 1) * 128],
                                        qk16[:, kp * 512 + j * 128:
                                             kp * 512 + (j + 1) * 128],
                                        ident16[:]),
                                        waits=(wv2 if j == 0 else []),
                                        inc=(s_pe if j == len(qk_slots) - 1 else None))
                                tr_read[kp] = ev_t2
                                last_evac = None
                                for j, (kind, idx, slot_i) in enumerate(qk_slots):
                                    if kind == "q":
                                        dst = big16[:, idx * 1024 + qb * 128:
                                                    idx * 1024 + qb * 128 + 128]
                                    else:
                                        dst = kT[:, (idx * 16 + kb) * 128:
                                                 (idx * 16 + kb) * 128 + 128]
                                    last_evac = em(p, "act",
                                                   lambda s, j=j, tp=tp, dst=dst:
                                                   s.activation(
                                        dst, ps_tr[tp][:, j * 128:(j + 1) * 128],
                                        AF.Identity),
                                        waits=([(s_pe, ev_t2)] if j == 0 else []),
                                        inc=(s_act if j == len(qk_slots) - 1
                                             else None))
                                tr_evac[tp] = last_evac
                            pending_tr.append(mk_tr)
                        if v_slots:
                            lo = v_slots[0][2] * 128
                            nv = len(v_slots)
                            kvh0 = v_slots[0][1]
                            vdst = AP(v_sb, (kvh0 * 16 + kb) * 128,
                                      [[8192, 128], [2048, nv], [1, 128]])
                            vsrc = AP(ps_qkv[qp], lo, [[512, 128], [128, nv], [1, 128]])
                            last_consume = em(p, "dve", lambda v, vdst=vdst, vsrc=vsrc:
                                              v.tensor_copy(vdst, vsrc),
                                              waits=[(s_pe, ev_mm)], inc=s_dve)
                        qkv_done[qp] = s_dve.n
                        citer += 1
                    wchunk += 1
            for fn in pending_tr:
                fn()
            pending_tr = []
            p.run(nc)

        if DEBUG_PROJ and is_a:
            p = Plan()
            DMA(p, kT_dbg, kT[:, :], eng="sync")
            DMA(p, v_dbg, v_sb[:, :], eng="sync")
            ev0 = DMA(p, q_dbg, big16[:, :], eng="sync")
            em(p, "sync", lambda sy: sy.nop(), waits=[(s_dma, ev0)])
            p.run(nc)

        if not is_a:
            def unpack_k(g):
                with g.register("rk") as rk:
                    g.cc_rank_ld(rk, replica_groups=RG)
                    g.reg_add(rk, rk, 7)
                    g.reg_mod(rk, rk, 8)
                    g.reg_mul(rk, rk, 128 * 8192)
                    return g.dma_start(
                        out=AP(kT, 0, [[8192, 128], [2048, 4], [1, 1024]]),
                        in_=AP(ag_out.tensor, rk, [[8192, 128], [1024, 4], [1, 1024]]))

            def unpack_v(g):
                with g.register("rk2") as rk:
                    g.cc_rank_ld(rk, replica_groups=RG)
                    g.reg_add(rk, rk, 7)
                    g.reg_mod(rk, rk, 8)
                    g.reg_mul(rk, rk, 128 * 8192)
                    g.reg_add(rk, rk, 4096)
                    return g.dma_start(
                        out=AP(v_sb, 0, [[8192, 128], [2048, 4], [1, 1024]]),
                        in_=AP(ag_out.tensor, rk, [[8192, 128], [1024, 4], [1, 1024]]))

        p = Plan()
        upk_ev = None
        if not is_a:
            em(p, "pool", unpack_k, waits=[(s_cc, s_cc.n)], inc=s_hdma, k=16)
            upk_ev = em(p, "pool", unpack_v, inc=s_hdma, k=16)
        with ExitStack() as pst:
            ps_l = [pst.enter_context(nc.psum_tensor(f"pl{variant}{i}", [128, 1024],
                                                     F32)) for i in range(2)]
            ps_ao = pst.enter_context(nc.psum_tensor(f"pa{variant}", [128, 1024], F32))
            ps_s = pst.enter_context(nc.psum_tensor(f"pss{variant}", [128, 1024], F32))
            probs = [a16(0, 1024), a16(1024, 1024)]
            AO = 2048
            rbuf_v = a32(18432, 1024)
            lnr_v = a32(20480, 1024)

            exp_done = [0, 0]
            mask_done = [0, 0]
            pv_done = [0, 0]
            aodiv_ev = 0
            r_ev = 0
            for grp in range(4):
                for kvh in range(NKV):
                    kbs = [kb for kb in range(nkb) if runs_for(is_a, grp, kb)]
                    if not is_a:
                        kbs = [kb for kb in kbs if kb >= 8] + \
                              [kb for kb in kbs if kb < 8]
                    bank_started = [False, False]
                    esm = [0]
                    halo_waited = [False]
                    pend = []

                    def emit_pv(ent, kvh=kvh, bank_started=bank_started, esm=esm):
                        runs, lp, kb, wv_pv, first_kvh = ent
                        first = True
                        for qp2 in range(2):
                            for (q0, ln, masks) in runs:
                                stA = not bank_started[qp2]
                                wv = list(wv_pv) if first else []
                                if first and first_kvh:
                                    if aodiv_ev:
                                        wv.append((s_dve, aodiv_ev))
                                    if r_ev:
                                        wv.append((s_act, r_ev))
                                first = False
                                vt = v_sb[:, (kvh * 16 + kb) * 128:
                                          (kvh * 16 + kb) * 128 + 128]
                                pr = AP(arena, lp * 1024 + qp2 * 512 + q0,
                                        [[24576, 128], [256, 2], [1, ln]])
                                aod = AP(ps_ao, qp2 * 512 + q0,
                                         [[1024, 128], [256, 2], [1, ln]])
                                smd = AP(ps_s, qp2 * 512 + q0,
                                         [[1024, 128], [256, 2], [1, ln]])
                                em(p, "pe", lambda t, aod=aod, vt=vt, pr=pr,
                                   stA=stA: t.matmul(aod, vt, pr, start=stA,
                                                     stop=False),
                                   waits=wv, inc=s_pe)
                                esm[0] = em(p, "pe", lambda t, smd=smd, pr=pr,
                                            stA=stA: t.matmul(smd, ones_bf[:], pr,
                                                              start=stA, stop=False),
                                            inc=s_pe)
                                bank_started[qp2] = True
                        pv_done[lp] = s_pe.n

                    kiter = 0
                    for kb in kbs:
                        runs = runs_for(is_a, grp, kb)
                        lp = kiter % 2
                        ev_mm_l = None
                        first = True
                        for qp2 in range(2):
                            h = kvh * 4 + 2 * qp2
                            for (q0, ln, masks) in runs:
                                wq = []
                                if first:
                                    if exp_done[lp]:
                                        wq.append((s_act, exp_done[lp]))
                                    if (not is_a) and kb < 8 and not halo_waited[0]:
                                        wq.append((s_hdma, upk_ev))
                                        halo_waited[0] = True
                                    first = False
                                lhs = kT[:, (kvh * 16 + kb) * 128:
                                         (kvh * 16 + kb) * 128 + 128]
                                rhs = AP(big16, h * 1024 + grp * 256 + q0,
                                         [[16384, 128], [1024, 2], [1, ln]])
                                dst = AP(ps_l[lp], qp2 * 512 + q0,
                                         [[1024, 128], [256, 2], [1, ln]])
                                ev_mm_l = em(p, "pe", lambda t, dst=dst, lhs=lhs,
                                             rhs=rhs: t.matmul(dst, lhs, rhs,
                                                               start=True, stop=True),
                                             waits=wq, inc=s_pe)
                        ev_exp = None
                        first = True
                        for (q0, ln, masks) in runs:
                            wv = [(s_pe, ev_mm_l)]
                            if first:
                                if pv_done[lp]:
                                    wv.append((s_pe, pv_done[lp]))
                                if mask_done[lp]:
                                    wv.append((s_dve, mask_done[lp]))
                                first = False
                            pdst = AP(arena, (lp * 1024) + q0, [[24576, 128], [256, 4],
                                                               [1, ln]])
                            psrc = AP(ps_l[lp], q0, [[1024, 128], [256, 4], [1, ln]])
                            bias = hbB[:, hb_off + kb:hb_off + kb + 1]
                            ev_exp = em(p, "act", lambda s, pdst=pdst, psrc=psrc,
                                        bias=bias: s.activation(
                                pdst, psrc, AF.Exp, scale=SCALE, bias=bias),
                                waits=wv, inc=s_act)
                        exp_done[lp] = ev_exp
                        nmask = 0
                        for (q0, ln, masks) in runs:
                            for (off, mcls) in masks:
                                mt = {"causal": mcaus,
                                      "upper": (mstrU if is_a else minclU)}[mcls]
                                pap = AP(arena, lp * 1024 + q0 + off,
                                         [[24576, 128], [256, 4], [1, 128]])
                                mb = AP(mt, 0, [[128, 128], [0, 4], [1, 128]])
                                em(p, "dve", lambda g, pap=pap, mb=mb:
                                   g.tensor_tensor(pap, pap, mb, ALU.mult),
                                   waits=([(s_act, ev_exp)] if nmask == 0 else []),
                                   inc=s_dve)
                                nmask += 1
                        if nmask:
                            mask_done[lp] = s_dve.n
                        wv_pv = [(s_act, ev_exp)]
                        if nmask:
                            wv_pv.append((s_dve, mask_done[lp]))
                        pend.append((runs, lp, kb, wv_pv, kiter == 0))
                        if len(pend) >= 2:
                            emit_pv(pend.pop(0))
                        kiter += 1
                    while pend:
                        emit_pv(pend.pop(0))
                    em(p, "act", lambda s: s.activation(lnr_v, ps_s[:], AF.Ln),
                       waits=[(s_pe, esm[0])])
                    r_ev = em(p, "act", lambda s: s.activation(
                        rbuf_v, lnr_v, AF.Exp, scale=-1.0), inc=s_act)
                    ao_dst = AP(arena, AO + (kvh * 4) * 1024 + grp * 256,
                                [[24576, 128], [1024, 4], [1, 256]])
                    ao_src = AP(ps_ao, 0, [[1024, 128], [256, 4], [1, 256]])
                    rr = rbuf_v.rearrange("p (q c) -> p q c", q=4)
                    aodiv_ev = em(p, "dve", lambda v, ao_dst=ao_dst, ao_src=ao_src,
                                  rr=rr: v.tensor_tensor(ao_dst, ao_src, rr, ALU.mult),
                                  waits=[(s_act, r_ev)], inc=s_dve)
            p.run(nc)

        p = Plan()
        with ExitStack() as pst:
            ps_o = [pst.enter_context(nc.psum_tensor(f"po{variant}{i}", [128, 512],
                                                     F32)) for i in range(2)]
            AO = 2048
            evac_ev = [0, 0]
            wo_mm_done = {}
            for c in range(4):
                par = c % 2
                wv = [(s_pe, wo_mm_done[c - 2])] if c - 2 in wo_mm_done else []
                wled = s_wdma
                weng = "pool"
                ev_w = DMA(p, AP(big16, par * 8192, [[16384, 128], [512, 16], [1, 512]]),
                           AP(wo.tensor, 512 * c, [[D, 128], [128 * D, 16], [1, 512]]),
                           led=wled, eng=weng, waits=wv)
                for qb in range(NQB):
                    op = (c * NQB + qb) % 2
                    wv2 = [(wled, ev_w)]
                    if evac_ev[op]:
                        wv2.append((s_dve, evac_ev[op]))
                    ev_mm = None
                    for hh in range(16):
                        lhs = AP(arena, AO + hh * 1024 + qb * 128, [[24576, 128],
                                                                   [1, 128]])
                        rhs = big16[:, par * 8192 + hh * 512:par * 8192 + (hh + 1) * 512]
                        ev_mm = em(p, "pe", lambda t, lhs=lhs, rhs=rhs, hh=hh, op=op:
                                   t.matmul(ps_o[op][:], lhs, rhs, start=(hh == 0),
                                            stop=(hh == 15)),
                                   waits=(wv2 if hh == 0 else []),
                                   inc=(s_pe if hh == 15 else None))
                    wo_mm_done[c] = ev_mm
                    xd = x_sb[:, qb * D + 512 * c:qb * D + 512 * c + 512]
                    evac_ev[op] = em(p, "dve", lambda v, xd=xd, op=op: v.tensor_tensor(
                        xd, xd, ps_o[op][:], ALU.add), waits=[(s_pe, ev_mm)],
                        inc=s_dve)
            p.run(nc)


    ffn_ct = [0]

    ffn_ct = [0]

    def ffn_layer():
        ffi = ffn_ct[0]; ffn_ct[0] += 1
        gssB = a32(0, 2048)
        bssB = a32(4096, 2048)
        wbuf = a32(8192, 2048)
        s2buf = a32(12288, 2048)
        pwbuf = a32(16384, 2048)
        ps2buf = a32(20480, 2048)
        BT_off, ST_off = 16384, 20480
        BT = a16(BT_off, 4096)
        ST = a16(ST_off, 4096)
        bsel16 = qk16[:, 512:768]
        ssel16 = qk16[:, 768:1024]
        ts_all = rsc[:, 0:512]
        OH_all = rsc[:, 512:1024]
        v64 = rsc[:, 1024:1536]

        p = Plan()
        with ExitStack() as pst:
            pp = [pst.enter_context(nc.psum_tensor(f"pp{ffi}_{i}", [128, 1536], F32))
                  for i in range(2)]
            ps_t = pst.enter_context(nc.psum_tensor(f"ps_t{ffi}", [128, 128], F32))
            ps_t16 = pst.enter_context(nc.psum_tensor(f"ps_t16{ffi}", [128, 128], BF16))
            ps_b2 = pp[0]

            prev_mm = 0
            prev_cp = 0
            for ri, dstB in ((0, gssB), (1, bssB)):
                wv = [(s_pe, prev_mm)] if prev_mm else []
                ev_l = DMA(p, wbuf[0:1, :], rows_d[ri:ri + 1, :], led=s_hdma,
                           waits=wv)
                for j in range(4):
                    wvj = [(s_hdma, ev_l)] if j == 0 else []
                    if prev_cp:
                        wvj.append((s_dve, prev_cp))
                    ev_b = em(p, "pe", lambda t, j=j: t.matmul(
                        ps_b2[:, 0:512], ones1[:], wbuf[0:1, j * 512:(j + 1) * 512],
                        start=True, stop=True), waits=wvj, inc=s_pe)
                    prev_cp = em(p, "dve", lambda v, j=j, dstB=dstB: v.tensor_copy(
                        dstB[:, j * 512:(j + 1) * 512], ps_b2[:, 0:512]),
                        waits=[(s_pe, ev_b)], inc=s_dve)
                prev_mm = s_pe.n
            ev_bc = s_dve.n

            ev_acc = None
            for qb in range(NQB):
                xs_ap = x_sb[:, qb * D:(qb + 1) * D]
                em(p, "act", lambda s, xs_ap=xs_ap, qb=qb: s.activation(
                    y16[:], xs_ap, AF.Square, accum_out=stats[:, 32 + qb:33 + qb]))
                ev_acc = em(p, "act", lambda s, xs_ap=xs_ap, qb=qb: s.activation(
                    y16[:], xs_ap, AF.Identity, accum_out=stats[:, qb:qb + 1]),
                    inc=s_act)
            mu8 = small[:, 224:232]
            t8 = small[:, 232:240]
            v8 = small[:, 240:248]
            i2b = AP(consts, 2, [[8, 128], [0, 8]])
            e1b = AP(consts, 1, [[8, 128], [0, 8]])
            em(p, "dve", lambda v: v.tensor_tensor(mu8, stats[:, 0:8], i2b, ALU.mult),
               waits=[(s_act, ev_acc)])
            em(p, "dve", lambda v: v.drain())
            em(p, "dve", lambda v: v.tensor_single_scalar(stats[:, 64:72], mu8, -1.0,
                                                          ALU.mult))
            em(p, "dve", lambda v: v.tensor_tensor(t8, mu8, mu8, ALU.mult))
            em(p, "dve", lambda v: v.drain())
            em(p, "dve", lambda v: v.tensor_tensor(v8, stats[:, 32:40], i2b, ALU.mult))
            em(p, "dve", lambda v: v.drain())
            ev1 = em(p, "dve", lambda v: v.tensor_sub(v8, v8, t8), inc=s_dve)
            em(p, "act", lambda s: s.activation(t8, v8, AF.Ln, bias=consts[:, 1:2]),
               waits=[(s_dve, ev1)])
            em(p, "act", lambda s: s.activation(stats[:, 96:104], t8, AF.Exp,
                                                scale=-0.5))
            ev2 = em(p, "act", lambda s: s.activation(small[:, 248:256], t8, AF.Exp,
                                                      scale=-1.0), inc=s_act)
            em(p, "dve", lambda v: v.tensor_tensor(small[:, 248:256], v8,
                                                   small[:, 248:256], ALU.mult),
               waits=[(s_act, ev2)])
            ev3 = em(p, "dve", lambda v: v.drain(), inc=s_dve)
            em(p, "act", lambda s: s.activation(small[:, 248:256], small[:, 248:256],
                                                AF.Ln, bias=consts[:, 1:2]),
               waits=[(s_dve, ev3)])
            ev4 = em(p, "act", lambda s: s.activation(small[:, 248:256],
                                                      small[:, 248:256], AF.Exp,
                                                      scale=-0.5), inc=s_act)
            em(p, "dve", lambda v: v.tensor_tensor(stats[:, 128:136], stats[:, 96:104],
                                                   small[:, 248:256], ALU.mult),
               waits=[(s_act, ev4)])
            ev_C = em(p, "dve", lambda v: v.drain(), inc=s_dve)
            em(p, "dve", lambda v: v.tensor_tensor(stats[:, 8:16], stats[:, 64:72],
                                                   stats[:, 96:104], ALU.mult))
            em(p, "dve", lambda v: v.tensor_tensor(stats[:, 16:24], stats[:, 64:72],
                                                   stats[:, 128:136], ALU.mult))
            ev_bias8 = em(p, "dve", lambda v: v.drain(), inc=s_dve)

            POOL_QBS = ()
            first_eng = {"dve": True, "pool": True}
            red_ev = {}
            pend1 = []

            def flush1(ent):
                qb, s2, ev_s2, sem_e = ent
                ev_sgn = em(p, "act", lambda s, s2=s2: s.activation(
                    s2[:], s2[:], AF.Sign), waits=[(sem_e, ev_s2)], inc=s_act)
                red_ev[qb] = em(p, "dve", lambda v, qb=qb, s2=s2: v.tensor_reduce(
                    ts_all[:, qb * 64:(qb + 1) * 64],
                    s2.rearrange("p (t j) -> p t j", t=64), AX.X, ALU.add),
                    waits=[(s_act, ev_sgn)], inc=s_dve)

            for qb in range(NQB):
                eng = "pool" if qb in POOL_QBS else "dve"
                sem_e = s_pool if eng == "pool" else s_dve
                if eng == "pool":
                    wb, s2 = pwbuf, ps2buf
                else:
                    wb = wbuf if qb % 2 == 0 else pwbuf
                    s2 = s2buf if qb % 2 == 0 else ps2buf
                xs_ap = x_sb[:, qb * D:(qb + 1) * D]
                wv0 = ([(s_dve, ev_C), (s_dve, ev_bc)]
                       if first_eng[eng] else [])
                first_eng[eng] = False
                if eng == "pool":
                    nmb = AP(stats, 64 + qb, [[156, 128], [0, D]])
                    cb = AP(stats, 128 + qb, [[156, 128], [0, D]])
                    rb = AP(stats, 96 + qb, [[156, 128], [0, D]])
                    em(p, eng, lambda v, xs_ap=xs_ap, wb=wb, nmb=nmb:
                       v.tensor_tensor(wb[:], xs_ap, nmb, ALU.add), waits=wv0)
                    em(p, eng, lambda v, wb=wb, s2=s2, rb=rb: v.tensor_tensor(
                        s2[:], wb[:], rb, ALU.mult))
                    em(p, eng, lambda v, xs_ap=xs_ap, s2=s2: v.tensor_tensor(
                        xs_ap, xs_ap, s2[:], ALU.add))
                    em(p, eng, lambda v, wb=wb, cb=cb: v.tensor_tensor(
                        wb[:], wb[:], cb, ALU.mult))
                    em(p, eng, lambda v, wb=wb, s2=s2: v.tensor_tensor(
                        s2[:], wb[:], gssB[:], ALU.mult))
                else:
                    em(p, "act", lambda s, xs_ap=xs_ap, qb=qb, s2=s2: s.activation(
                        s2[:], xs_ap, AF.Identity,
                        scale=stats[:, 96 + qb:97 + qb],
                        bias=stats[:, 8 + qb:9 + qb]),
                        waits=([(s_dve, ev_bias8)] if qb < 2 else
                               [(s_dve, red_ev[qb - 2])]))
                    ev_w = em(p, "act", lambda s, xs_ap=xs_ap, qb=qb, wb=wb:
                              s.activation(
                        wb[:], xs_ap, AF.Identity,
                        scale=stats[:, 128 + qb:129 + qb],
                        bias=stats[:, 16 + qb:17 + qb]), inc=s_act)
                    em(p, "dve", lambda v, xs_ap=xs_ap, s2=s2: v.tensor_tensor(
                        xs_ap, xs_ap, s2[:], ALU.add), waits=[(s_act, ev_w)])
                    em(p, "dve", lambda v, wb=wb, s2=s2: v.tensor_tensor(
                        s2[:], wb[:], gssB[:], ALU.mult))
                ev_s2 = em(p, eng, lambda v, s2=s2: v.tensor_tensor(
                    s2[:], s2[:], bssB[:], ALU.add), inc=sem_e)
                pend1.append((qb, s2, ev_s2, sem_e))
                if len(pend1) >= 2:
                    flush1(pend1.pop(0))
            while pend1:
                flush1(pend1.pop(0))
            ev_ts_pool = s_pool.n

            cs_all = small[:, 0:64]
            m8 = small[:, 64:72]
            bc8 = small[:, 72:80]
            em(p, "dve", lambda v: v.tensor_reduce(
                cs_all, ts_all.rearrange("p (q c t) -> p (q c) t", q=8, c=8),
                AX.X, ALU.add), waits=[(s_pool, ev_ts_pool)])
            em(p, "dve", lambda v: v.drain())
            em(p, "dve", lambda v: v.tensor_reduce(
                m8, cs_all.rearrange("p (q c) -> p q c", q=8), AX.X, ALU.max))
            em(p, "dve", lambda v: v.drain())
            m8b = AP(small, 64, [[256, 128], [1, 8], [0, 8]])
            em(p, "dve", lambda v, m8b=m8b: v.tensor_tensor(
                v64[:, 0:64].rearrange("p (q c) -> p q c", q=8),
                cs_all.rearrange("p (q c) -> p q c", q=8), m8b, ALU.is_ge))
            em(p, "dve", lambda v: v.drain())
            i8t = AP(iota8m, 0, [[8, 128], [0, 8], [1, 8]])
            em(p, "dve", lambda v, i8t=i8t: v.tensor_tensor(
                v64[:, 0:64].rearrange("p (q c) -> p q c", q=8),
                v64[:, 0:64].rearrange("p (q c) -> p q c", q=8), i8t, ALU.mult))
            em(p, "dve", lambda v: v.drain())
            em(p, "dve", lambda v: v.tensor_reduce(
                bc8, v64[:, 0:64].rearrange("p (q c) -> p q c", q=8), AX.X, ALU.min))
            em(p, "dve", lambda v: v.drain())
            em(p, "dve", lambda v: v.tensor_single_scalar(bc8, bc8, BIGV, ALU.add))
            em(p, "dve", lambda v: v.drain())
            ca_t = AP(ca64, 0, [[64, 128], [0, 8], [1, 64]])
            bc_b = AP(small, 72, [[256, 128], [1, 8], [0, 64]])
            em(p, "dve", lambda v, ca_t=ca_t, bc_b=bc_b: v.tensor_tensor(
                v64.rearrange("p (q t) -> p q t", q=8), ca_t, bc_b, ALU.is_equal))
            em(p, "dve", lambda v: v.drain())
            em(p, "dve", lambda v: v.tensor_single_scalar(ts_all[:], ts_all[:], BIGV,
                                                          ALU.add))
            em(p, "dve", lambda v: v.drain())
            em(p, "dve", lambda v: v.tensor_tensor(v64[:], v64[:], ts_all[:], ALU.mult))
            em(p, "dve", lambda v: v.drain())
            em(p, "dve", lambda v: v.tensor_single_scalar(v64[:], v64[:], BIGV,
                                                          ALU.subtract))
            em(p, "dve", lambda v: v.drain())
            em(p, "dve", lambda v: v.tensor_reduce(
                m8, v64.rearrange("p (q t) -> p q t", q=8), AX.X, ALU.max))
            em(p, "dve", lambda v: v.drain())
            m8b2 = AP(small, 64, [[256, 128], [1, 8], [0, 64]])
            em(p, "dve", lambda v, m8b2=m8b2: v.tensor_tensor(
                OH_all.rearrange("p (q t) -> p q t", q=8),
                v64.rearrange("p (q t) -> p q t", q=8), m8b2, ALU.is_ge))
            em(p, "dve", lambda v: v.drain())
            i64t = AP(iota64m, 0, [[64, 128], [0, 8], [1, 64]])
            em(p, "dve", lambda v, i64t=i64t: v.tensor_tensor(
                v64.rearrange("p (q t) -> p q t", q=8),
                OH_all.rearrange("p (q t) -> p q t", q=8), i64t, ALU.mult))
            em(p, "dve", lambda v: v.drain())
            em(p, "dve", lambda v: v.tensor_reduce(
                ffsc[:, FC_TIDX:FC_TIDX + 8],
                v64.rearrange("p (q t) -> p q t", q=8), AX.X, ALU.min))
            em(p, "dve", lambda v: v.drain())
            em(p, "dve", lambda v: v.tensor_single_scalar(
                ffsc[:, FC_TIDX:FC_TIDX + 8], ffsc[:, FC_TIDX:FC_TIDX + 8], BIGV,
                ALU.add))
            em(p, "dve", lambda v: v.drain())
            io64t = AP(iota64, 0, [[64, 128], [0, 8], [1, 64]])
            ti_b = AP(ffsc, FC_TIDX, [[816, 128], [1, 8], [0, 64]])
            ev_oh_all = em(p, "dve", lambda v, io64t=io64t, ti_b=ti_b: v.tensor_tensor(
                OH_all.rearrange("p (q t) -> p q t", q=8), io64t, ti_b, ALU.is_equal),
                inc=s_dve)
            em(p, "dve", lambda v: v.drain())

            em(p, "dve", lambda v: v.tensor_single_scalar(small[:, 224:232],
                                                          stats[:, 96:104], 1.0,
                                                          ALU.add))
            em(p, "dve", lambda v: v.drain())
            em(p, "dve", lambda v: v.reciprocal(small[:, 232:240], small[:, 224:232]))
            em(p, "dve", lambda v: v.drain())
            em(p, "dve", lambda v: v.tensor_tensor(small[:, 232:240], small[:, 232:240],
                                                   stats[:, 128:136], ALU.mult))
            em(p, "dve", lambda v: v.drain())
            em(p, "dve", lambda v: v.tensor_tensor(stats[:, 24:32], stats[:, 64:72],
                                                   small[:, 232:240], ALU.mult))
            ev_cc8 = em(p, "dve", lambda v: v.drain(), inc=s_dve)
            gt_evac = 0
            prev_g3 = 0
            stt_ev = {}
            tm_in = AP(iota64, 0, [[64, 128], [1, 64], [0, 32]])
            for qb in range(NQB):
                xs_ap = x_sb[:, qb * D:(qb + 1) * D]
                wb6 = wbuf if qb % 2 == 0 else pwbuf
                ev_wact = em(p, "act", lambda s, xs_ap=xs_ap, qb=qb, wb6=wb6:
                             s.activation(
                    wb6[:], xs_ap, AF.Identity,
                    scale=small[:, 232 + qb:233 + qb],
                    bias=stats[:, 24 + qb:25 + qb]),
                    waits=([(s_dve, ev_cc8)] if qb < 2 else
                           [(s_dve, stt_ev[qb - 2])]), inc=s_act)
                stt_ev[qb] = em(p, "dve", lambda v, qb=qb, wb6=wb6:
                                v.scalar_tensor_tensor(
                    s2buf.rearrange("p (t j) -> p t j", t=64), tm_in,
                    ffsc[:, FC_TIDX + qb:FC_TIDX + qb + 1],
                    wb6.rearrange("p (t j) -> p t j", t=64),
                    ALU.is_equal, ALU.mult), waits=[(s_act, ev_wact)], inc=s_dve)
                em(p, "dve", lambda v: v.drain())
                ev_wsel = em(p, "dve", lambda v, qb=qb: v.tensor_reduce(
                    small[:, 0:32], s2buf.rearrange("p (t j) -> p j t", t=64),
                    AX.X, ALU.add), inc=s_dve)
                ohs = OH_all[:, qb * 64:(qb + 1) * 64]
                wv_t = [(s_dve, ev_wsel)]
                if prev_g3:
                    wv_t.append((s_pe, prev_g3))
                ev_trf = em(p, "pe", lambda t, ohs=ohs: t.transpose(
                    ps_t[0:64, :], ohs[:, 0:64], identf[:]), waits=wv_t, inc=s_pe)
                wv_e = [(s_pe, ev_trf)]
                if gt_evac:
                    wv_e.append((s_dve, gt_evac))
                gt_evac = em(p, "dve", lambda v: v.tensor_copy(
                    ohTf[0:64, 0:128], ps_t[0:64, :]), waits=wv_e, inc=s_dve)
                prev_g3 = ev_g3 = em(p, "pe", lambda t: t.matmul(
                    pp[0][:, 1024:1090], ohTf[0:64, 0:128],
                    tableT[0:64, 0:66], start=True, stop=True),
                    waits=[(s_dve, gt_evac)], inc=s_pe)
                em(p, "dve", lambda v: v.tensor_copy(
                    gsel[:, 0:66], pp[0][:, 1024:1090]),
                    waits=[(s_pe, ev_g3)])
                em(p, "dve", lambda v: v.drain())
                em(p, "dve", lambda v: v.tensor_tensor(
                    small[:, 0:32], small[:, 0:32], gsel[:, 0:32], ALU.mult))
                em(p, "dve", lambda v: v.drain())
                em(p, "dve", lambda v, qb=qb: v.tensor_tensor(
                    ffsc[:, FC_XS + qb * 32:FC_XS + (qb + 1) * 32], small[:, 0:32],
                    gsel[:, 32:64], ALU.add))
                em(p, "dve", lambda v, qb=qb: v.tensor_copy(
                    ffsc[:, FC_OSEL + qb:FC_OSEL + qb + 1], gsel[:, 64:65]))
                em(p, "dve", lambda v: v.drain())
            em(p, "dve", lambda v: v.tensor_reduce(
                ffsc[:, FC_TMIN:FC_TMIN + 8],
                ffsc[:, FC_XS:FC_XS + 256].rearrange("p (q j) -> p q j", q=8),
                AX.X, ALU.min))
            em(p, "dve", lambda v: v.tensor_reduce(
                ffsc[:, FC_TMAX:FC_TMAX + 8],
                ffsc[:, FC_XS:FC_XS + 256].rearrange("p (q j) -> p q j", q=8),
                AX.X, ALU.max))
            em(p, "dve", lambda v: v.drain())
            ev_u8 = em(p, "dve", lambda v: v.tensor_copy(ohall_u8[:], OH_all),
                       inc=s_dve)
            em(p, "dve", lambda v: v.drain())
            em(p, "dve", lambda v: v.tensor_single_scalar(small[:, 76:77],
                                                          consts[:, 7:8], -1.0,
                                                          ALU.mult))
            em(p, "dve", lambda v: v.drain())
            u8m = ohall_u8.ap()
            tminb = AP(ffsc, FC_TMIN, [[816, 128], [1, 8], [0, 64]])
            tmaxb = AP(ffsc, FC_TMAX, [[816, 128], [1, 8], [0, 64]])
            bigp = AP(consts, 7, [[8, 128], [0, 8], [0, 64]])
            bign = AP(small, 76, [[256, 128], [0, 8], [0, 64]])
            ev_selmin = em(p, "dve", lambda v: v.select(
                v64.rearrange("p (q t) -> p q t", q=8),
                u8m.rearrange("p (q t) -> p q t", q=8), tminb, bigp), inc=s_dve)
            em(p, "dve", lambda v: v.drain())
            red_d = [0, 0]
            ev_sel = ev_selmin
            for mode, col0, alu in ((0, 0, ALU.min), (1, 10, ALU.max)):
                if mode == 1:
                    ev_sel = em(p, "dve", lambda v: v.select(
                        v64.rearrange("p (q t) -> p q t", q=8),
                        u8m.rearrange("p (q t) -> p q t", q=8), tmaxb, bign),
                        waits=[(s_dve, red_d[0]), (s_dve, red_d[1])], inc=s_dve)
                    em(p, "dve", lambda v: v.drain())
                for qb in range(NQB):
                    par = qb % 2
                    dst = ps_t if par == 0 else pp[1]
                    wv = [(s_dve, ev_sel)] if qb < 2 else []
                    if red_d[par]:
                        wv.append((s_dve, red_d[par]))
                    ev_tt = em(p, "pe", lambda t, qb=qb, dst=dst: t.transpose(
                        dst[0:64, 0:128], v64[:, qb * 64:(qb + 1) * 64], identf[:]),
                        waits=wv, inc=s_pe)
                    red_d[par] = em(p, "dve", lambda v, qb=qb, col0=col0, alu=alu,
                                    dst=dst: v.tensor_reduce(
                        tmin_s[0:64, col0 + qb:col0 + qb + 1], dst[0:64, 0:128],
                        AX.X, alu), waits=[(s_pe, ev_tt)], inc=s_dve)
            red_last = em(p, "dve", lambda v: v.drain(),
                          waits=[(s_dve, red_d[0]), (s_dve, red_d[1])], inc=s_dve)
            pp1_guard = red_d[1]
            em(p, "dve", lambda v: v.tensor_reduce(agr[0:64, 0:1], tmin_s[0:64, 0:8],
                                                   AX.X, ALU.min))
            em(p, "dve", lambda v: v.drain())
            em(p, "dve", lambda v: v.tensor_single_scalar(agr[0:64, 0:1],
                                                          agr[0:64, 0:1], -1.0,
                                                          ALU.mult))
            em(p, "dve", lambda v: v.tensor_reduce(agr[0:64, 1:2], tmin_s[0:64, 10:18],
                                                   AX.X, ALU.max))
            ev_agr = em(p, "dve", lambda v: v.drain(), inc=s_dve)
            ev_st = em(p, "sync", lambda sy: sy.dma_start(out=ar_in,
                                                          in_=agr[0:64, 0:2]),
                       waits=[(s_dve, ev_agr)], inc=s_dma, k=16)
            ev_cc = em(p, "pool", lambda g: g.collective_compute(
                "AllReduce", ALU.max, RG, [ar_in], [ar_out]),
                waits=[(s_dma, ev_st)], inc=s_cc, k=1)
            t16_evac = 0
            mm_done = [0, 0]
            prev_tr16 = 0
            for qb in range(NQB):
                par = qb % 2
                ohs = OH_all[:, qb * 64:(qb + 1) * 64]
                wv_oh = [(s_pe, prev_tr16)] if prev_tr16 else []
                ev_oh16 = em(p, "dve", lambda v, ohs=ohs: v.tensor_copy(
                    ohq16[:, 0:64], ohs), waits=wv_oh, inc=s_dve)
                wv_t2 = [(s_dve, ev_oh16)]
                if t16_evac:
                    wv_t2.append((s_dve, t16_evac))
                prev_tr16 = ev_tr16 = em(p, "pe", lambda t: t.transpose(
                    ps_t16[0:64, :], ohq16[:, 0:64], ident16[:]), waits=wv_t2,
                    inc=s_pe)
                t16_evac = em(p, "dve", lambda v: v.tensor_copy(
                    ohT16[0:64, 0:128], ps_t16[0:64, :]),
                    waits=[(s_pe, ev_tr16)], inc=s_dve)
                wv_mm = [(s_dve, t16_evac)]
                if mm_done[par]:
                    wv_mm.append((s_dve, mm_done[par]))
                elif par == 1:
                    wv_mm.append((s_dve, pp1_guard))
                em(p, "pe", lambda t, par=par: t.matmul(
                    pp[par][:, 0:512], ohT16[0:64, 0:128],
                    table16[0:64, 0:512], start=True, stop=True), waits=wv_mm)
                ev_g2 = em(p, "pe", lambda t, par=par: t.matmul(
                    pp[par][:, 512:1024], ohT16[0:64, 0:128],
                    table16[0:64, 512:1024], start=True, stop=True), inc=s_pe)
                em(p, "dve", lambda v, qb=qb, par=par: v.tensor_copy(
                    BT[:, qb * 512:(qb + 1) * 512], pp[par][:, 0:512]),
                    waits=[(s_pe, ev_g2)])
                mm_done[par] = em(p, "dve", lambda v, qb=qb, par=par: v.tensor_copy(
                    ST[:, qb * 512:(qb + 1) * 512], pp[par][:, 512:1024]),
                    inc=s_dve)
            ev_bst = s_dve.n
            ev_ld = em(p, "pool", lambda g: g.dma_start(out=agr[0:64, 2:4], in_=ar_out),
                       waits=[(s_cc, ev_cc)], inc=s_hdma, k=16)
            em(p, "dve", lambda v: v.tensor_single_scalar(
                agr[0:64, 2:3], agr[0:64, 2:3], -1.0, ALU.mult),
                waits=[(s_hdma, ev_ld)])
            ev_n = em(p, "dve", lambda v: v.drain(), inc=s_dve)
            ev_t1 = em(p, "pe", lambda t: t.transpose(
                ps_t[0:1, 0:64], agr[0:64, 2:3], identf[0:64, 0:64]),
                waits=[(s_dve, ev_n)], inc=s_pe)
            ev_t2 = em(p, "pe", lambda t: t.transpose(
                ps_t[0:1, 64:128], agr[0:64, 3:4], identf[0:64, 0:64]), inc=s_pe)
            em(p, "dve", lambda v: v.tensor_copy(small[0:1, 0:64], ps_t[0:1, 0:64]),
               waits=[(s_pe, ev_t2)])
            ev_rw = em(p, "dve", lambda v: v.tensor_copy(small[0:1, 64:128],
                                                         ps_t[0:1, 64:128]),
                       inc=s_dve)
            ev_b1 = em(p, "pe", lambda t: t.matmul(
                ps_t[:, 0:64], ones1[:], small[0:1, 0:64], start=True, stop=True),
                waits=[(s_dve, ev_rw)], inc=s_pe)
            ev_b2 = em(p, "pe", lambda t: t.matmul(
                ps_t[:, 64:128], ones1[:], small[0:1, 64:128], start=True,
                stop=True), inc=s_pe)
            em(p, "dve", lambda v: v.tensor_copy(small[:, 88:152], ps_t[:, 0:64]),
               waits=[(s_pe, ev_b2)])
            ev_gb = em(p, "dve", lambda v: v.tensor_copy(small[:, 152:216],
                                                         ps_t[:, 64:128]),
                       inc=s_dve)
            em(p, "dve", lambda v: v.drain())
            tgb = AP(small, 88, [[256, 128], [0, 8], [1, 64]])
            txb = AP(small, 152, [[256, 128], [0, 8], [1, 64]])
            em(p, "dve", lambda v, tgb=tgb: v.tensor_tensor(
                v64.rearrange("p (q t) -> p q t", q=8),
                OH_all.rearrange("p (q t) -> p q t", q=8), tgb, ALU.mult))
            em(p, "dve", lambda v: v.drain())
            em(p, "dve", lambda v: v.tensor_reduce(
                small[:, 96:104], v64.rearrange("p (q t) -> p q t", q=8),
                AX.X, ALU.add))
            em(p, "dve", lambda v, txb=txb: v.tensor_tensor(
                v64.rearrange("p (q t) -> p q t", q=8),
                OH_all.rearrange("p (q t) -> p q t", q=8), txb, ALU.mult))
            em(p, "dve", lambda v: v.drain())
            em(p, "dve", lambda v: v.tensor_reduce(
                small[:, 104:112], v64.rearrange("p (q t) -> p q t", q=8),
                AX.X, ALU.add))
            em(p, "dve", lambda v: v.drain())
            em(p, "dve", lambda v: v.tensor_sub(small[:, 112:120], small[:, 104:112],
                                                small[:, 96:104]))
            em(p, "dve", lambda v: v.drain())
            e8b = AP(consts, 6, [[8, 128], [0, 8]])
            em(p, "dve", lambda v, e8b=e8b: v.tensor_tensor(
                small[:, 112:120], small[:, 112:120], e8b, ALU.add))
            em(p, "dve", lambda v: v.drain())
            em(p, "dve", lambda v: v.reciprocal(small[:, 120:128], small[:, 112:120]))
            em(p, "dve", lambda v: v.drain())
            em(p, "dve", lambda v: v.tensor_single_scalar(
                small[:, 120:128], small[:, 120:128], 16.0, ALU.mult))
            em(p, "dve", lambda v: v.drain())
            U16MAX = float(np.float32(1.0 - 1e-6) * 16)
            mnb = AP(small, 96, [[256, 128], [1, 8], [0, 32]])
            invb = AP(small, 120, [[256, 128], [1, 8], [0, 32]])
            GI = ffsc[:, FC_GI:FC_GI + 256]
            XL = ffsc[:, FC_XL:FC_XL + 256]
            XS = ffsc[:, FC_XS:FC_XS + 256]
            em(p, "dve", lambda v, mnb=mnb: v.tensor_tensor(
                GI.rearrange("p (q j) -> p q j", q=8),
                XS.rearrange("p (q j) -> p q j", q=8), mnb, ALU.subtract))
            em(p, "dve", lambda v: v.drain())
            em(p, "dve", lambda v, invb=invb: v.tensor_tensor(
                GI.rearrange("p (q j) -> p q j", q=8),
                GI.rearrange("p (q j) -> p q j", q=8), invb, ALU.mult))
            em(p, "dve", lambda v: v.drain())
            em(p, "dve", lambda v: v.tensor_scalar(GI, GI, 0.0, U16MAX,
                                                   ALU.max, ALU.min))
            em(p, "dve", lambda v: v.drain())
            em(p, "dve", lambda v: v.tensor_scalar(XL, GI, MAGIC, MAGIC,
                                                   ALU.add, ALU.subtract))
            em(p, "dve", lambda v: v.drain())
            em(p, "dve", lambda v: v.tensor_tensor(small[:, 0:256], XL, GI,
                                                   ALU.is_gt))
            em(p, "dve", lambda v: v.drain())
            em(p, "dve", lambda v: v.tensor_sub(XL, XL, small[:, 0:256]))
            em(p, "dve", lambda v: v.drain())
            em(p, "dve", lambda v: v.tensor_sub(GI, GI, XL))
            em(p, "dve", lambda v: v.drain())
            eq16 = a16(8192, 4096)
            pbt16 = a16(0, 4096)
            pst16 = a16(4096, 4096)
            eq3v = eq16.rearrange("p (qj g) -> p qj g", g=16)
            GIb = AP(ffsc, FC_XL, [[816, 128], [1, 256], [0, 16]])
            g16 = AP(iota64, 0, [[64, 128], [0, 256], [1, 16]])
            ev_eq = em(p, "dve", lambda v: v.tensor_tensor(
                eq3v, GIb, g16, ALU.is_equal), inc=s_dve)
            em(p, "dve", lambda v: v.drain())
            BTg3 = AP(arena, BT_off, [[24576, 128], [16, 256], [1, 16]])
            STg3 = AP(arena, ST_off, [[24576, 128], [16, 256], [1, 16]])
            pb3 = pbt16.rearrange("p (qj g) -> p qj g", g=16)
            ps3 = pst16.rearrange("p (qj g) -> p qj g", g=16)
            bselF = cosg[:, 0:256]
            sselF = cosg[:, 256:512]
            USE_POOL_SPLINE = False
            if USE_POOL_SPLINE:
                ev_psTT = em(p, "pool", lambda v: v.tensor_tensor(
                    ps3, eq3v, STg3, ALU.mult), waits=[(s_dve, ev_eq)], inc=s_pool)
            else:
                ev_psTT = em(p, "dve", lambda v: v.tensor_tensor(
                    ps3, eq3v, STg3, ALU.mult), inc=s_dve)
            em(p, "dve", lambda v: v.tensor_tensor(pb3, eq3v, BTg3, ALU.mult))
            em(p, "dve", lambda v: v.drain())
            em(p, "dve", lambda v: v.tensor_reduce(bselF, pb3, AX.X, ALU.add))
            em(p, "dve", lambda v: v.tensor_reduce(sselF, ps3, AX.X, ALU.add),
               waits=([(s_pool, ev_psTT)] if USE_POOL_SPLINE else []))
            em(p, "dve", lambda v: v.drain())
            em(p, "dve", lambda v: v.tensor_tensor(small[:, 0:256], sselF,
                                                   GI, ALU.mult))
            em(p, "dve", lambda v: v.drain())
            em(p, "dve", lambda v: v.tensor_tensor(small[:, 0:256], small[:, 0:256],
                                                   bselF, ALU.add))
            em(p, "dve", lambda v: v.drain())
            osb = AP(ffsc, FC_OSEL, [[816, 128], [1, 8], [0, 32]])
            ev_delta = em(p, "dve", lambda v, osb=osb: v.tensor_tensor(
                small[:, 0:256].rearrange("p (q j) -> p q j", q=8),
                small[:, 0:256].rearrange("p (q j) -> p q j", q=8), osb, ALU.mult),
                inc=s_dve)
            em(p, "dve", lambda v: v.drain())
            pbufs = [wbuf, s2buf]
            pfree = [0, 0]
            for qb in range(NQB):
                to_pool = qb in POOL_QBS
                wb = pbufs[qb % 2] if to_pool else wbuf
                dexp = AP(small, qb * 32, [[256, 128], [0, 64], [1, 32]])
                wv = [(s_pool, pfree[qb % 2])] if (to_pool and pfree[qb % 2]) \
                    else []
                ev_w3 = em(p, "dve", lambda v, qb=qb, wb=wb, dexp=dexp:
                           v.scalar_tensor_tensor(
                    wb.rearrange("p (t j) -> p t j", t=64), tm_in,
                    ffsc[:, FC_TIDX + qb:FC_TIDX + qb + 1], dexp,
                    ALU.is_equal, ALU.mult), waits=wv, inc=s_dve)
                em(p, "dve", lambda v: v.drain())
                if to_pool:
                    pfree[qb % 2] = em(p, "pool", lambda v, qb=qb, wb=wb:
                                       v.tensor_tensor(
                        x_sb[:, qb * D:(qb + 1) * D],
                        x_sb[:, qb * D:(qb + 1) * D], wb[:], ALU.add),
                        waits=[(s_dve, ev_w3)], inc=s_pool)
                else:
                    em(p, "dve", lambda v, qb=qb, wb=wb: v.tensor_tensor(
                        x_sb[:, qb * D:(qb + 1) * D],
                        x_sb[:, qb * D:(qb + 1) * D], wb[:], ALU.add),
                        inc=s_dve)
            p.run(nc)

    attn_layer("a")
    if N_SUBLAYERS >= 2:
        ffn_layer()
    if DEBUG_FFN:
        p = Plan()
        DMA(p, ffsc_dbg, ffsc[:, :], eng="sync")
        DMA(p, tmin_dbg, tmin_s[0:64, :], eng="sync")
        DMA(p, agr_dbg, agr[0:64, :], eng="sync")
        DMA(p, bt_dbg, arena[:, 16384:20480], eng="sync")
        ev0 = DMA(p, st_dbg, arena[:, 20480:24576], eng="sync")
        em(p, "sync", lambda sy: sy.nop(), waits=[(s_dma, ev0)])
        p.run(nc)
    if N_SUBLAYERS >= 3:
        attn_layer("b")
    if N_SUBLAYERS >= 4:
        ffn_layer()

    p = Plan()
    ev = em(p, "sync", lambda sy: sy.dma_start(
        out=AP(x_out.tensor, 0, [[D, 128], [128 * D, NQB], [1, D]]),
        in_=AP(x_sb, 0, [[NQB * D, 128], [D, NQB], [1, D]])),
        waits=[(s_dve, s_dve.n), (s_pool, s_pool.n)], inc=s_dma, k=16)
    em(p, "sync", lambda sy: sy.nop(), waits=[(s_dma, ev)])
    p.run(nc)

    st.close()
    return nc


_NC_CACHE = None


def kernel(L0_attn_a_x, L0_attn_a_wq, L0_attn_a_wk, L0_attn_a_wv, L0_attn_a_wo,
           L0_attn_b_wq, L0_attn_b_wk, L0_attn_b_wv, L0_attn_b_wo,
           L0_ffn_ln_gamma, L0_ffn_ln_beta, L0_ffn_spline_bases,
           L0_ffn_spline_slopes, L0_ffn_output_scale):
    global _NC_CACHE
    x = np.ascontiguousarray(np.asarray(L0_attn_a_x, dtype=np.float32).reshape(8192, D))
    wqkv_a_h = np.ascontiguousarray(
        np.concatenate([L0_attn_a_wq, L0_attn_a_wk, L0_attn_a_wv], axis=1)
        .astype(np.float32))
    wqkv_b_h = np.ascontiguousarray(
        np.concatenate([L0_attn_b_wq, L0_attn_b_wk, L0_attn_b_wv], axis=1)
        .astype(np.float32))
    wo_a_h = np.ascontiguousarray(np.asarray(L0_attn_a_wo, dtype=np.float32))
    wo_b_h = np.ascontiguousarray(np.asarray(L0_attn_b_wo, dtype=np.float32))
    tb = np.ascontiguousarray(np.asarray(L0_ffn_spline_bases, dtype=np.float32))
    ts_ = np.ascontiguousarray(np.asarray(L0_ffn_spline_slopes, dtype=np.float32))
    lg = np.ascontiguousarray(np.asarray(L0_ffn_ln_gamma, dtype=np.float32))
    lb = np.ascontiguousarray(np.asarray(L0_ffn_ln_beta, dtype=np.float32))
    osc = np.ascontiguousarray(np.asarray(L0_ffn_output_scale, dtype=np.float32))

    if _NC_CACHE is None:
        _NC_CACHE = build_nc()
    nc = _NC_CACHE

    in_maps = []
    for c in range(8):
        lo = c * T - 512
        if lo < 0:
            xs = np.concatenate([np.zeros((512, D), np.float32), x[0:c * T + T]],
                                axis=0)
        else:
            xs = x[lo:c * T + T]
        hb_a = np.zeros(NKB_A, np.float32)
        hb_b = np.zeros(NKB_B, np.float32)
        if c == 0:
            hb_a[:HALO_A] = HBIAS
            hb_b[:8] = HBIAS
        posb = (np.arange(NKB_A, dtype=np.float32) * 128.0) + float(c * T - 512)
        if c == 0:
            posb[:HALO_A] = 0.0
        in_maps.append({
            "x_in": np.ascontiguousarray(xs),
            "wqkv_a": wqkv_a_h, "wo_a": wo_a_h,
            "wqkv_b": wqkv_b_h, "wo_b": wo_b_h,
            "tbases": tb, "tslopes": ts_, "lgamma": lg, "lbeta": lb, "oscale": osc,
            "hb_a_in": hb_a, "hb_b_in": hb_b, "posb_in": posb,
        })
    res = run_bass_kernel_spmd(nc, in_maps, core_ids=list(range(8)))
    out = np.concatenate([res.results[c]["x_out"] for c in range(8)], axis=0)
    return out.reshape(1, 8192, D).astype(np.float32)
